# version 29
# baseline (speedup 1.0000x reference)
import numpy as np
import ml_dtypes
from contextlib import ExitStack

import concourse.bass as bass
import concourse.mybir as mybir
from concourse.bass_core_rust import IndirectOffsetOnAxis
from concourse.bass_utils import run_bass_kernel_spmd

F32 = mybir.dt.float32
BF16 = mybir.dt.bfloat16
I32 = mybir.dt.int32
U32 = mybir.dt.uint32
AF = mybir.ActivationFunctionType
ALU = mybir.AluOpType
AXX = mybir.AxisListType.X

NCORES = 8
TOK = 2048
HALO = 256
TOKH = TOK + HALO
NT = TOKH // 128
OWNT = TOK // 128
D = 1024
DC = D // 128
DFF = 4096
JC = DFF // 128
NE = 8
CAP = 2560
LCAP = 640
RT = LCAP // 128
GT = 16384 // 128
GSZ = 512
NG = TOK // GSZ
BIG = 6.0e6
E23 = 8388608.0
BIG_I = 6000000

PHASES = 6
KDEBUG = False


class _NS:
    pass


class _ColView:

    def __init__(self, h, off, width):
        self.h = h; self.off = off; self.w = width

    def __getitem__(self, idx):
        if idx == slice(None):
            return self.h[:, self.off:self.off + self.w]
        ps, fs = idx
        a = fs.start if fs.start is not None else 0
        b = fs.stop if fs.stop is not None else self.w
        return self.h[:, self.off + a:self.off + b]


def build_nc():
    nc = bass.Bass("TRN2", target_bir_lowering=False)
    S = _NS()
    S.nc = nc

    def din(name, shape, dtype=F32):
        return nc.dram_tensor(name, list(shape), dtype, kind="ExternalInput").ap()

    S.xsl = din("xsl", (TOKH, D))
    S.xt = din("xt", (D, 16384))
    S.Wf_h = din("Wf", (D, D)); S.Wi_h = din("Wi", (D, D)); S.Wh_h = din("Wh", (D, D))
    S.Wr_h = din("Wr", (D, NE))
    S.W1_h = din("W1", (NE, D, DFF)); S.b1_h = din("b1", (NE, DFF))
    S.W2_h = din("W2", (NE, DFF, D)); S.b2_h = din("b2", (NE, D))
    S.fw1_h = din("fw1", (D, DFF)); S.fb1_h = din("fb1", (1, DFF))
    S.fw2_h = din("fw2", (DFF, D)); S.fb2_h = din("fb2", (1, D))
    S.b1t_h = din("b1t", (128, NE * JC)); S.fb1t_h = din("fb1t", (128, JC))
    consts_in = {}
    for nm, shape, dt in [
            ("idf", (128, 128), F32), ("idb", (128, 128), BF16),
            ("cumW", (128, 128), BF16), ("cumSW", (128, 128), BF16),
            ("onesB", (128, 128), BF16), ("onesR", (1, 128), F32),
            ("onesC", (128, 1), F32), ("epscol", (128, 1), F32),
            ("iota8", (128, NE), F32), ("iotam8", (128, NE), F32),
            ("negbig", (128, 1), F32), ("imask", (128, 1), I32),
            ("halo_s", (128, 1), F32), ("halo_h", (128, 1), F32),
            ("sown", (128, OWNT), F32),
            ("paytok", (128, OWNT), I32), ("pdst1", (128, OWNT), I32),
            ("pdst2", (128, OWNT), I32)]:
        consts_in[nm] = din(nm, shape, dt)

    S.out_h = nc.dram_tensor("out", [TOK, D], F32, kind="ExternalOutput").ap()
    S.taps = {}
    if KDEBUG:
        def tap(name, shape, dtype=F32):
            S.taps[name] = nc.dram_tensor(name, list(shape), dtype,
                                          kind="ExternalOutput").ap()
        tap("t_xn", (TOKH, D)); tap("t_utm", (128, GT * NE))
        tap("t_keep", (128, OWNT * NE)); tap("t_route", (128, OWNT * 12))
        tap("t_h", (TOK, D)); tap("t_idx", (128, RT * NE * 2), I32)
        tap("t_y", (2 * TOK, D))

    S.xn_dram = nc.dram_tensor("xn_dram", [TOKH, D], BF16, kind="Internal").ap()
    S.xnfm_dram = nc.dram_tensor("xnfm_dram", [NG * DC * 128, GSZ], BF16,
                                 kind="Internal").ap()
    S.h_dram = nc.dram_tensor("h_dram", [TOK, D], F32, kind="Internal").ap()
    S.idx_dram = nc.dram_tensor("idx_dram", [NE * LCAP, 1], I32, kind="Internal").ap()
    S.dst_dram = nc.dram_tensor("dst_dram", [NE * LCAP, 1], I32, kind="Internal").ap()
    S.Y_dram = nc.dram_tensor("Y_dram", [2 * TOK, D], F32, kind="Internal").ap()
    S.fw1bf = nc.dram_tensor("fw1bf", [D, DFF], BF16, kind="Internal").ap()
    S.fw2bf = nc.dram_tensor("fw2bf", [DFF, D], BF16, kind="Internal").ap()

    ctx = ExitStack()
    with ctx:
        def sb(name, shape, dtype=F32, stk=None):
            return (stk or ctx).enter_context(nc.sbuf_tensor(name, list(shape), dtype))
        S.sb = sb

        class Ctr:
            def __init__(self, name):
                self.s = ctx.enter_context(nc.semaphore(name)); self.n = 0
                self.wm = {}
            def inc(self, instr, k=1):
                instr.then_inc(self.s, k); self.n += k; return instr
            def wait(self, eng, n=None):
                v = self.n if n is None else n
                key = id(eng)
                if self.wm.get(key, -1) >= v:
                    return
                eng.wait_ge(self.s, v)
                self.wm[key] = v

        class Q:
            def __init__(self, eng, name):
                self.eng = eng
                self.s = ctx.enter_context(nc.semaphore(name)); self.n = 0
                self.d = Ctr(name + "d")
            def dma(self, instr):
                instr.then_inc(self.s, 16); self.n += 16; return instr
            def chk(self):
                self.eng.wait_ge(self.s, self.n)
                self.d.inc(self.eng.nop())
                return self.d.n
            def mark(self):
                return self.n
            def chk_for(self, n):
                self.eng.wait_ge(self.s, n)
                self.d.inc(self.eng.nop())
                return self.d.n
            def wait(self, eng, tok=None):
                self.d.wait(eng, tok)

        S.vec = nc.vector; S.act = nc.scalar; S.pe = nc.tensor
        S.g = nc.gpsimd; S.sy = nc.sync

        S.SQ = Q(S.sy, "SQ")
        S.AQ = Q(S.act, "AQ")
        S.GQ = Q(S.g, "GQ")
        S.PEG = Ctr("PEG")
        S.PQ = Ctr("PQ")
        S.VS = Ctr("VS")
        S.AS = Ctr("AS")
        S.EV = Ctr("EV")

        def V(instr):
            return S.VS.inc(instr)

        def VW():
            S.vec.wait_ge(S.VS.s, S.VS.n)

        def EVm():
            S.vec.wait_ge(S.VS.s, S.VS.n)
            S.EV.inc(S.vec.nop())

        S.V = V; S.VW = VW; S.EVm = EVm

        def barrier():
            S.SQ.chk(); S.GQ.chk(); S.AQ.chk()
            nc.all_engine_barrier()
        S.barrier = barrier

        for nm in consts_in:
            dt = consts_in[nm].dtype
            shp = list(consts_in[nm].shape)
            setattr(S, nm, sb("c_" + nm, shp, dt))
        S.rstd_all = sb("rstd_all", (128, NT))
        S.s1t = sb("s1t", (128, OWNT)); S.s2t = sb("s2t", (128, OWNT))
        S.mself = sb("mself", (128, OWNT))
        S.ms1t = sb("ms1t", (128, OWNT)); S.ms2t = sb("ms2t", (128, OWNT))
        S.mscf = sb("mscf", (128, OWNT))
        S.keep_own = sb("keep_own", (128, OWNT, NE))
        S.i1own = sb("i1own", (128, OWNT)); S.i2own = sb("i2own", (128, OWNT))
        S.idxtok = sb("idxtok", (128, RT * NE), I32)
        S.idxdsty = sb("idxdsty", (128, RT * NE), I32)
        S.b1call = sb("b1call", (128, NE * JC))
        S.fb1c = sb("fb1c", (128, JC))
        S.b2all = sb("b2all", (NE + 1, D))

        S.pb = [ctx.enter_context(nc.psum_tensor(f"pb{i}", [128, 512], F32))
                for i in range(7)]
        S.pbh = ctx.enter_context(nc.psum_tensor("pbh", [128, 1024], BF16))

        for nm in consts_in:
            S.SQ.dma(S.sy.dma_start(out=getattr(S, nm)[:], in_=consts_in[nm]))
        ctok = S.SQ.chk()
        for eng in (S.pe, S.vec, S.act, S.g):
            S.SQ.wait(eng, ctok)

        S.bnd_idx = S.g.alloc_register("bnd_idx"); S.g.reg_mov(S.bnd_idx, NE * LCAP - 1)
        S.bnd_xn = S.g.alloc_register("bnd_xn"); S.g.reg_mov(S.bnd_xn, TOKH - 1)
        S.bnd_y = S.g.alloc_register("bnd_y"); S.g.reg_mov(S.bnd_y, 2 * TOK - 1)

        with ExitStack() as s34:
            S.xg_tm = [[sb(f"xg_tm{ss}_{i}", (128, D), BF16, s34)
                        for i in range(RT)] for ss in range(2)]
            S.zb = sb("zb", (128, D), F32, s34)
            S.dst1i = sb("dst1i", (128, OWNT), I32, s34)
            S.dst2i = sb("dst2i", (128, OWNT), I32, s34)
            S.bigt = sb("bigt", (128, RT * NE), I32, s34)
            with ExitStack() as scopeA:
                S.WTall = {w: sb(f"W{w}Tall", (128, DC * D), BF16, scopeA)
                           for w in "fih"}
                S.WT = {w: [_ColView(S.WTall[w], r * D, D) for r in range(DC)]
                        for w in "fih"}
                S.wr_hi = [sb(f"wr_hi{r}", (128, NE), BF16, scopeA)
                           for r in range(DC)]
                S.wr_lo = [sb(f"wr_lo{r}", (128, NE), BF16, scopeA)
                           for r in range(DC)]
                S.xn_fmall = sb("xn_fmall", (128, DC * TOKH), BF16, scopeA)
                S.xn_fm = [_ColView(S.xn_fmall, r * TOKH, TOKH) for r in range(DC)]
                S.U_tm = sb("U_tm", (128, GT * NE), F32, scopeA)
                S.pbh_vs = [0]

                _phase0(S)
                barrier()
                with ExitStack() as scopeB:
                    p0b_tile = _phase0b_emitter(S, scopeB)
                    p1_tile, p1_fin = _phase1_emitter(S, scopeB)
                    _phase2a(S, scopeB, [(3 * DC, p0b_tile, 30), (NT, p1_tile, 22)])
                    p1_fin()
                _phase2b(S)
                _phase3(S)
                barrier()
            if PHASES >= 4:
                _phase4(S)
                barrier()
        if PHASES >= 5:
            _phase5(S)
        S.sy.wait_ge(S.SQ.s, S.SQ.n)
        S.g.wait_ge(S.GQ.s, S.GQ.n)

    return nc, S.taps


def _phase0(S):
    sb = S.sb; V = S.V; VW = S.VW; EVm = S.EVm
    pe, vec, act, g, sy = S.pe, S.vec, S.act, S.g, S.sy
    SQ, GQ, PEG, EV, VS = S.SQ, S.GQ, S.PEG, S.EV, S.VS
    act = S.act
    pbh, pb = S.pbh, S.pb

    with ExitStack() as s0:
        zb = S.zb
        VW(); ins = V(vec.memset(zb[:], 0.0)); EVm()
        EV.wait(act)
        for i in range(2 * OWNT):
            S.AQ.dma(act.dma_start(out=S.Y_dram[128 * i:128 * (i + 1), :], in_=zb[:]))
        S.yz_tok = S.AQ.chk()

        wr_f = [sb(f"wr_f{r}", (128, NE), stk=s0) for r in range(DC)]
        srow = sb("srow", (1, NE), stk=s0)
        sneg = sb("sneg", (1, NE), stk=s0)
        sbc = sb("sbc", (128, NE), stk=s0)
        wrhat = sb("wrhat", (128, NE), stk=s0)

        for r in range(DC):
            SQ.dma(sy.dma_start(out=wr_f[r][:], in_=S.Wr_h[128 * r:128 * (r + 1), :]))
        tok = SQ.chk()
        SQ.dma(sy.dma_start(out=S.b1call[:], in_=S.b1t_h))
        SQ.dma(sy.dma_start(out=S.fb1c[:], in_=S.fb1t_h))
        SQ.dma(sy.dma_start(out=S.b2all[0:NE, :], in_=S.b2_h[:, :]))
        SQ.dma(sy.dma_start(out=S.b2all[NE:NE + 1, :], in_=S.fb2_h))
        SQ.chk()
        SQ.wait(pe, tok)
        EV.wait(pe)
        for r in range(DC):
            last = pe.matmul(pb[5][0:1, 0:NE], S.onesC[:, 0:1], wr_f[r][:],
                             start=(r == 0), stop=(r == DC - 1))
        PEG.inc(last)
        PEG.wait(vec)
        V(vec.tensor_copy(srow[:], pb[5][0:1, 0:NE]))
        VW()
        ins = V(vec.tensor_scalar(sneg[:], srow[:], -1.0 / 1024.0, None, ALU.mult))
        EVm()
        EV.wait(pe)
        VS.wait(pe)
        last = pe.matmul(pb[5][0:128, 0:NE], S.onesR[0:1, :], sneg[:],
                         start=True, stop=True)
        PEG.inc(last)
        PEG.wait(vec)
        V(vec.tensor_copy(sbc[:], pb[5][0:128, 0:NE]))
        for r in range(DC):
            VW()
            V(vec.tensor_tensor(wrhat[:], wr_f[r][:], sbc[:], ALU.add))
            VW()
            V(vec.tensor_copy(S.wr_hi[r][:], wrhat[:]))
            VW()
            V(vec.tensor_tensor(S.wr_lo[r][:], wrhat[:], S.wr_hi[r][:], ALU.subtract))
        VW()
        ins = V(vec.nop())
        EVm()


def _phase0b_emitter(S, stk):
    sb = S.sb; V = S.V; EVm = S.EVm
    pe, vec, act, g = S.pe, S.vec, S.act, S.g
    GQ, PEG, EV, AS_, VS = S.GQ, S.PEG, S.EV, S.AS, S.VS
    pbh = S.pbh

    wrow = [sb(f"wrow{i}", (128, D), BF16, stk) for i in range(2)]
    wsrc = [("f", S.Wf_h), ("i", S.Wi_h), ("h", S.Wh_h)]
    wr_peg = [0, 0]

    def emit_chunk(k):
        w, whbm = wsrc[k // DC]
        ec = k % DC
        par = k % 2
        buf = wrow[par]
        PEG.wait(g, wr_peg[par])
        GQ.dma(g.dma_start(out=buf[:], in_=whbm[128 * ec:128 * (ec + 1), :]))
        tok = GQ.chk()
        GQ.wait(pe, tok)
        VS.wait(pe, S.pbh_vs[0])
        last = None
        for dr in range(DC):
            last = pe.transpose(pbh[:, 128 * dr:128 * (dr + 1)],
                                buf[:, 128 * dr:128 * (dr + 1)], S.idb[:])
        PEG.inc(last)
        wr_peg[par] = S.PEG.n
        PEG.wait(vec, wr_peg[par])
        V(vec.tensor_copy(
            S.WTall[w][:].rearrange("p (r d) -> p r d", r=DC)[:, :,
                                                             128 * ec:128 * (ec + 1)],
            pbh[:].rearrange("p (r q) -> p r q", r=DC)))
        S.pbh_vs[0] = S.VS.n

    return emit_chunk


def _phase1_emitter(S, stk):
    sb = S.sb; V = S.V; VW = S.VW; EVm = S.EVm
    pe, vec, act, g, sy = S.pe, S.vec, S.act, S.g, S.sy
    SQ, GQ, PEG, EV, VS, AS_ = S.AQ, S.GQ, S.PEG, S.EV, S.VS, S.AS
    pbh, pb = S.pbh, S.pb

    if True:
        xtile = [sb(f"xtile{i}", (128, D), stk=stk) for i in range(2)]
        sqs = sb("sqs", (128, D), stk=stk)
        mucol = sb("mucol", (128, 1), stk=stk)
        m2col = sb("m2col", (128, 1), stk=stk)
        varcol = sb("varcol", (128, 1), stk=stk)
        sdcol = sb("sdcol", (128, 1), stk=stk)
        xnbf = [sb(f"xnbf{i}", (128, D), BF16, stk) for i in range(2)]

        vs_xn = [0, 0]
        peg_xb = [0, 0]
        ev_var = [0]
        def emit_tile(i):
            par = i % 2
            buf = xtile[par]
            VS.wait(act, vs_xn[par])
            SQ.dma(act.dma_start(out=buf[:], in_=S.xsl[128 * i:128 * (i + 1), :]))
            ltok = SQ.chk()
            SQ.wait(vec, ltok)
            V(vec.reduce_sum(mucol[:], buf[:], AXX))
            SQ.wait(act, ltok)
            EV.wait(act, ev_var[0])
            AS_.inc(act.activation(sqs[:], buf[:], AF.Square, accum_out=pb[0][:, 0:1]))
            VW(); V(vec.tensor_scalar(mucol[:], mucol[:], 1.0 / 1024.0, None, ALU.mult))
            VW(); V(vec.tensor_tensor(m2col[:], mucol[:], mucol[:], ALU.mult))
            AS_.wait(vec)
            VW()
            ins = V(vec.scalar_tensor_tensor(varcol[:], pb[0][:, 0:1], 1.0 / 1024.0,
                                             m2col[:], ALU.mult, ALU.subtract))
            EVm()
            ev_var[0] = S.EV.n
            VS.wait(act)
            AS_.inc(act.activation(sdcol[:], varcol[:], AF.Sqrt, bias=S.epscol[:, 0:1]))
            AS_.wait(vec)
            V(vec.reciprocal(S.rstd_all[:, i:i + 1], sdcol[:]))
            VW()
            xb = xnbf[par]
            SQ.d.wait(vec)
            PEG.wait(vec, peg_xb[par])
            ins = V(vec.tensor_scalar(xb[:], buf[:], mucol[:], S.rstd_all[:, i:i + 1],
                                      ALU.subtract, ALU.mult))
            EVm()
            vs_xn[par] = S.VS.n
            VS.wait(act)
            SQ.dma(act.dma_start(out=S.xn_dram[128 * i:128 * (i + 1), :], in_=xb[:]))
            SQ.chk()
            VS.wait(pe, max(vs_xn[par], S.pbh_vs[0]))
            for r in range(DC):
                last = pe.transpose(pbh[:, 128 * r:128 * (r + 1)],
                                    xb[:, 128 * r:128 * (r + 1)], S.idb[:])
            PEG.inc(last)
            peg_xb[par] = PEG.n
            PEG.wait(vec, peg_xb[par])
            V(vec.tensor_copy(
                S.xn_fmall[:].rearrange("p (r t) -> p r t", r=DC)[:, :,
                                                                 128 * i:
                                                                 128 * (i + 1)],
                pbh[:].rearrange("p (r q) -> p r q", r=DC)))
            S.pbh_vs[0] = S.VS.n

        def finish():
            VS.wait(act)
            for gi in range(NG):
                for r in range(DC):
                    SQ.dma(act.dma_start(
                        out=S.xnfm_dram[DC * 128 * gi + 128 * r:
                                        DC * 128 * gi + 128 * (r + 1), :],
                        in_=S.xn_fm[r][:, HALO + GSZ * gi:HALO + GSZ * (gi + 1)]))
            _phase1_debug(S, sb)

        return emit_tile, finish


def _phase1_debug(S, sb):
    V = S.V; VW = S.VW; EVm = S.EVm
    act, vec, sy = S.act, S.vec, S.sy
    SQ, VS = S.AQ, S.VS
    if True:
        if KDEBUG:
            SQ.chk()
            with ExitStack() as sdbg:
                xbb = sb("xbb_dbg", (128, D), BF16, sdbg)
                xff = sb("xff_dbg", (128, D), F32, sdbg)
                for i in range(NT):
                    SQ.d.wait(sy)
                    SQ.dma(act.dma_start(out=xbb[:],
                                        in_=S.xn_dram[128 * i:128 * (i + 1), :]))
                    t2 = SQ.chk()
                    SQ.wait(vec, t2)
                    VW(); ins = V(vec.tensor_copy(xff[:], xbb[:]))
                    EVm()
                    VS.wait(act)
                    SQ.dma(act.dma_start(out=S.taps["t_xn"][128 * i:128 * (i + 1), :],
                                        in_=xff[:]))
                    SQ.chk()


def _phase2a(S, stk, tile_cb=None):
    nc = S.nc; sb = S.sb; V = S.V; VW = S.VW; EVm = S.EVm
    pe, vec, act, g, sy = S.pe, S.vec, S.act, S.g, S.sy
    SQ, GQ, PEG, EV, VS, AS_, PQ = S.SQ, S.GQ, S.PEG, S.EV, S.VS, S.AS, S.PQ
    pbh, pb = S.pbh, S.pb
    U_tm = S.U_tm

    with ExitStack() as s2:
        p2 = lambda name, shape, dtype=F32: sb(name, shape, dtype, stk)
        xtf = [p2(f"xtf{i}", (128, 512)) for i in range(10)]
        xth = [p2(f"xth{i}", (128, 512), BF16) for i in range(10)]
        xtl = [p2(f"xtl{i}", (128, 512), BF16) for i in range(10)]
        ufm = [p2(f"ufm{i}", (8, 512)) for i in range(2)]

        NSL = 10
        LSL = [stk.enter_context(nc.semaphore(f"LSL{i}")) for i in range(NSL)]
        CSL = [stk.enter_context(nc.semaphore(f"CSL{i}")) for i in range(NSL)]
        ln_ctr = [0] * NSL
        cs_ctr = [0] * NSL

        sub_done = {}; mm_done = {}; evac_hist = {}; tr_done = {}
        emitted = [0] * len(tile_cb or [])

        def _pace(t):
            for ei, (total, fn, hz) in enumerate(tile_cb or []):
                while emitted[ei] < min(total, ((t + 1) * total) // hz):
                    fn(emitted[ei]); emitted[ei] += 1

        for t in range(32):
            bank8 = (pb[5] if t % 2 == 0 else pb[6])[0:8, 0:512]
            for r in range(DC):
                q = t * DC + r
                bi = q % NSL
                if q >= NSL:
                    sy.wait_ge(CSL[bi], 16 * cs_ctr[bi])
                    VS.wait(sy, sub_done[q - NSL])
                ins = sy.dma_start(out=xtf[bi][:],
                                   in_=S.xt[128 * r:128 * (r + 1), 512 * t:512 * (t + 1)])
                ins.then_inc(LSL[bi], 16); ln_ctr[bi] += 1
                if q >= NSL:
                    PEG.wait(g, mm_done[q - NSL])
                    VS.wait(g, sub_done[q - NSL])
                g.wait_ge(LSL[bi], 16 * ln_ctr[bi])
                ins = g.dma_start(out=xth[bi][:], in_=xtf[bi][:])
                ins.then_inc(CSL[bi], 16); cs_ctr[bi] += 1
                if q >= NSL:
                    PEG.wait(vec, mm_done[q - NSL])
                vec.wait_ge(CSL[bi], 16 * cs_ctr[bi])
                V(vec.tensor_tensor(xtl[bi][:], xtf[bi][:], xth[bi][:], ALU.subtract))
                sub_done[q] = S.VS.n
                pe.wait_ge(CSL[bi], 16 * cs_ctr[bi])
                VS.wait(pe, sub_done[q])
                if r == 0 and t >= 2:
                    EV.wait(pe, evac_hist[t - 2])
                pe.matmul(bank8, S.wr_hi[r][:], xth[bi][:], start=(r == 0), stop=False)
                m2 = pe.matmul(bank8, S.wr_lo[r][:], xth[bi][:], start=False, stop=False)
                PEG.inc(m2)
                last = pe.matmul(bank8, S.wr_hi[r][:], xtl[bi][:], start=False,
                                 stop=(r == DC - 1))
                PEG.inc(last)
                mm_done[q] = PEG.n
            ub = ufm[t % 2]
            PEG.wait(vec)
            if t >= 2:
                PEG.wait(vec, tr_done[t - 2])
            ins = V(vec.tensor_copy(ub[:], bank8))
            EVm()
            evac_hist[t] = EV.n
            pack = (t // 4) % 2
            EV.wait(pe, evac_hist[t])
            for q4 in range(4):
                last = pe.transpose(pb[4][:, 256 * pack + 32 * (t % 4) + 8 * q4:
                                          256 * pack + 32 * (t % 4) + 8 * q4 + 8],
                                    ub[:, 128 * q4:128 * (q4 + 1)], S.idf[0:8, 0:8])
            PEG.inc(last)
            tr_done[t] = PEG.n
            if t % 4 == 3:
                gt0 = (t - 3) * 4
                PEG.wait(vec)
                ins = V(vec.tensor_copy(U_tm[:, NE * gt0:NE * gt0 + 128],
                                        pb[4][:, 256 * pack:256 * pack + 128]))
                EVm()
            _pace(t)

        if KDEBUG:
            VS.wait(sy)
            AS_.wait(sy)
            SQ.dma(sy.dma_start(out=S.taps["t_utm"], in_=U_tm[:]))
            SQ.chk()


def _phase2b(S):
    nc = S.nc; sb = S.sb; V = S.V; VW = S.VW; EVm = S.EVm
    pe, vec, act, g, sy = S.pe, S.vec, S.act, S.g, S.sy
    SQ, GQ, PEG, EV, VS, AS_, PQ = S.SQ, S.GQ, S.PEG, S.EV, S.VS, S.AS, S.PQ
    pbh, pb = S.pbh, S.pb
    U_tm3 = S.U_tm[:].rearrange("p (a b) -> p a b", b=NE)

    with ExitStack() as s2:
        p2 = lambda name, shape, dtype=F32: sb(name, shape, dtype, s2)

        for e in (sy, g, act, vec):
            PEG.wait(e); VS.wait(e); AS_.wait(e); S.AQ.wait(e)
        EV.wait(pe); AS_.wait(pe)

        m1 = p2("m1", (128, GT)); i1f = p2("i1f", (128, GT))
        m2 = p2("m2", (128, GT)); i2f = p2("i2f", (128, GT))
        equ = p2("equ", (128, GT, NE))
        scr3 = p2("scr3", (128, GT, NE))
        U2s = p2("U2s", (128, GT, NE))

        b3 = lambda colap: colap.unsqueeze(1).to_broadcast([128, GT, NE])
        bm = lambda m: m.unsqueeze(2).to_broadcast([128, GT, NE])
        iota3 = S.iota8[:].unsqueeze(1).to_broadcast([128, GT, NE])

        iotam3 = S.iotam8[:].unsqueeze(1).to_broadcast([128, GT, NE])
        BIGM = 1.0e38
        VW(); V(vec.tensor_reduce(m1[:], U_tm3, AXX, ALU.max))
        VW(); V(vec.tensor_tensor(equ[:], U_tm3, bm(m1[:]), ALU.is_ge))
        VW(); V(vec.tensor_tensor(scr3[:], equ[:], iotam3, ALU.mult))
        VW(); V(vec.tensor_scalar(scr3[:], scr3[:], 8.0, None, ALU.add))
        VW(); V(vec.tensor_reduce(i1f[:], scr3[:], AXX, ALU.min))
        VW(); V(vec.tensor_tensor(equ[:], iota3, bm(i1f[:]), ALU.is_equal))
        VW(); V(vec.tensor_scalar(U2s[:], U_tm3, BIGM, None, ALU.add))
        VW(); V(vec.tensor_tensor(U2s[:], equ[:], U2s[:], ALU.mult))
        VW(); V(vec.tensor_tensor(U2s[:], U_tm3, U2s[:], ALU.subtract))
        VW(); V(vec.tensor_reduce(m2[:], U2s[:], AXX, ALU.max))
        VW(); V(vec.tensor_tensor(equ[:], U2s[:], bm(m2[:]), ALU.is_ge))
        VW(); V(vec.tensor_tensor(scr3[:], equ[:], iotam3, ALU.mult))
        VW(); V(vec.tensor_scalar(scr3[:], scr3[:], 8.0, None, ALU.add))
        VW(); V(vec.tensor_reduce(i2f[:], scr3[:], AXX, ALU.min))

        nfb = p2("nfb", (128, 128))
        z128 = p2("z128", (128, 128))
        aT = p2("aT", (128, 128), BF16)
        a_sm = p2("a_sm", (128, 128))
        elig = p2("elig", (128, 128))
        rowc = p2("rowc", (128, 128))
        rtot = p2("rtot", (128, 1), BF16)
        rank = p2("rank", (128, 128))
        c1t = p2("c1t", (128, 128))
        keepC = p2("keepC", (128, 128))
        overC = p2("overC", (128, 128))
        scr128 = p2("scr128", (128, 128))
        sel16 = p2("sel16", (128, 128))

        def extract_own(src_ap, dst_ap, seg_src):
            if not seg_src:
                VS.wait(pe)
                EV.wait(pe)
                last = pe.transpose(pb[4][:, 0:128], src_ap, S.idf[:])
                PEG.inc(last)
                PEG.wait(vec)
                VW(); ins = V(vec.tensor_copy(a_sm[:], pb[4][:, 0:128]))
                EVm()
                seg = a_sm[:]
            else:
                seg = src_ap
            VS.wait(pe)
            EV.wait(pe)
            last = pe.matmul(pb[4][0:OWNT, 0:128], S.sown[:], seg, start=True, stop=True)
            PEG.inc(last)
            PEG.wait(vec)
            VW(); ins = V(vec.tensor_copy(sel16[0:OWNT, :], pb[4][0:OWNT, 0:128]))
            EVm()
            VS.wait(pe)
            EV.wait(pe)
            last = pe.transpose(pb[4][:, 128:256], sel16[:], S.idf[:])
            PEG.inc(last)
            PEG.wait(vec)
            VW(); ins = V(vec.tensor_copy(dst_ap, pb[4][:, 128:128 + OWNT]))
            EVm()

        VW(); V(vec.memset(nfb[:], 1.0))
        VW(); V(vec.memset(z128[:], 0.0))
        for e in range(NE):
            fe = float(e)
            VW(); V(vec.tensor_scalar(scr128[:], i1f[:], fe, None, ALU.is_equal))
            VW(); V(vec.tensor_scalar(overC[:], i2f[:], fe, None, ALU.is_equal))
            VW(); ins = V(vec.tensor_tensor(aT[:], scr128[:], overC[:], ALU.add))
            EVm()
            EV.wait(pe)
            VS.wait(pe)
            last = pe.transpose(pbh[:, 0:128], aT[:], S.idb[:])
            PEG.inc(last)
            PEG.wait(vec)
            VW(); ins = V(vec.tensor_copy(a_sm[:], pbh[:, 0:128]))
            EVm()
            VW(); V(vec.tensor_tensor(elig[:], a_sm[:], nfb[:], ALU.mult))
            VW(); V(vec.tensor_tensor_scan(rowc[:], elig[:], z128[:], 0.0,
                                           ALU.add, ALU.add))
            VW(); ins = V(vec.tensor_copy(rtot[:], rowc[:, 127:128]))
            EVm()
            EV.wait(pe)
            VS.wait(pe)
            last = pe.matmul(pb[6][:, 0:1], S.cumSW[:], rtot[:], start=True, stop=True)
            PEG.inc(last)
            PEG.wait(vec)
            VW(); ins = V(vec.tensor_scalar(rank[:], rowc[:], pb[6][:, 0:1], None, ALU.add))
            EVm()
            VW(); V(vec.tensor_scalar(c1t[:], rank[:], float(CAP), None, ALU.is_le))
            VW(); V(vec.tensor_tensor(keepC[:], elig[:], c1t[:], ALU.mult))
            VW(); V(vec.tensor_tensor(overC[:], elig[:], keepC[:], ALU.subtract))
            VW(); V(vec.tensor_tensor(nfb[:], nfb[:], overC[:], ALU.subtract))
            extract_own(keepC[:], S.keep_own[:, :, e:e + 1].squeeze(2), seg_src=True)

        extract_own(m1[:], S.s1t[:, 0:OWNT], seg_src=False)
        extract_own(m2[:], S.s2t[:, 0:OWNT], seg_src=False)
        extract_own(i1f[:], S.i1own[:, 0:OWNT], seg_src=False)
        extract_own(i2f[:], S.i2own[:, 0:OWNT], seg_src=False)

        d12 = p2("d12", (128, OWNT))
        z12 = p2("z12", (128, OWNT))
        k1c = p2("k1c", (128, OWNT)); k2c = p2("k2c", (128, OWNT))
        o3 = p2("o3", (128, OWNT, NE))
        lcum = p2("lcum", (128, OWNT, NE))
        slot1 = p2("slot1", (128, OWNT)); slot2 = p2("slot2", (128, OWNT))
        msel = p2("msel", (128, OWNT))
        scrO = p2("scrO", (128, OWNT))
        scrO2 = p2("scrO2", (128, OWNT))
        keep_bf = p2("keep_bf", (128, OWNT, NE), BF16)
        dst1i = S.dst1i
        dst2i = S.dst2i
        bigt = S.bigt

        VW(); V(vec.tensor_tensor(d12[:], S.s1t[:], S.s2t[:], ALU.subtract))
        VW(); V(vec.tensor_tensor(z12[:], d12[:], S.rstd_all[:, 2:2 + OWNT], ALU.mult))
        VS.wait(act)
        AS_.inc(act.activation(S.s1t[:], z12[:], AF.Sigmoid))
        AS_.wait(vec)
        VW(); V(vec.tensor_scalar(S.s2t[:], S.s1t[:], -1.0, 1.0, ALU.mult, ALU.add))

        bO = lambda m: m.unsqueeze(2).to_broadcast([128, OWNT, NE])
        iotaO = S.iota8[:].unsqueeze(1).to_broadcast([128, OWNT, NE])
        sO = scr3[:, 0:OWNT, :]
        VW(); V(vec.tensor_tensor(o3[:], iotaO, bO(S.i1own[:]), ALU.is_equal))
        VW(); V(vec.tensor_tensor(sO, o3[:], S.keep_own[:], ALU.mult))
        VW(); V(vec.tensor_reduce(k1c[:], sO, AXX, ALU.max))
        VW(); V(vec.tensor_tensor(o3[:], iotaO, bO(S.i2own[:]), ALU.is_equal))
        VW(); V(vec.tensor_tensor(sO, o3[:], S.keep_own[:], ALU.mult))
        VW(); V(vec.tensor_reduce(k2c[:], sO, AXX, ALU.max))
        VW(); ins = V(vec.tensor_copy(keep_bf[:], S.keep_own[:]))
        EVm()
        EV.wait(pe)
        VS.wait(pe)
        for i in range(OWNT):
            dst = pb[5][:, 8 * i:8 * (i + 1)]
            for jj in range(i + 1):
                last = pe.matmul(dst, lhs if False else (S.cumW[:] if jj == i else S.onesB[:]),
                                 keep_bf[:, jj, :], start=(jj == 0), stop=(jj == i))
        PEG.inc(last)
        PEG.wait(vec)
        VW()
        ins = V(vec.tensor_copy(lcum[:],
                                pb[5][:, 0:128].rearrange("p (a b) -> p a b", a=OWNT)))
        EVm()
        VW(); V(vec.tensor_scalar(sO, lcum[:], 1.0, None, ALU.subtract))
        VW(); V(vec.tensor_tensor(o3[:], iotaO, bO(S.i1own[:]), ALU.is_equal))
        VW(); V(vec.tensor_tensor(U2s[:, 0:OWNT, :], o3[:], sO, ALU.mult))
        VW(); V(vec.tensor_reduce(slot1[:], U2s[:, 0:OWNT, :], AXX, ALU.max))
        VW(); V(vec.tensor_tensor(o3[:], iotaO, bO(S.i2own[:]), ALU.is_equal))
        VW(); V(vec.tensor_tensor(U2s[:, 0:OWNT, :], o3[:], sO, ALU.mult))
        VW(); V(vec.tensor_reduce(slot2[:], U2s[:, 0:OWNT, :], AXX, ALU.max))
        VW(); V(vec.tensor_scalar(scrO[:], slot1[:], float(LCAP), None, ALU.is_lt))
        VW(); V(vec.tensor_tensor(k1c[:], k1c[:], scrO[:], ALU.mult))
        VW(); V(vec.tensor_scalar(scrO[:], slot2[:], float(LCAP), None, ALU.is_lt))
        VW(); V(vec.tensor_tensor(k2c[:], k2c[:], scrO[:], ALU.mult))
        VW(); V(vec.tensor_tensor(S.mself[:], k1c[:], k2c[:], ALU.mult))
        VW(); V(vec.tensor_copy(msel[:], S.mself[:]))
        VW(); V(vec.tensor_tensor(S.ms1t[:], S.mself[:], S.s1t[:], ALU.mult))
        VW(); V(vec.tensor_tensor(S.ms2t[:], S.mself[:], S.s2t[:], ALU.mult))
        VW(); V(vec.tensor_scalar(S.mscf[:], S.mself[:], -1.0, 1.0,
                                  ALU.mult, ALU.add))

        def build_dest(slot, ksel, eown, dst_i32_ap):
            VW(); V(vec.tensor_scalar(scrO[:], slot[:], 128.0, None, ALU.is_ge))
            for thr in (256.0, 384.0, 512.0):
                VW(); V(vec.tensor_scalar(scrO2[:], slot[:], thr, None, ALU.is_ge))
                VW(); V(vec.tensor_tensor(scrO[:], scrO[:], scrO2[:], ALU.add))
            VW(); V(vec.scalar_tensor_tensor(scrO2[:], scrO[:], -128.0, slot[:],
                                             ALU.mult, ALU.add))
            VW(); V(vec.tensor_scalar(scrO2[:], scrO2[:], float(RT * NE), None, ALU.mult))
            VW(); V(vec.tensor_tensor(scrO2[:], scrO2[:], scrO[:], ALU.add))
            VW(); V(vec.scalar_tensor_tensor(scrO2[:], eown[:], float(RT), scrO2[:],
                                             ALU.mult, ALU.add))
            VW(); V(vec.tensor_scalar(scrO2[:], scrO2[:], BIG, None, ALU.add))
            VW(); V(vec.scalar_tensor_tensor(scrO2[:], ksel[:], -BIG, scrO2[:],
                                             ALU.mult, ALU.add))
            VW(); V(vec.tensor_scalar(scrO2[:], scrO2[:], E23, None, ALU.add))
            VW(); ins = V(vec.tensor_scalar(dst_i32_ap, scrO2[:].bitcast(I32),
                                            S.imask[:, 0:1], None, ALU.bitwise_and))
            EVm()

        build_dest(slot1, k1c, S.i1own, dst1i[:])
        build_dest(slot2, k2c, S.i2own, dst2i[:])

        VW(); ins = V(vec.memset(bigt[:], BIG_I))
        EVm()
        EV.wait(g)
        VS.wait(g)
        GQ.dma(g.dma_start(out=S.idx_dram.rearrange("(p a) b -> p (a b)", p=128),
                           in_=bigt[:]))
        GQ.dma(g.dma_start(out=S.dst_dram.rearrange("(p a) b -> p (a b)", p=128),
                           in_=bigt[:]))
        GQ.chk()
        for i in range(OWNT):
            for dcol, pd in ((dst1i, S.pdst1), (dst2i, S.pdst2)):
                GQ.dma(g.indirect_dma_start(
                    out=S.idx_dram,
                    out_offset=IndirectOffsetOnAxis(dcol[:, i:i + 1], 0),
                    in_=S.paytok[:, i:i + 1], in_offset=None,
                    bounds_check=S.bnd_idx, oob_is_err=False))
                GQ.dma(g.indirect_dma_start(
                    out=S.dst_dram,
                    out_offset=IndirectOffsetOnAxis(dcol[:, i:i + 1], 0),
                    in_=pd[:, i:i + 1], in_offset=None,
                    bounds_check=S.bnd_idx, oob_is_err=False))
        GQ.chk()
        GQ.dma(g.dma_start(out=S.idxtok[:],
                           in_=S.idx_dram.rearrange("(p a) b -> p (a b)", p=128)))
        GQ.dma(g.dma_start(out=S.idxdsty[:],
                           in_=S.dst_dram.rearrange("(p a) b -> p (a b)", p=128)))
        GQ.chk()
        if KDEBUG:
            GQ.dma(g.dma_start(out=S.taps["t_idx"][:, 0:RT * NE], in_=S.idxtok[:]))
            GQ.dma(g.dma_start(out=S.taps["t_idx"][:, RT * NE:], in_=S.idxdsty[:]))
            GQ.chk()
            tb = p2("tb", (128, OWNT, 12))
            VW(); V(vec.memset(tb[:], 0.0))
            for ci, src in enumerate([S.s1t, S.s2t, S.mself, S.i1own, S.i2own,
                                      slot1, slot2, k1c, k2c]):
                VW(); V(vec.tensor_copy(tb[:, :, ci:ci + 1].squeeze(2), src[:, 0:OWNT]))
            VW(); ins = V(vec.tensor_copy(tb[:, :, 9:10].squeeze(2),
                                          S.rstd_all[:, 2:2 + OWNT]))
            EVm()
            VS.wait(sy)
            SQ.dma(sy.dma_start(out=S.taps["t_route"],
                                in_=tb[:].rearrange("p a b -> p (a b)")))
            kk = p2("kk", (128, OWNT * NE))
            VW(); ins = V(vec.tensor_copy(kk[:].rearrange("p (a b) -> p a b", a=OWNT),
                                          S.keep_own[:]))
            EVm()
            VS.wait(sy)
            SQ.dma(sy.dma_start(out=S.taps["t_keep"], in_=kk[:]))
            SQ.chk()


def _phase3(S):
    sb = S.sb; V = S.V; VW = S.VW; EVm = S.EVm
    pe, vec, act, g, sy = S.pe, S.vec, S.act, S.g, S.sy
    SQ, GQ, PEG, EV, VS, AS_ = S.SQ, S.GQ, S.PEG, S.EV, S.VS, S.AS
    pbh, pb = S.pbh, S.pb

    with ExitStack() as s3:
        fbuf = [sb(f"fbuf{i}", (128, TOKH), stk=s3) for i in range(2)]
        vbuf = [sb(f"vbuf{i}", (128, TOKH), stk=s3) for i in range(2)]
        sgbuf = [sb(f"sgbuf{i}", (128, TOKH), stk=s3) for i in range(2)]
        hbuf = [sb(f"hbuf{i}", (128, TOKH), stk=s3) for i in range(2)]
        h4 = [sb(f"h4_{i}", (128, 512), stk=s3) for i in range(2)]

        SUBS = [(0, 512), (512, 512), (1024, 512), (1536, 512), (2048, 256)]
        bank_free = []
        scan_done = [0, 0]
        tr_done = [0, 0]
        h4_st = [0, 0]
        pb4_as = [0]
        gctr = 0
        EV.wait(pe)
        AS_.wait(pe)
        VS.wait(pe)

        for r in range(DC):
            GQ.dma(g.dma_start(out=S.fw1bf[128 * r:128 * (r + 1), :],
                               in_=S.fw1_h[128 * r:128 * (r + 1), :]))
        for j in range(0, JC, 4):
            GQ.dma(g.dma_start(out=S.fw2bf[128 * j:128 * (j + 4), :],
                               in_=S.fw2_h[128 * j:128 * (j + 4), :]))
        S.SQ.d.wait(g)
        S.g_toks = []
        for e2 in range(2):
            for rt in range(RT):
                GQ.dma(g.indirect_dma_start(
                    out=S.xg_tm[e2][rt][:], out_offset=None,
                    in_=S.xn_dram,
                    in_offset=IndirectOffsetOnAxis(
                        S.idxtok[:, RT * e2 + rt:RT * e2 + rt + 1], 0),
                    bounds_check=S.bnd_xn, oob_is_err=False))
            S.g_toks.append(GQ.chk())
        for e in (act, vec):
            SQ.wait(e); PEG.wait(e)
        AS_.wait(vec); VS.wait(act); EV.wait(act); EV.wait(vec)

        def _emit_h_transposes(cc):
            parc = cc % 2
            hbc = hbuf[parc]
            for grp in range(4):
                VS.wait(pe, scan_done[parc])
                S.AQ.d.wait(pe, h4_st[parc])
                AS_.wait(pe, pb4_as[0])
                for q in range(4):
                    i = 4 * grp + q
                    last = pe.transpose(pb[4][:, 128 * q:128 * (q + 1)],
                                        hbc[:, HALO + 128 * i:HALO + 128 * (i + 1)],
                                        S.idf[:])
                PEG.inc(last)
                PEG.wait(act)
                S.AQ.d.wait(act, h4_st[parc])
                AS_.inc(act.activation(h4[parc][:], pb[4][:, 0:512], AF.Copy))
                pb4_as[0] = S.AS.n
                AS_.wait(act)
                for q in range(4):
                    i = 4 * grp + q
                    S.AQ.dma(act.dma_start(
                        out=S.h_dram[128 * i:128 * (i + 1),
                                     128 * cc:128 * (cc + 1)],
                        in_=h4[parc][:, 128 * q:128 * (q + 1)]))
                h4_st[parc] = S.AQ.chk()
            tr_done[parc] = PEG.n

        for c in range(DC):
            par = c % 2
            fb_, vb_, sg_, hb_ = fbuf[par], vbuf[par], sgbuf[par], hbuf[par]
            VS.wait(act, scan_done[par])
            for (t0, tl) in SUBS:
                for qi, wname in enumerate("fih"):
                    bank = pb[gctr % 4]
                    if gctr >= 4:
                        kind, val = bank_free[gctr - 4]
                        (AS_ if kind == "A" else VS).wait(pe, val)
                    for r in range(DC):
                        last = pe.matmul(bank[:, 0:tl],
                                         S.WT[wname][r][:, 128 * c:128 * (c + 1)],
                                         S.xn_fm[r][:, t0:t0 + tl],
                                         start=(r == 0), stop=(r == DC - 1))
                    PEG.inc(last)
                    PEG.wait(act)
                    if wname == "h":
                        AS_.inc(act.activation(sg_[:, t0:t0 + tl], bank[:, 0:tl],
                                               AF.Sigmoid))
                        AS_.wait(vec)
                        PEG.wait(vec, tr_done[par])
                        VW()
                        ins = V(vec.scalar_tensor_tensor(hb_[:, t0:t0 + tl],
                                                         bank[:, 0:tl], 0.5,
                                                         sg_[:, t0:t0 + tl],
                                                         ALU.add, ALU.max))
                        bank_free.append(("V", VS.n))
                    else:
                        dest = fb_ if wname == "f" else vb_
                        AS_.inc(act.activation(dest[:, t0:t0 + tl], bank[:, 0:tl],
                                               AF.Sigmoid))
                        bank_free.append(("A", AS_.n))
                    gctr += 1
            AS_.wait(vec)
            VW(); V(vec.tensor_tensor(sg_[:], vb_[:], hb_[:], ALU.mult))
            VW(); V(vec.tensor_scalar(fb_[:, 0:HALO], fb_[:, 0:HALO],
                                      S.halo_s[:, 0:1], None, ALU.mult))
            VW(); V(vec.tensor_scalar(sg_[:, 0:HALO], sg_[:, 0:HALO],
                                      S.halo_s[:, 0:1], None, ALU.mult))
            VW(); V(vec.tensor_scalar(sg_[:, HALO - 1:HALO], sg_[:, HALO - 1:HALO],
                                      S.halo_h[:, 0:1], None, ALU.add))
            VW(); ins = V(vec.tensor_tensor_scan(hb_[:], fb_[:], sg_[:], 0.0,
                                                 ALU.mult, ALU.add))
            scan_done[par] = VS.n
            if c >= 1:
                _emit_h_transposes(c - 1)
            if c == DC - 1:
                _emit_h_transposes(c)

        if KDEBUG:
            with ExitStack() as sd:
                hh = sb("hh_dbg", (128, D), stk=sd)
                S.AQ.d.wait(sy)
                for i in range(OWNT):
                    SQ.d.wait(sy)
                    SQ.dma(sy.dma_start(out=hh[:], in_=S.h_dram[128 * i:128 * (i + 1), :]))
                    SQ.chk()
                    SQ.dma(sy.dma_start(out=S.taps["t_h"][128 * i:128 * (i + 1), :],
                                        in_=hh[:]))
                    SQ.chk()


def _phase4(S):
    sb = S.sb; V = S.V; VW = S.VW; EVm = S.EVm
    pe, vec, act, g, sy = S.pe, S.vec, S.act, S.g, S.sy
    SQ, GQ, PEG, EV, VS, AS_, PQ = S.SQ, S.GQ, S.PEG, S.EV, S.VS, S.AS, S.PQ
    pbh, pb = S.pbh, S.pb

    with ExitStack() as s4:
        p4 = lambda name, shape, dtype=F32: sb(name, shape, dtype, s4)
        W2ball = p4("W2ball", (128, JC * D), BF16)
        W2b = [_ColView(W2ball, j * D, D) for j in range(JC)]
        W1gall = [p4(f"W1gall{pp}", (128, DC * 512), BF16) for pp in range(4)]
        W1g = [[_ColView(W1gall[pp], r * 512, 512) for r in range(DC)]
               for pp in range(4)]
        ubuf = [p4(f"ubuf{j}", (128, LCAP), BF16) for j in range(JC)]
        xg_fm = [p4(f"xg_fm{r}", (128, LCAP), BF16) for r in range(DC)]
        b2bc = p4("b2bc", (128, D))
        yesb = p4("yesb", (128, D))

        S.AQ.d.wait(g, S.yz_tok)

        HALVES = [(0, 384), (384, 256)]
        trs_peg_prev = 0
        s2_peg_prev = 0
        b2bc_ev = 0
        evh = []
        scat_tok = 0
        trs_peg_set = [0, 0]

        def _gathers(e2, defer=False):
            ss = e2 % 2
            PEG.wait(g, trs_peg_set[ss])
            for rt in range(RT):
                GQ.dma(g.indirect_dma_start(
                    out=S.xg_tm[ss][rt][:], out_offset=None,
                    in_=S.xn_dram,
                    in_offset=IndirectOffsetOnAxis(
                        S.idxtok[:, RT * e2 + rt:RT * e2 + rt + 1], 0),
                    bounds_check=S.bnd_xn, oob_is_err=False))
            return GQ.mark() if defer else GQ.chk()

        g_marks = [0, 0]
        g_toks = list(S.g_toks)

        s1_peg_step = {}
        w1_done = {}

        def _w1_issue(n2):
            e2, jg2 = divmod(n2, 8)
            if e2 >= NE:
                return
            PEG.wait(g, s1_peg_step.get(n2 - 4, 0))
            GQ.dma(g.dma_start(
                out=W1gall[n2 % 4][:].rearrange("p (r c) -> p r c", r=DC),
                in_=S.W1_h[e2, :, 512 * jg2:512 * (jg2 + 1)]
                .rearrange("(r p) c -> p r c", p=128)))

        for n2 in range(3):
            _w1_issue(n2)
        w1_done[0] = w1_done[1] = w1_done[2] = GQ.chk()

        for e in range(NE):
            g_tok = g_toks[e % 2]
            PEG.wait(g, s2_peg_prev)
            EV.wait(pe, b2bc_ev)
            sel = S.idf[0:NE + 1, e:e + 1].to_broadcast([NE + 1, 128])
            PEG.inc(pe.matmul(pb[6][:, 0:512], sel, S.b2all[:, 0:512],
                              start=True, stop=True))
            PEG.inc(pe.matmul(pb[5][:, 0:512], sel, S.b2all[:, 512:1024],
                              start=True, stop=True))
            PEG.wait(vec)
            V(vec.tensor_copy(b2bc[:, 0:512], pb[6][:, 0:512]))
            ins = V(vec.tensor_copy(b2bc[:, 512:1024], pb[5][:, 0:512]))
            EVm()
            b2bc_ev = S.EV.n

            GQ.wait(pe, g_tok)
            for rt in range(RT):
                EV.wait(pe)
                for r in range(DC):
                    last = pe.transpose(pbh[:, 128 * r:128 * (r + 1)],
                                        S.xg_tm[e % 2][rt][:, 128 * r:128 * (r + 1)],
                                        S.idb[:])
                PEG.inc(last)
                PEG.wait(vec)
                ins = None
                for r in range(DC):
                    ins = V(vec.tensor_copy(xg_fm[r][:, 128 * rt:128 * (rt + 1)],
                                            pbh[:, 128 * r:128 * (r + 1)]))
                EVm()
            trs_peg_set[e % 2] = PEG.n

            gelu_hist = []
            for jg in range(8):
                n = 8 * e + jg
                par = n % 4
                if n not in w1_done:
                    w1_done[n] = GQ.chk()
                if jg < 4:
                    w2v = W2ball[:].rearrange("p (j c) -> p j c", c=D)
                    for b in (2 * jg, 2 * jg + 1):
                        GQ.dma(g.dma_start(
                            out=w2v[:, 4 * b:4 * (b + 1), :],
                            in_=S.W2_h[e, 512 * b:512 * (b + 1), :]
                            .rearrange("(q p) c -> p q c", p=128)))
                _w1_issue(n + 3)
                if jg == 2 and e + 2 < NE:
                    g_marks[e % 2] = _gathers(e + 2, defer=True)
                if jg == 6 and e + 2 < NE:
                    g_toks[e % 2] = GQ.chk()
                if jg == 7:
                    w2_tok = GQ.chk()
                GQ.wait(pe, w1_done[n])
                VS.wait(pe)
                for j4 in range(4):
                    j = 4 * jg + j4
                    for hi, (h0, hl) in enumerate(HALVES):
                        gidx = len(gelu_hist)
                        bank = pb[gidx % 4]
                        if gidx >= 4:
                            AS_.wait(pe, gelu_hist[gidx - 4])
                        elif evh:
                            EV.wait(pe, evh[-1])
                        for r in range(DC):
                            last = pe.matmul(bank[:, 0:hl],
                                             W1g[par][r][:, 128 * j4:128 * (j4 + 1)],
                                             xg_fm[r][:, h0:h0 + hl],
                                             start=(r == 0), stop=(r == DC - 1))
                        PEG.inc(last)
                        PEG.wait(act)
                        AS_.inc(act.activation(ubuf[j][:, h0:h0 + hl], bank[:, 0:hl],
                                               AF.Gelu,
                                               bias=S.b1call[:, JC * e + j:
                                                             JC * e + j + 1]))
                        gelu_hist.append(AS_.n)
                s1_peg_step[n] = PEG.n

            GQ.wait(pe, w2_tok)
            AS_.wait(pe)
            for rt in range(RT):
                for dh in range(2):
                    bank = pb[len(evh) % 4]
                    if len(evh) >= 4:
                        EV.wait(pe, evh[len(evh) - 4])
                    for j in range(JC):
                        last = pe.matmul(bank[:, 0:512],
                                         ubuf[j][:, 128 * rt:128 * (rt + 1)],
                                         W2b[j][:, 512 * dh:512 * (dh + 1)],
                                         start=(j == 0), stop=(j == JC - 1))
                    PEG.inc(last)
                    PEG.wait(vec)
                    GQ.d.wait(vec, scat_tok)
                    ins = V(vec.tensor_tensor(yesb[:, 512 * dh:512 * (dh + 1)],
                                              bank[:, 0:512],
                                              b2bc[:, 512 * dh:512 * (dh + 1)],
                                              ALU.add))
                    EVm()
                    evh.append(S.EV.n)
                EV.wait(g)
                GQ.dma(g.indirect_dma_start(
                    out=S.Y_dram,
                    out_offset=IndirectOffsetOnAxis(
                        S.idxdsty[:, RT * e + rt:RT * e + rt + 1], 0),
                    in_=yesb[:], in_offset=None,
                    bounds_check=S.bnd_y, oob_is_err=False))
                scat_tok = GQ.chk()
            s2_peg_prev = PEG.n

        if KDEBUG:
            GQ.chk()
            with ExitStack() as sd:
                yy = sb("yy_dbg", (128, D), stk=sd)
                for i in range(2 * OWNT):
                    SQ.d.wait(sy)
                    GQ.d.wait(sy)
                    SQ.dma(sy.dma_start(out=yy[:], in_=S.Y_dram[128 * i:128 * (i + 1), :]))
                    SQ.chk()
                    SQ.dma(sy.dma_start(out=S.taps["t_y"][128 * i:128 * (i + 1), :],
                                        in_=yy[:]))
                    SQ.chk()


def _phase5(S):
    sb = S.sb; V = S.V; VW = S.VW; EVm = S.EVm
    pe, vec, act, g, sy = S.pe, S.vec, S.act, S.g, S.sy
    SQ, GQ, PEG, EV, VS, AS_ = S.SQ, S.GQ, S.PEG, S.EV, S.VS, S.AS
    pbh, pb = S.pbh, S.pb

    with ExitStack() as s5:
        p5 = lambda name, shape, dtype=F32: sb(name, shape, dtype, s5)
        fw1b = [p5(f"fw1b{r}", (128, DFF), BF16) for r in range(DC)]
        fw2b = [p5(f"fw2b{j}", (128, D), BF16) for j in range(JC)]
        u2 = [p5(f"u2_{j}", (128, GSZ), BF16) for j in range(JC)]
        xsn2 = [[p5(f"xsn{p}_{r}", (128, GSZ), BF16) for r in range(DC)]
                for p in range(2)]
        yfb = p5("yfb", (128, D))
        xres = p5("xres", (128, D))
        hres = p5("hres", (128, D))
        y1 = p5("y1", (128, D))
        y2 = p5("y2", (128, D))

        for r in range(DC):
            qq = (SQ, S.AQ)[r % 2]
            qq.dma(qq.eng.dma_start(out=fw1b[r][:],
                                    in_=S.fw1bf[128 * r:128 * (r + 1), :]))
        fw1_toks = (SQ.chk(), S.AQ.chk())
        for j in range(JC):
            qq = (GQ, SQ, S.AQ)[j % 3]
            qq.dma(qq.eng.dma_start(out=fw2b[j][:],
                                    in_=S.fw2bf[128 * j:128 * (j + 1), :]))
        fw2_toks = (GQ.chk(), SQ.chk(), S.AQ.chk())


        s1_peg = [0, 0]
        evh5 = []
        xs_toks = [None, None]
        S.out_st_tok = 0

        aq_xs = [None]

        def _xsn_load(gi2):
            par2 = gi2 % 2
            t02 = gi2 * GSZ
            PEG.wait(sy, s1_peg[par2])
            base = DC * 128 * gi2
            rr = range(DC // 2, DC) if gi2 == 0 else range(DC)
            for r in rr:
                SQ.dma(sy.dma_start(out=xsn2[par2][r][:],
                                    in_=S.xnfm_dram[base + 128 * r:
                                                    base + 128 * (r + 1), :]))
            xs_toks[par2] = SQ.chk()
            if gi2 == 0:
                for r in range(DC // 2):
                    S.AQ.dma(act.dma_start(
                        out=xsn2[par2][r][:],
                        in_=S.xnfm_dram[base + 128 * r:base + 128 * (r + 1), :]))
                aq_xs[0] = S.AQ.chk()

        _xsn_load(0)
        for gi in range(NG):
            t0 = gi * GSZ
            par = gi % 2
            xsn = xsn2[par]
            SQ.wait(pe, xs_toks[par])
            if gi == 0:
                S.AQ.wait(pe, aq_xs[0])
                SQ.wait(pe, fw1_toks[0]); S.AQ.wait(pe, fw1_toks[1])
            if evh5:
                EV.wait(pe, evh5[-1])
            gelu_hist = []
            for j in range(JC):
                bank = pb[j % 4]
                if j >= 4:
                    AS_.wait(pe, gelu_hist[j - 4])
                for r in range(DC):
                    last = pe.matmul(bank[:, 0:GSZ], fw1b[r][:, 128 * j:128 * (j + 1)],
                                     xsn[r][:], start=(r == 0), stop=(r == DC - 1))
                PEG.inc(last)
                PEG.wait(act)
                AS_.inc(act.activation(u2[j][:], bank[:, 0:GSZ], AF.Gelu,
                                       bias=S.fb1c[:, j:j + 1]))
                gelu_hist.append(AS_.n)
            s1_peg[par] = PEG.n
            if gi + 1 < NG:
                _xsn_load(gi + 1)
            if gi == 0:
                GQ.wait(pe, fw2_toks[0]); SQ.wait(pe, fw2_toks[1])
                S.AQ.wait(pe, fw2_toks[2])
            AS_.wait(pe)
            for tt2 in range(GSZ // 128):
                tt = gi * (GSZ // 128) + tt2
                for dh in range(2):
                    bank = pb[(2 * tt2 + dh) % 4]
                    if len(evh5) >= 4:
                        EV.wait(pe, evh5[len(evh5) - 4])
                    for j in range(JC):
                        pe.matmul(bank[:, 0:512],
                                  u2[j][:, 128 * tt2:128 * (tt2 + 1)],
                                  fw2b[j][:, 512 * dh:512 * (dh + 1)],
                                  start=(j == 0), stop=False)
                    last = pe.matmul(bank[:, 0:512],
                                     S.idf[0:NE + 1, NE:NE + 1]
                                     .to_broadcast([NE + 1, 128]),
                                     S.b2all[:, 512 * dh:512 * (dh + 1)],
                                     start=False, stop=True)
                    PEG.inc(last)
                    PEG.wait(vec)
                    ins = V(vec.tensor_copy(yfb[:, 512 * dh:512 * (dh + 1)],
                                            bank[:, 0:512]))
                    EVm()
                    evh5.append(S.EV.n)
                VS.wait(act)
                SQ.d.wait(act, S.out_st_tok)
                S.AQ.dma(act.dma_start(out=xres[:],
                                       in_=S.xsl[HALO + 128 * tt:
                                                 HALO + 128 * (tt + 1), :]))
                S.AQ.dma(act.dma_start(out=hres[:],
                                       in_=S.h_dram[128 * tt:128 * (tt + 1), :]))
                S.AQ.dma(act.dma_start(out=y1[:],
                                       in_=S.Y_dram[128 * tt:128 * (tt + 1), :]))
                S.AQ.dma(act.dma_start(out=y2[:],
                                       in_=S.Y_dram[TOK + 128 * tt:
                                                    TOK + 128 * (tt + 1), :]))
                f_tok = S.AQ.chk()
                S.AQ.wait(vec, f_tok)
                VW(); V(vec.tensor_scalar(y1[:], y1[:], S.ms1t[:, tt:tt + 1], None,
                                          ALU.mult))
                VW(); V(vec.scalar_tensor_tensor(y1[:], y2[:], S.ms2t[:, tt:tt + 1],
                                                 y1[:], ALU.mult, ALU.add))
                VW(); V(vec.scalar_tensor_tensor(y1[:], yfb[:],
                                                 S.mscf[:, tt:tt + 1], y1[:],
                                                 ALU.mult, ALU.add))
                VW(); V(vec.tensor_tensor(y1[:], y1[:], xres[:], ALU.add))
                VW(); ins = V(vec.tensor_tensor(y2[:], y1[:], hres[:], ALU.add))
                EVm()
                VS.wait(sy)
                SQ.dma(sy.dma_start(out=S.out_h[128 * tt:128 * (tt + 1), :], in_=y2[:]))
                S.out_st_tok = SQ.chk()


_NC_CACHE = {}


def _consts():
    c = {}
    k = np.arange(128)
    c["idf"] = np.eye(128, dtype=np.float32)
    c["idb"] = np.eye(128, dtype=ml_dtypes.bfloat16)
    c["cumW"] = (k[:, None] <= k[None, :]).astype(ml_dtypes.bfloat16)
    c["cumSW"] = (k[:, None] < k[None, :]).astype(ml_dtypes.bfloat16)
    c["onesB"] = np.ones((128, 128), dtype=ml_dtypes.bfloat16)
    c["onesR"] = np.ones((1, 128), dtype=np.float32)
    c["onesC"] = np.ones((128, 1), dtype=np.float32)
    c["epscol"] = np.full((128, 1), 1e-5, dtype=np.float32)
    c["iota8"] = np.broadcast_to(np.arange(NE, dtype=np.float32), (128, NE)).copy()
    c["iotam8"] = c["iota8"] - 8.0
    c["negbig"] = np.full((128, 1), -3.0e38, dtype=np.float32)
    c["imask"] = np.full((128, 1), 0x7FFFFF, dtype=np.int32)
    lt = (np.arange(OWNT)[None, :] * 128 + k[:, None]).astype(np.int64)
    c["paytok"] = (lt + HALO).astype(np.int32)
    c["pdst1"] = lt.astype(np.int32)
    c["pdst2"] = (lt + TOK).astype(np.int32)
    return c


def kernel(L0_attn_x, L0_attn_Wf, L0_attn_Wi, L0_attn_Wh, L0_ffn_Wr, L0_ffn_W1,
           L0_ffn_b1, L0_ffn_W2, L0_ffn_b2, L0_ffn_fw1, L0_ffn_fb1, L0_ffn_fw2,
           L0_ffn_fb2):
    X = np.ascontiguousarray(np.asarray(L0_attn_x, dtype=np.float32).reshape(16384, D))
    xt = np.ascontiguousarray(X.T)
    consts = _consts()

    if "nc" not in _NC_CACHE:
        _NC_CACHE["nc"] = build_nc()
    nc, taps = _NC_CACHE["nc"]

    weights = dict(
        Wf=np.asarray(L0_attn_Wf, np.float32), Wi=np.asarray(L0_attn_Wi, np.float32),
        Wh=np.asarray(L0_attn_Wh, np.float32), Wr=np.asarray(L0_ffn_Wr, np.float32),
        W1=np.asarray(L0_ffn_W1, np.float32), b1=np.asarray(L0_ffn_b1, np.float32),
        W2=np.asarray(L0_ffn_W2, np.float32), b2=np.asarray(L0_ffn_b2, np.float32),
        fw1=np.asarray(L0_ffn_fw1, np.float32),
        fb1=np.asarray(L0_ffn_fb1, np.float32).reshape(1, DFF),
        fw2=np.asarray(L0_ffn_fw2, np.float32),
        fb2=np.asarray(L0_ffn_fb2, np.float32).reshape(1, D),
    )
    weights["b1t"] = np.ascontiguousarray(
        weights["b1"].reshape(NE, JC, 128).transpose(2, 0, 1).reshape(128, NE * JC))
    weights["fb1t"] = np.ascontiguousarray(
        weights["fb1"].reshape(JC, 128).T)

    in_maps = []
    zpad = np.zeros((HALO, D), dtype=np.float32)
    for c in range(NCORES):
        first = c in (0, NCORES // 2)
        if first:
            xslc = np.concatenate([zpad, X[TOK * c:TOK * (c + 1)]], axis=0)
        else:
            xslc = X[TOK * c - HALO:TOK * (c + 1)]
        sown = np.zeros((128, OWNT), dtype=np.float32)
        for j in range(OWNT):
            sown[OWNT * c + j, j] = 1.0
        im = dict(consts)
        im.update(weights)
        im["xsl"] = np.ascontiguousarray(xslc)
        im["xt"] = xt
        im["sown"] = sown
        im["halo_s"] = np.full((128, 1), 0.0 if first else 1.0, dtype=np.float32)
        im["halo_h"] = np.full((128, 1), 0.5 if first else 0.0, dtype=np.float32)
        in_maps.append(im)

    res = run_bass_kernel_spmd(nc, in_maps, core_ids=list(range(NCORES)))
    kernel._last = res
    out = np.concatenate([res.results[c]["out"] for c in range(NCORES)], axis=0)
    return out.reshape(2, 8192, D)


# version 30
# speedup vs baseline: 1.0721x; 1.0721x over previous
import numpy as np
import ml_dtypes
from contextlib import ExitStack

import concourse.bass as bass
import concourse.mybir as mybir
from concourse.bass_core_rust import IndirectOffsetOnAxis
from concourse.bass_utils import run_bass_kernel_spmd

F32 = mybir.dt.float32
BF16 = mybir.dt.bfloat16
I32 = mybir.dt.int32
U32 = mybir.dt.uint32
AF = mybir.ActivationFunctionType
ALU = mybir.AluOpType
AXX = mybir.AxisListType.X

NCORES = 8
TOK = 2048
HALO = 256
TOKH = TOK + HALO
NT = TOKH // 128
OWNT = TOK // 128
D = 1024
DC = D // 128
DFF = 4096
JC = DFF // 128
NE = 8
CAP = 2560
LCAP = 640
RT = LCAP // 128
GT = 16384 // 128
GSZ = 512
NG = TOK // GSZ
BIG = 6.0e6
E23 = 8388608.0
BIG_I = 6000000

PHASES = 6
KDEBUG = False


class _NS:
    pass


class _ColView:

    def __init__(self, h, off, width):
        self.h = h; self.off = off; self.w = width

    def __getitem__(self, idx):
        if idx == slice(None):
            return self.h[:, self.off:self.off + self.w]
        ps, fs = idx
        a = fs.start if fs.start is not None else 0
        b = fs.stop if fs.stop is not None else self.w
        return self.h[:, self.off + a:self.off + b]


def build_nc():
    nc = bass.Bass("TRN2", target_bir_lowering=False)
    S = _NS()
    S.nc = nc

    def din(name, shape, dtype=F32):
        return nc.dram_tensor(name, list(shape), dtype, kind="ExternalInput").ap()

    S.xsl = din("xsl", (TOKH, D))
    S.xt = din("xt", (D, 16384))
    S.Wf_h = din("Wf", (D, D)); S.Wi_h = din("Wi", (D, D)); S.Wh_h = din("Wh", (D, D))
    S.Wr_h = din("Wr", (D, NE))
    S.W1_h = din("W1", (NE, D, DFF)); S.b1_h = din("b1", (NE, DFF))
    S.W2_h = din("W2", (NE, DFF, D)); S.b2_h = din("b2", (NE, D))
    S.fw1_h = din("fw1", (D, DFF)); S.fb1_h = din("fb1", (1, DFF))
    S.fw2_h = din("fw2", (DFF, D)); S.fb2_h = din("fb2", (1, D))
    S.b1t_h = din("b1t", (128, NE * JC)); S.fb1t_h = din("fb1t", (128, JC))
    consts_in = {}
    for nm, shape, dt in [
            ("idf", (128, 128), F32), ("idb", (128, 128), BF16),
            ("cumW", (128, 128), BF16), ("cumSW", (128, 128), BF16),
            ("onesB", (128, 128), BF16), ("onesR", (1, 128), F32),
            ("onesC", (128, 1), F32), ("epscol", (128, 1), F32),
            ("iota8", (128, NE), F32), ("iotam8", (128, NE), F32),
            ("negbig", (128, 1), F32), ("imask", (128, 1), I32),
            ("halo_s", (128, 1), F32), ("halo_h", (128, 1), F32),
            ("sown", (128, OWNT), F32),
            ("paytok", (128, OWNT), I32), ("pdst1", (128, OWNT), I32),
            ("pdst2", (128, OWNT), I32)]:
        consts_in[nm] = din(nm, shape, dt)

    S.out_h = nc.dram_tensor("out", [TOK, D], F32, kind="ExternalOutput").ap()
    S.taps = {}
    if KDEBUG:
        def tap(name, shape, dtype=F32):
            S.taps[name] = nc.dram_tensor(name, list(shape), dtype,
                                          kind="ExternalOutput").ap()
        tap("t_xn", (TOKH, D)); tap("t_utm", (128, GT * NE))
        tap("t_keep", (128, OWNT * NE)); tap("t_route", (128, OWNT * 12))
        tap("t_h", (TOK, D)); tap("t_idx", (128, RT * NE * 2), I32)
        tap("t_y", (2 * TOK, D))

    S.xn_dram = nc.dram_tensor("xn_dram", [TOKH, D], BF16, kind="Internal").ap()
    S.xnfm_dram = nc.dram_tensor("xnfm_dram", [NG * DC * 128, GSZ], BF16,
                                 kind="Internal").ap()
    S.h_dram = nc.dram_tensor("h_dram", [TOK, D], F32, kind="Internal").ap()
    S.idx_dram = nc.dram_tensor("idx_dram", [NE * LCAP, 1], I32, kind="Internal").ap()
    S.dst_dram = nc.dram_tensor("dst_dram", [NE * LCAP, 1], I32, kind="Internal").ap()
    S.Y_dram = nc.dram_tensor("Y_dram", [2 * TOK, D], F32, kind="Internal").ap()
    S.fw1bf = nc.dram_tensor("fw1bf", [D, DFF], BF16, kind="Internal").ap()
    S.fw2bf = nc.dram_tensor("fw2bf", [DFF, D], BF16, kind="Internal").ap()

    ctx = ExitStack()
    with ctx:
        def sb(name, shape, dtype=F32, stk=None):
            return (stk or ctx).enter_context(nc.sbuf_tensor(name, list(shape), dtype))
        S.sb = sb

        class Ctr:
            def __init__(self, name):
                self.s = ctx.enter_context(nc.semaphore(name)); self.n = 0
                self.wm = {}
            def inc(self, instr, k=1):
                instr.then_inc(self.s, k); self.n += k; return instr
            def wait(self, eng, n=None):
                v = self.n if n is None else n
                key = id(eng)
                if self.wm.get(key, -1) >= v:
                    return
                eng.wait_ge(self.s, v)
                self.wm[key] = v

        class Q:
            def __init__(self, eng, name):
                self.eng = eng
                self.s = ctx.enter_context(nc.semaphore(name)); self.n = 0
                self.d = Ctr(name + "d")
            def dma(self, instr):
                instr.then_inc(self.s, 16); self.n += 16; return instr
            def chk(self):
                self.eng.wait_ge(self.s, self.n)
                self.d.inc(self.eng.nop())
                return self.d.n
            def mark(self):
                return self.n
            def chk_for(self, n):
                self.eng.wait_ge(self.s, n)
                self.d.inc(self.eng.nop())
                return self.d.n
            def wait(self, eng, tok=None):
                self.d.wait(eng, tok)

        S.vec = nc.vector; S.act = nc.scalar; S.pe = nc.tensor
        S.g = nc.gpsimd; S.sy = nc.sync

        S.SQ = Q(S.sy, "SQ")
        S.AQ = Q(S.act, "AQ")
        S.GQ = Q(S.g, "GQ")
        S.PEG = Ctr("PEG")
        S.PQ = Ctr("PQ")
        S.VS = Ctr("VS")
        S.AS = Ctr("AS")
        S.EV = Ctr("EV")

        def V(instr):
            return S.VS.inc(instr)

        def VW():
            S.vec.wait_ge(S.VS.s, S.VS.n)

        def EVm():
            S.vec.wait_ge(S.VS.s, S.VS.n)
            S.EV.inc(S.vec.nop())

        S.V = V; S.VW = VW; S.EVm = EVm

        def barrier():
            S.SQ.chk(); S.GQ.chk(); S.AQ.chk()
            nc.all_engine_barrier()
        S.barrier = barrier

        for nm in consts_in:
            dt = consts_in[nm].dtype
            shp = list(consts_in[nm].shape)
            setattr(S, nm, sb("c_" + nm, shp, dt))
        S.rstd_all = sb("rstd_all", (128, NT))
        S.s1t = sb("s1t", (128, OWNT)); S.s2t = sb("s2t", (128, OWNT))
        S.mself = sb("mself", (128, OWNT))
        S.ms1t = sb("ms1t", (128, OWNT)); S.ms2t = sb("ms2t", (128, OWNT))
        S.mscf = sb("mscf", (128, OWNT))
        S.keep_own = sb("keep_own", (128, OWNT, NE))
        S.i1own = sb("i1own", (128, OWNT)); S.i2own = sb("i2own", (128, OWNT))
        S.idxtok = sb("idxtok", (128, RT * NE), I32)
        S.idxdsty = sb("idxdsty", (128, RT * NE), I32)
        S.b1call = sb("b1call", (128, NE * JC))
        S.fb1c = sb("fb1c", (128, JC))
        S.b2all = sb("b2all", (NE + 1, D))

        S.pb = [ctx.enter_context(nc.psum_tensor(f"pb{i}", [128, 512], F32))
                for i in range(7)]
        S.pbh = ctx.enter_context(nc.psum_tensor("pbh", [128, 1024], BF16))

        for nm in consts_in:
            S.SQ.dma(S.sy.dma_start(out=getattr(S, nm)[:], in_=consts_in[nm]))
        ctok = S.SQ.chk()
        for eng in (S.pe, S.vec, S.act, S.g):
            S.SQ.wait(eng, ctok)

        S.bnd_idx = S.g.alloc_register("bnd_idx"); S.g.reg_mov(S.bnd_idx, NE * LCAP - 1)
        S.bnd_xn = S.g.alloc_register("bnd_xn"); S.g.reg_mov(S.bnd_xn, TOKH - 1)
        S.bnd_y = S.g.alloc_register("bnd_y"); S.g.reg_mov(S.bnd_y, 2 * TOK - 1)

        with ExitStack() as s34:
            S.xg_tm = [[sb(f"xg_tm{ss}_{i}", (128, D), BF16, s34)
                        for i in range(RT)] for ss in range(2)]
            S.zb = sb("zb", (128, D), F32, s34)
            S.dst1i = sb("dst1i", (128, OWNT), I32, s34)
            S.dst2i = sb("dst2i", (128, OWNT), I32, s34)
            S.bigt = sb("bigt", (128, RT * NE), I32, s34)
            with ExitStack() as scopeA:
                S.WTall = {w: sb(f"W{w}Tall", (128, DC * D), BF16, scopeA)
                           for w in "fih"}
                S.WT = {w: [_ColView(S.WTall[w], r * D, D) for r in range(DC)]
                        for w in "fih"}
                S.wr_hi = [sb(f"wr_hi{r}", (128, NE), BF16, scopeA)
                           for r in range(DC)]
                S.wr_lo = [sb(f"wr_lo{r}", (128, NE), BF16, scopeA)
                           for r in range(DC)]
                S.xn_fmall = sb("xn_fmall", (128, DC * TOKH), BF16, scopeA)
                S.xn_fm = [_ColView(S.xn_fmall, r * TOKH, TOKH) for r in range(DC)]
                S.U_tm = sb("U_tm", (128, GT * NE), F32, scopeA)
                S.pbh_vs = [0]

                _phase0(S)
                barrier()
                with ExitStack() as scopeB:
                    p0b_tile = _phase0b_emitter(S, scopeB)
                    p1_tile, p1_fin = _phase1_emitter(S, scopeB)
                    _phase2a(S, scopeB, [(3 * DC, p0b_tile, 30), (NT, p1_tile, 22)])
                    p1_fin()
                _phase2b(S)
                _phase3(S)
                barrier()
            if PHASES >= 4:
                _phase4(S)
                barrier()
        if PHASES >= 5:
            _phase5(S)
        S.sy.wait_ge(S.SQ.s, S.SQ.n)
        S.g.wait_ge(S.GQ.s, S.GQ.n)

    return nc, S.taps


def _phase0(S):
    sb = S.sb; V = S.V; VW = S.VW; EVm = S.EVm
    pe, vec, act, g, sy = S.pe, S.vec, S.act, S.g, S.sy
    SQ, GQ, PEG, EV, VS = S.SQ, S.GQ, S.PEG, S.EV, S.VS
    act = S.act
    pbh, pb = S.pbh, S.pb

    with ExitStack() as s0:
        zb = S.zb
        VW(); ins = V(vec.memset(zb[:], 0.0)); EVm()
        EV.wait(act)
        for i in range(2 * OWNT):
            S.AQ.dma(act.dma_start(out=S.Y_dram[128 * i:128 * (i + 1), :], in_=zb[:]))
        S.yz_tok = S.AQ.chk()

        wr_f = [sb(f"wr_f{r}", (128, NE), stk=s0) for r in range(DC)]
        srow = sb("srow", (1, NE), stk=s0)
        sneg = sb("sneg", (1, NE), stk=s0)
        sbc = sb("sbc", (128, NE), stk=s0)
        wrhat = sb("wrhat", (128, NE), stk=s0)

        for r in range(DC):
            SQ.dma(sy.dma_start(out=wr_f[r][:], in_=S.Wr_h[128 * r:128 * (r + 1), :]))
        tok = SQ.chk()
        SQ.dma(sy.dma_start(out=S.b1call[:], in_=S.b1t_h))
        SQ.dma(sy.dma_start(out=S.fb1c[:], in_=S.fb1t_h))
        SQ.dma(sy.dma_start(out=S.b2all[0:NE, :], in_=S.b2_h[:, :]))
        SQ.dma(sy.dma_start(out=S.b2all[NE:NE + 1, :], in_=S.fb2_h))
        SQ.chk()
        SQ.wait(pe, tok)
        EV.wait(pe)
        for r in range(DC):
            last = pe.matmul(pb[5][0:1, 0:NE], S.onesC[:, 0:1], wr_f[r][:],
                             start=(r == 0), stop=(r == DC - 1))
        PEG.inc(last)
        PEG.wait(vec)
        V(vec.tensor_copy(srow[:], pb[5][0:1, 0:NE]))
        VW()
        ins = V(vec.tensor_scalar(sneg[:], srow[:], -1.0 / 1024.0, None, ALU.mult))
        EVm()
        EV.wait(pe)
        VS.wait(pe)
        last = pe.matmul(pb[5][0:128, 0:NE], S.onesR[0:1, :], sneg[:],
                         start=True, stop=True)
        PEG.inc(last)
        PEG.wait(vec)
        V(vec.tensor_copy(sbc[:], pb[5][0:128, 0:NE]))
        for r in range(DC):
            VW()
            V(vec.tensor_tensor(wrhat[:], wr_f[r][:], sbc[:], ALU.add))
            VW()
            V(vec.tensor_copy(S.wr_hi[r][:], wrhat[:]))
            VW()
            V(vec.tensor_tensor(S.wr_lo[r][:], wrhat[:], S.wr_hi[r][:], ALU.subtract))
        VW()
        ins = V(vec.nop())
        EVm()


def _phase0b_emitter(S, stk):
    sb = S.sb; V = S.V; EVm = S.EVm
    pe, vec, act, g = S.pe, S.vec, S.act, S.g
    GQ, PEG, EV, AS_, VS = S.GQ, S.PEG, S.EV, S.AS, S.VS
    pbh = S.pbh

    wrow = [sb(f"wrow{i}", (128, D), BF16, stk) for i in range(2)]
    wsrc = [("f", S.Wf_h), ("i", S.Wi_h), ("h", S.Wh_h)]
    wr_peg = [0, 0]

    def emit_chunk(k):
        w, whbm = wsrc[k // DC]
        ec = k % DC
        par = k % 2
        buf = wrow[par]
        PEG.wait(g, wr_peg[par])
        GQ.dma(g.dma_start(out=buf[:], in_=whbm[128 * ec:128 * (ec + 1), :]))
        tok = GQ.chk()
        GQ.wait(pe, tok)
        VS.wait(pe, S.pbh_vs[0])
        last = None
        for dr in range(DC):
            last = pe.transpose(pbh[:, 128 * dr:128 * (dr + 1)],
                                buf[:, 128 * dr:128 * (dr + 1)], S.idb[:])
        PEG.inc(last)
        wr_peg[par] = S.PEG.n
        PEG.wait(vec, wr_peg[par])
        V(vec.tensor_copy(
            S.WTall[w][:].rearrange("p (r d) -> p r d", r=DC)[:, :,
                                                             128 * ec:128 * (ec + 1)],
            pbh[:].rearrange("p (r q) -> p r q", r=DC)))
        S.pbh_vs[0] = S.VS.n

    return emit_chunk


def _phase1_emitter(S, stk):
    sb = S.sb; V = S.V; VW = S.VW; EVm = S.EVm
    pe, vec, act, g, sy = S.pe, S.vec, S.act, S.g, S.sy
    SQ, GQ, PEG, EV, VS, AS_ = S.AQ, S.GQ, S.PEG, S.EV, S.VS, S.AS
    pbh, pb = S.pbh, S.pb

    if True:
        xtile = [sb(f"xtile{i}", (128, D), stk=stk) for i in range(2)]
        sqs = sb("sqs", (128, D), stk=stk)
        mucol = sb("mucol", (128, 1), stk=stk)
        m2col = sb("m2col", (128, 1), stk=stk)
        varcol = sb("varcol", (128, 1), stk=stk)
        sdcol = sb("sdcol", (128, 1), stk=stk)
        xnbf = [sb(f"xnbf{i}", (128, D), BF16, stk) for i in range(2)]

        vs_xn = [0, 0]
        peg_xb = [0, 0]
        ev_var = [0]
        def emit_tile(i):
            par = i % 2
            buf = xtile[par]
            VS.wait(act, vs_xn[par])
            SQ.dma(act.dma_start(out=buf[:], in_=S.xsl[128 * i:128 * (i + 1), :]))
            ltok = SQ.chk()
            SQ.wait(vec, ltok)
            V(vec.reduce_sum(mucol[:], buf[:], AXX))
            SQ.wait(act, ltok)
            EV.wait(act, ev_var[0])
            AS_.inc(act.activation(sqs[:], buf[:], AF.Square, accum_out=pb[0][:, 0:1]))
            VW(); V(vec.tensor_scalar(mucol[:], mucol[:], 1.0 / 1024.0, None, ALU.mult))
            VW(); V(vec.tensor_tensor(m2col[:], mucol[:], mucol[:], ALU.mult))
            AS_.wait(vec)
            VW()
            ins = V(vec.scalar_tensor_tensor(varcol[:], pb[0][:, 0:1], 1.0 / 1024.0,
                                             m2col[:], ALU.mult, ALU.subtract))
            EVm()
            ev_var[0] = S.EV.n
            VS.wait(act)
            AS_.inc(act.activation(sdcol[:], varcol[:], AF.Sqrt, bias=S.epscol[:, 0:1]))
            AS_.wait(vec)
            V(vec.reciprocal(S.rstd_all[:, i:i + 1], sdcol[:]))
            VW()
            xb = xnbf[par]
            SQ.d.wait(vec)
            PEG.wait(vec, peg_xb[par])
            ins = V(vec.tensor_scalar(xb[:], buf[:], mucol[:], S.rstd_all[:, i:i + 1],
                                      ALU.subtract, ALU.mult))
            EVm()
            vs_xn[par] = S.VS.n
            VS.wait(act)
            SQ.dma(act.dma_start(out=S.xn_dram[128 * i:128 * (i + 1), :], in_=xb[:]))
            SQ.chk()
            VS.wait(pe, max(vs_xn[par], S.pbh_vs[0]))
            for r in range(DC):
                last = pe.transpose(pbh[:, 128 * r:128 * (r + 1)],
                                    xb[:, 128 * r:128 * (r + 1)], S.idb[:])
            PEG.inc(last)
            peg_xb[par] = PEG.n
            PEG.wait(vec, peg_xb[par])
            V(vec.tensor_copy(
                S.xn_fmall[:].rearrange("p (r t) -> p r t", r=DC)[:, :,
                                                                 128 * i:
                                                                 128 * (i + 1)],
                pbh[:].rearrange("p (r q) -> p r q", r=DC)))
            S.pbh_vs[0] = S.VS.n

        def finish():
            VS.wait(act)
            for gi in range(NG):
                for r in range(DC):
                    SQ.dma(act.dma_start(
                        out=S.xnfm_dram[DC * 128 * gi + 128 * r:
                                        DC * 128 * gi + 128 * (r + 1), :],
                        in_=S.xn_fm[r][:, HALO + GSZ * gi:HALO + GSZ * (gi + 1)]))
            _phase1_debug(S, sb)

        return emit_tile, finish


def _phase1_debug(S, sb):
    V = S.V; VW = S.VW; EVm = S.EVm
    act, vec, sy = S.act, S.vec, S.sy
    SQ, VS = S.AQ, S.VS
    if True:
        if KDEBUG:
            SQ.chk()
            with ExitStack() as sdbg:
                xbb = sb("xbb_dbg", (128, D), BF16, sdbg)
                xff = sb("xff_dbg", (128, D), F32, sdbg)
                for i in range(NT):
                    SQ.d.wait(sy)
                    SQ.dma(act.dma_start(out=xbb[:],
                                        in_=S.xn_dram[128 * i:128 * (i + 1), :]))
                    t2 = SQ.chk()
                    SQ.wait(vec, t2)
                    VW(); ins = V(vec.tensor_copy(xff[:], xbb[:]))
                    EVm()
                    VS.wait(act)
                    SQ.dma(act.dma_start(out=S.taps["t_xn"][128 * i:128 * (i + 1), :],
                                        in_=xff[:]))
                    SQ.chk()


def _phase2a(S, stk, tile_cb=None):
    nc = S.nc; sb = S.sb; V = S.V; VW = S.VW; EVm = S.EVm
    pe, vec, act, g, sy = S.pe, S.vec, S.act, S.g, S.sy
    SQ, GQ, PEG, EV, VS, AS_, PQ = S.SQ, S.GQ, S.PEG, S.EV, S.VS, S.AS, S.PQ
    pbh, pb = S.pbh, S.pb
    U_tm = S.U_tm

    with ExitStack() as s2:
        p2 = lambda name, shape, dtype=F32: sb(name, shape, dtype, stk)
        xtf = [p2(f"xtf{i}", (128, 512)) for i in range(10)]
        xth = [p2(f"xth{i}", (128, 512), BF16) for i in range(10)]
        xtl = [p2(f"xtl{i}", (128, 512), BF16) for i in range(10)]
        ufm = [p2(f"ufm{i}", (8, 512)) for i in range(2)]

        NSL = 10
        LSL = [stk.enter_context(nc.semaphore(f"LSL{i}")) for i in range(NSL)]
        CSL = [stk.enter_context(nc.semaphore(f"CSL{i}")) for i in range(NSL)]
        ln_ctr = [0] * NSL
        cs_ctr = [0] * NSL

        sub_done = {}; mm_done = {}; evac_hist = {}; tr_done = {}
        emitted = [0] * len(tile_cb or [])

        def _pace(t):
            for ei, (total, fn, hz) in enumerate(tile_cb or []):
                while emitted[ei] < min(total, ((t + 1) * total) // hz):
                    fn(emitted[ei]); emitted[ei] += 1

        for t in range(32):
            bank8 = (pb[5] if t % 2 == 0 else pb[6])[0:8, 0:512]
            for r in range(DC):
                q = t * DC + r
                bi = q % NSL
                if q >= NSL:
                    sy.wait_ge(CSL[bi], 16 * cs_ctr[bi])
                    VS.wait(sy, sub_done[q - NSL])
                ins = sy.dma_start(out=xtf[bi][:],
                                   in_=S.xt[128 * r:128 * (r + 1), 512 * t:512 * (t + 1)])
                ins.then_inc(LSL[bi], 16); ln_ctr[bi] += 1
                if q >= NSL:
                    PEG.wait(g, mm_done[q - NSL])
                    VS.wait(g, sub_done[q - NSL])
                g.wait_ge(LSL[bi], 16 * ln_ctr[bi])
                ins = g.dma_start(out=xth[bi][:], in_=xtf[bi][:])
                ins.then_inc(CSL[bi], 16); cs_ctr[bi] += 1
                if q >= NSL:
                    PEG.wait(vec, mm_done[q - NSL])
                vec.wait_ge(CSL[bi], 16 * cs_ctr[bi])
                V(vec.tensor_tensor(xtl[bi][:], xtf[bi][:], xth[bi][:], ALU.subtract))
                sub_done[q] = S.VS.n
                pe.wait_ge(CSL[bi], 16 * cs_ctr[bi])
                VS.wait(pe, sub_done[q])
                if r == 0 and t >= 2:
                    EV.wait(pe, evac_hist[t - 2])
                pe.matmul(bank8, S.wr_hi[r][:], xth[bi][:], start=(r == 0), stop=False)
                m2 = pe.matmul(bank8, S.wr_lo[r][:], xth[bi][:], start=False, stop=False)
                PEG.inc(m2)
                last = pe.matmul(bank8, S.wr_hi[r][:], xtl[bi][:], start=False,
                                 stop=(r == DC - 1))
                PEG.inc(last)
                mm_done[q] = PEG.n
            ub = ufm[t % 2]
            PEG.wait(vec)
            if t >= 2:
                PEG.wait(vec, tr_done[t - 2])
            ins = V(vec.tensor_copy(ub[:], bank8))
            EVm()
            evac_hist[t] = EV.n
            pack = (t // 4) % 2
            EV.wait(pe, evac_hist[t])
            for q4 in range(4):
                last = pe.transpose(pb[4][:, 256 * pack + 32 * (t % 4) + 8 * q4:
                                          256 * pack + 32 * (t % 4) + 8 * q4 + 8],
                                    ub[:, 128 * q4:128 * (q4 + 1)], S.idf[0:8, 0:8])
            PEG.inc(last)
            tr_done[t] = PEG.n
            if t % 4 == 3:
                gt0 = (t - 3) * 4
                PEG.wait(vec)
                ins = V(vec.tensor_copy(U_tm[:, NE * gt0:NE * gt0 + 128],
                                        pb[4][:, 256 * pack:256 * pack + 128]))
                EVm()
            _pace(t)

        if KDEBUG:
            VS.wait(sy)
            AS_.wait(sy)
            SQ.dma(sy.dma_start(out=S.taps["t_utm"], in_=U_tm[:]))
            SQ.chk()


def _phase2b(S):
    nc = S.nc; sb = S.sb; V = S.V; VW = S.VW; EVm = S.EVm
    pe, vec, act, g, sy = S.pe, S.vec, S.act, S.g, S.sy
    SQ, GQ, PEG, EV, VS, AS_, PQ = S.SQ, S.GQ, S.PEG, S.EV, S.VS, S.AS, S.PQ
    pbh, pb = S.pbh, S.pb
    U_tm3 = S.U_tm[:].rearrange("p (a b) -> p a b", b=NE)

    with ExitStack() as s2:
        p2 = lambda name, shape, dtype=F32: sb(name, shape, dtype, s2)

        for e in (sy, g, act, vec):
            PEG.wait(e); VS.wait(e); AS_.wait(e); S.AQ.wait(e)
        EV.wait(pe); AS_.wait(pe)

        m1 = p2("m1", (128, GT)); i1f = p2("i1f", (128, GT))
        m2 = p2("m2", (128, GT)); i2f = p2("i2f", (128, GT))
        equ = p2("equ", (128, GT, NE))
        scr3 = p2("scr3", (128, GT, NE))
        U2s = p2("U2s", (128, GT, NE))

        b3 = lambda colap: colap.unsqueeze(1).to_broadcast([128, GT, NE])
        bm = lambda m: m.unsqueeze(2).to_broadcast([128, GT, NE])
        iota3 = S.iota8[:].unsqueeze(1).to_broadcast([128, GT, NE])

        iotam3 = S.iotam8[:].unsqueeze(1).to_broadcast([128, GT, NE])
        BIGM = 1.0e38
        VW(); V(vec.tensor_reduce(m1[:], U_tm3, AXX, ALU.max))
        VW(); V(vec.tensor_tensor(equ[:], U_tm3, bm(m1[:]), ALU.is_ge))
        VW(); V(vec.tensor_tensor(scr3[:], equ[:], iotam3, ALU.mult))
        VW(); V(vec.tensor_scalar(scr3[:], scr3[:], 8.0, None, ALU.add))
        VW(); V(vec.tensor_reduce(i1f[:], scr3[:], AXX, ALU.min))
        VW(); V(vec.tensor_tensor(equ[:], iota3, bm(i1f[:]), ALU.is_equal))
        VW(); V(vec.tensor_scalar(U2s[:], U_tm3, BIGM, None, ALU.add))
        VW(); V(vec.tensor_tensor(U2s[:], equ[:], U2s[:], ALU.mult))
        VW(); V(vec.tensor_tensor(U2s[:], U_tm3, U2s[:], ALU.subtract))
        VW(); V(vec.tensor_reduce(m2[:], U2s[:], AXX, ALU.max))
        VW(); V(vec.tensor_tensor(equ[:], U2s[:], bm(m2[:]), ALU.is_ge))
        VW(); V(vec.tensor_tensor(scr3[:], equ[:], iotam3, ALU.mult))
        VW(); V(vec.tensor_scalar(scr3[:], scr3[:], 8.0, None, ALU.add))
        VW(); V(vec.tensor_reduce(i2f[:], scr3[:], AXX, ALU.min))

        nfb = p2("nfb", (128, 128))
        z128 = p2("z128", (128, 128))
        aT = p2("aT", (128, 128), BF16)
        a_sm = p2("a_sm", (128, 128))
        elig = p2("elig", (128, 128))
        rowc = p2("rowc", (128, 128))
        rtot = p2("rtot", (128, 1), BF16)
        rank = p2("rank", (128, 128))
        c1t = p2("c1t", (128, 128))
        keepC = p2("keepC", (128, 128))
        overC = p2("overC", (128, 128))
        scr128 = p2("scr128", (128, 128))
        sel16 = p2("sel16", (128, 128))

        def extract_own(src_ap, dst_ap, seg_src):
            if not seg_src:
                VS.wait(pe)
                EV.wait(pe)
                last = pe.transpose(pb[4][:, 0:128], src_ap, S.idf[:])
                PEG.inc(last)
                PEG.wait(vec)
                VW(); ins = V(vec.tensor_copy(a_sm[:], pb[4][:, 0:128]))
                EVm()
                seg = a_sm[:]
            else:
                seg = src_ap
            VS.wait(pe)
            EV.wait(pe)
            last = pe.matmul(pb[4][0:OWNT, 0:128], S.sown[:], seg, start=True, stop=True)
            PEG.inc(last)
            PEG.wait(vec)
            VW(); ins = V(vec.tensor_copy(sel16[0:OWNT, :], pb[4][0:OWNT, 0:128]))
            EVm()
            VS.wait(pe)
            EV.wait(pe)
            last = pe.transpose(pb[4][:, 128:256], sel16[:], S.idf[:])
            PEG.inc(last)
            PEG.wait(vec)
            VW(); ins = V(vec.tensor_copy(dst_ap, pb[4][:, 128:128 + OWNT]))
            EVm()

        VW(); V(vec.memset(nfb[:], 1.0))
        VW(); V(vec.memset(z128[:], 0.0))
        for e in range(NE):
            fe = float(e)
            VW(); V(vec.tensor_scalar(scr128[:], i1f[:], fe, None, ALU.is_equal))
            VW(); V(vec.tensor_scalar(overC[:], i2f[:], fe, None, ALU.is_equal))
            VW(); ins = V(vec.tensor_tensor(aT[:], scr128[:], overC[:], ALU.add))
            EVm()
            EV.wait(pe)
            VS.wait(pe)
            last = pe.transpose(pbh[:, 0:128], aT[:], S.idb[:])
            PEG.inc(last)
            PEG.wait(vec)
            VW(); ins = V(vec.tensor_copy(a_sm[:], pbh[:, 0:128]))
            EVm()
            VW(); V(vec.tensor_tensor(elig[:], a_sm[:], nfb[:], ALU.mult))
            VW(); V(vec.tensor_tensor_scan(rowc[:], elig[:], z128[:], 0.0,
                                           ALU.add, ALU.add))
            VW(); ins = V(vec.tensor_copy(rtot[:], rowc[:, 127:128]))
            EVm()
            EV.wait(pe)
            VS.wait(pe)
            last = pe.matmul(pb[6][:, 0:1], S.cumSW[:], rtot[:], start=True, stop=True)
            PEG.inc(last)
            PEG.wait(vec)
            VW(); ins = V(vec.tensor_scalar(rank[:], rowc[:], pb[6][:, 0:1], None, ALU.add))
            EVm()
            VW(); V(vec.tensor_scalar(c1t[:], rank[:], float(CAP), None, ALU.is_le))
            VW(); V(vec.tensor_tensor(keepC[:], elig[:], c1t[:], ALU.mult))
            VW(); V(vec.tensor_tensor(overC[:], elig[:], keepC[:], ALU.subtract))
            VW(); V(vec.tensor_tensor(nfb[:], nfb[:], overC[:], ALU.subtract))
            extract_own(keepC[:], S.keep_own[:, :, e:e + 1].squeeze(2), seg_src=True)

        extract_own(m1[:], S.s1t[:, 0:OWNT], seg_src=False)
        extract_own(m2[:], S.s2t[:, 0:OWNT], seg_src=False)
        extract_own(i1f[:], S.i1own[:, 0:OWNT], seg_src=False)
        extract_own(i2f[:], S.i2own[:, 0:OWNT], seg_src=False)

        d12 = p2("d12", (128, OWNT))
        z12 = p2("z12", (128, OWNT))
        k1c = p2("k1c", (128, OWNT)); k2c = p2("k2c", (128, OWNT))
        o3 = p2("o3", (128, OWNT, NE))
        lcum = p2("lcum", (128, OWNT, NE))
        slot1 = p2("slot1", (128, OWNT)); slot2 = p2("slot2", (128, OWNT))
        msel = p2("msel", (128, OWNT))
        scrO = p2("scrO", (128, OWNT))
        scrO2 = p2("scrO2", (128, OWNT))
        keep_bf = p2("keep_bf", (128, OWNT, NE), BF16)
        dst1i = S.dst1i
        dst2i = S.dst2i
        bigt = S.bigt

        VW(); V(vec.tensor_tensor(d12[:], S.s1t[:], S.s2t[:], ALU.subtract))
        VW(); V(vec.tensor_tensor(z12[:], d12[:], S.rstd_all[:, 2:2 + OWNT], ALU.mult))
        VS.wait(act)
        AS_.inc(act.activation(S.s1t[:], z12[:], AF.Sigmoid))
        AS_.wait(vec)
        VW(); V(vec.tensor_scalar(S.s2t[:], S.s1t[:], -1.0, 1.0, ALU.mult, ALU.add))

        bO = lambda m: m.unsqueeze(2).to_broadcast([128, OWNT, NE])
        iotaO = S.iota8[:].unsqueeze(1).to_broadcast([128, OWNT, NE])
        sO = scr3[:, 0:OWNT, :]
        VW(); V(vec.tensor_tensor(o3[:], iotaO, bO(S.i1own[:]), ALU.is_equal))
        VW(); V(vec.tensor_tensor(sO, o3[:], S.keep_own[:], ALU.mult))
        VW(); V(vec.tensor_reduce(k1c[:], sO, AXX, ALU.max))
        VW(); V(vec.tensor_tensor(o3[:], iotaO, bO(S.i2own[:]), ALU.is_equal))
        VW(); V(vec.tensor_tensor(sO, o3[:], S.keep_own[:], ALU.mult))
        VW(); V(vec.tensor_reduce(k2c[:], sO, AXX, ALU.max))
        VW(); ins = V(vec.tensor_copy(keep_bf[:], S.keep_own[:]))
        EVm()
        EV.wait(pe)
        VS.wait(pe)
        for i in range(OWNT):
            dst = pb[5][:, 8 * i:8 * (i + 1)]
            for jj in range(i + 1):
                last = pe.matmul(dst, lhs if False else (S.cumW[:] if jj == i else S.onesB[:]),
                                 keep_bf[:, jj, :], start=(jj == 0), stop=(jj == i))
        PEG.inc(last)
        PEG.wait(vec)
        VW()
        ins = V(vec.tensor_copy(lcum[:],
                                pb[5][:, 0:128].rearrange("p (a b) -> p a b", a=OWNT)))
        EVm()
        VW(); V(vec.tensor_scalar(sO, lcum[:], 1.0, None, ALU.subtract))
        VW(); V(vec.tensor_tensor(o3[:], iotaO, bO(S.i1own[:]), ALU.is_equal))
        VW(); V(vec.tensor_tensor(U2s[:, 0:OWNT, :], o3[:], sO, ALU.mult))
        VW(); V(vec.tensor_reduce(slot1[:], U2s[:, 0:OWNT, :], AXX, ALU.max))
        VW(); V(vec.tensor_tensor(o3[:], iotaO, bO(S.i2own[:]), ALU.is_equal))
        VW(); V(vec.tensor_tensor(U2s[:, 0:OWNT, :], o3[:], sO, ALU.mult))
        VW(); V(vec.tensor_reduce(slot2[:], U2s[:, 0:OWNT, :], AXX, ALU.max))
        VW(); V(vec.tensor_scalar(scrO[:], slot1[:], float(LCAP), None, ALU.is_lt))
        VW(); V(vec.tensor_tensor(k1c[:], k1c[:], scrO[:], ALU.mult))
        VW(); V(vec.tensor_scalar(scrO[:], slot2[:], float(LCAP), None, ALU.is_lt))
        VW(); V(vec.tensor_tensor(k2c[:], k2c[:], scrO[:], ALU.mult))
        VW(); V(vec.tensor_tensor(S.mself[:], k1c[:], k2c[:], ALU.mult))
        VW(); V(vec.tensor_copy(msel[:], S.mself[:]))
        VW(); V(vec.tensor_tensor(S.ms1t[:], S.mself[:], S.s1t[:], ALU.mult))
        VW(); V(vec.tensor_tensor(S.ms2t[:], S.mself[:], S.s2t[:], ALU.mult))
        VW(); V(vec.tensor_scalar(S.mscf[:], S.mself[:], -1.0, 1.0,
                                  ALU.mult, ALU.add))

        def build_dest(slot, ksel, eown, dst_i32_ap):
            VW(); V(vec.tensor_scalar(scrO[:], slot[:], 128.0, None, ALU.is_ge))
            for thr in (256.0, 384.0, 512.0):
                VW(); V(vec.tensor_scalar(scrO2[:], slot[:], thr, None, ALU.is_ge))
                VW(); V(vec.tensor_tensor(scrO[:], scrO[:], scrO2[:], ALU.add))
            VW(); V(vec.scalar_tensor_tensor(scrO2[:], scrO[:], -128.0, slot[:],
                                             ALU.mult, ALU.add))
            VW(); V(vec.tensor_scalar(scrO2[:], scrO2[:], float(RT * NE), None, ALU.mult))
            VW(); V(vec.tensor_tensor(scrO2[:], scrO2[:], scrO[:], ALU.add))
            VW(); V(vec.scalar_tensor_tensor(scrO2[:], eown[:], float(RT), scrO2[:],
                                             ALU.mult, ALU.add))
            VW(); V(vec.tensor_scalar(scrO2[:], scrO2[:], BIG, None, ALU.add))
            VW(); V(vec.scalar_tensor_tensor(scrO2[:], ksel[:], -BIG, scrO2[:],
                                             ALU.mult, ALU.add))
            VW(); V(vec.tensor_scalar(scrO2[:], scrO2[:], E23, None, ALU.add))
            VW(); ins = V(vec.tensor_scalar(dst_i32_ap, scrO2[:].bitcast(I32),
                                            S.imask[:, 0:1], None, ALU.bitwise_and))
            EVm()

        build_dest(slot1, k1c, S.i1own, dst1i[:])
        build_dest(slot2, k2c, S.i2own, dst2i[:])

        VW(); ins = V(vec.memset(bigt[:], BIG_I))
        EVm()
        EV.wait(g)
        VS.wait(g)
        GQ.dma(g.dma_start(out=S.idx_dram.rearrange("(p a) b -> p (a b)", p=128),
                           in_=bigt[:]))
        GQ.dma(g.dma_start(out=S.dst_dram.rearrange("(p a) b -> p (a b)", p=128),
                           in_=bigt[:]))
        GQ.chk()
        for i in range(OWNT):
            for dcol, pd in ((dst1i, S.pdst1), (dst2i, S.pdst2)):
                GQ.dma(g.indirect_dma_start(
                    out=S.idx_dram,
                    out_offset=IndirectOffsetOnAxis(dcol[:, i:i + 1], 0),
                    in_=S.paytok[:, i:i + 1], in_offset=None,
                    bounds_check=S.bnd_idx, oob_is_err=False))
                GQ.dma(g.indirect_dma_start(
                    out=S.dst_dram,
                    out_offset=IndirectOffsetOnAxis(dcol[:, i:i + 1], 0),
                    in_=pd[:, i:i + 1], in_offset=None,
                    bounds_check=S.bnd_idx, oob_is_err=False))
        GQ.chk()
        GQ.dma(g.dma_start(out=S.idxtok[:],
                           in_=S.idx_dram.rearrange("(p a) b -> p (a b)", p=128)))
        GQ.dma(g.dma_start(out=S.idxdsty[:],
                           in_=S.dst_dram.rearrange("(p a) b -> p (a b)", p=128)))
        GQ.chk()
        if KDEBUG:
            GQ.dma(g.dma_start(out=S.taps["t_idx"][:, 0:RT * NE], in_=S.idxtok[:]))
            GQ.dma(g.dma_start(out=S.taps["t_idx"][:, RT * NE:], in_=S.idxdsty[:]))
            GQ.chk()
            tb = p2("tb", (128, OWNT, 12))
            VW(); V(vec.memset(tb[:], 0.0))
            for ci, src in enumerate([S.s1t, S.s2t, S.mself, S.i1own, S.i2own,
                                      slot1, slot2, k1c, k2c]):
                VW(); V(vec.tensor_copy(tb[:, :, ci:ci + 1].squeeze(2), src[:, 0:OWNT]))
            VW(); ins = V(vec.tensor_copy(tb[:, :, 9:10].squeeze(2),
                                          S.rstd_all[:, 2:2 + OWNT]))
            EVm()
            VS.wait(sy)
            SQ.dma(sy.dma_start(out=S.taps["t_route"],
                                in_=tb[:].rearrange("p a b -> p (a b)")))
            kk = p2("kk", (128, OWNT * NE))
            VW(); ins = V(vec.tensor_copy(kk[:].rearrange("p (a b) -> p a b", a=OWNT),
                                          S.keep_own[:]))
            EVm()
            VS.wait(sy)
            SQ.dma(sy.dma_start(out=S.taps["t_keep"], in_=kk[:]))
            SQ.chk()


def _phase3(S):
    sb = S.sb; V = S.V; VW = S.VW; EVm = S.EVm
    pe, vec, act, g, sy = S.pe, S.vec, S.act, S.g, S.sy
    SQ, GQ, PEG, EV, VS, AS_ = S.SQ, S.GQ, S.PEG, S.EV, S.VS, S.AS
    pbh, pb = S.pbh, S.pb

    with ExitStack() as s3:
        fbuf = [sb(f"fbuf{i}", (128, TOKH), stk=s3) for i in range(2)]
        vbuf = [sb(f"vbuf{i}", (128, TOKH), stk=s3) for i in range(2)]
        sgbuf = [sb(f"sgbuf{i}", (128, TOKH), stk=s3) for i in range(2)]
        hbuf = [sb(f"hbuf{i}", (128, TOKH), stk=s3) for i in range(2)]
        h4 = [sb(f"h4_{i}", (128, 512), stk=s3) for i in range(2)]

        SUBS = [(0, 512), (512, 512), (1024, 512), (1536, 512), (2048, 256)]
        bank_free = []
        scan_done = [0, 0]
        tr_done = [0, 0]
        h4_st = [0, 0]
        pb4_as = [0]
        gctr = 0
        EV.wait(pe)
        AS_.wait(pe)
        VS.wait(pe)

        for r in range(DC):
            GQ.dma(g.dma_start(out=S.fw1bf[128 * r:128 * (r + 1), :],
                               in_=S.fw1_h[128 * r:128 * (r + 1), :]))
        for j in range(0, JC, 4):
            GQ.dma(g.dma_start(out=S.fw2bf[128 * j:128 * (j + 4), :],
                               in_=S.fw2_h[128 * j:128 * (j + 4), :]))
        S.SQ.d.wait(g)
        S.g_toks = []
        for e2 in range(2):
            for rt in range(RT):
                GQ.dma(g.indirect_dma_start(
                    out=S.xg_tm[e2][rt][:], out_offset=None,
                    in_=S.xn_dram,
                    in_offset=IndirectOffsetOnAxis(
                        S.idxtok[:, RT * e2 + rt:RT * e2 + rt + 1], 0),
                    bounds_check=S.bnd_xn, oob_is_err=False))
            S.g_toks.append(GQ.chk())
        for e in (act, vec):
            SQ.wait(e); PEG.wait(e)
        AS_.wait(vec); VS.wait(act); EV.wait(act); EV.wait(vec)

        def _emit_h_transposes(cc):
            parc = cc % 2
            hbc = hbuf[parc]
            for grp in range(4):
                VS.wait(pe, scan_done[parc])
                S.AQ.d.wait(pe, h4_st[parc])
                AS_.wait(pe, pb4_as[0])
                for q in range(4):
                    i = 4 * grp + q
                    last = pe.transpose(pb[4][:, 128 * q:128 * (q + 1)],
                                        hbc[:, HALO + 128 * i:HALO + 128 * (i + 1)],
                                        S.idf[:])
                PEG.inc(last)
                PEG.wait(act)
                S.AQ.d.wait(act, h4_st[parc])
                AS_.inc(act.activation(h4[parc][:], pb[4][:, 0:512], AF.Copy))
                pb4_as[0] = S.AS.n
                AS_.wait(act)
                for q in range(4):
                    i = 4 * grp + q
                    S.AQ.dma(act.dma_start(
                        out=S.h_dram[128 * i:128 * (i + 1),
                                     128 * cc:128 * (cc + 1)],
                        in_=h4[parc][:, 128 * q:128 * (q + 1)]))
                h4_st[parc] = S.AQ.chk()
            tr_done[parc] = PEG.n

        for c in range(DC):
            par = c % 2
            fb_, vb_, sg_, hb_ = fbuf[par], vbuf[par], sgbuf[par], hbuf[par]
            VS.wait(act, scan_done[par])
            for (t0, tl) in SUBS:
                for qi, wname in enumerate("fih"):
                    bank = pb[gctr % 4]
                    if gctr >= 4:
                        kind, val = bank_free[gctr - 4]
                        (AS_ if kind == "A" else VS).wait(pe, val)
                    for r in range(DC):
                        last = pe.matmul(bank[:, 0:tl],
                                         S.WT[wname][r][:, 128 * c:128 * (c + 1)],
                                         S.xn_fm[r][:, t0:t0 + tl],
                                         start=(r == 0), stop=(r == DC - 1))
                    PEG.inc(last)
                    PEG.wait(act)
                    if wname == "h":
                        AS_.inc(act.activation(sg_[:, t0:t0 + tl], bank[:, 0:tl],
                                               AF.Sigmoid))
                        AS_.wait(vec)
                        PEG.wait(vec, tr_done[par])
                        VW()
                        ins = V(vec.scalar_tensor_tensor(hb_[:, t0:t0 + tl],
                                                         bank[:, 0:tl], 0.5,
                                                         sg_[:, t0:t0 + tl],
                                                         ALU.add, ALU.max))
                        bank_free.append(("V", VS.n))
                    else:
                        dest = fb_ if wname == "f" else vb_
                        AS_.inc(act.activation(dest[:, t0:t0 + tl], bank[:, 0:tl],
                                               AF.Sigmoid))
                        bank_free.append(("A", AS_.n))
                    gctr += 1
            AS_.wait(vec)
            VW(); V(vec.tensor_tensor(sg_[:], vb_[:], hb_[:], ALU.mult))
            VW(); V(vec.tensor_scalar(fb_[:, 0:HALO], fb_[:, 0:HALO],
                                      S.halo_s[:, 0:1], None, ALU.mult))
            VW(); V(vec.tensor_scalar(sg_[:, 0:HALO], sg_[:, 0:HALO],
                                      S.halo_s[:, 0:1], None, ALU.mult))
            VW(); V(vec.tensor_scalar(sg_[:, HALO - 1:HALO], sg_[:, HALO - 1:HALO],
                                      S.halo_h[:, 0:1], None, ALU.add))
            VW(); ins = V(vec.tensor_tensor_scan(hb_[:], fb_[:], sg_[:], 0.0,
                                                 ALU.mult, ALU.add))
            scan_done[par] = VS.n
            if c >= 1:
                _emit_h_transposes(c - 1)
            if c == DC - 1:
                _emit_h_transposes(c)

        if KDEBUG:
            with ExitStack() as sd:
                hh = sb("hh_dbg", (128, D), stk=sd)
                S.AQ.d.wait(sy)
                for i in range(OWNT):
                    SQ.d.wait(sy)
                    SQ.dma(sy.dma_start(out=hh[:], in_=S.h_dram[128 * i:128 * (i + 1), :]))
                    SQ.chk()
                    SQ.dma(sy.dma_start(out=S.taps["t_h"][128 * i:128 * (i + 1), :],
                                        in_=hh[:]))
                    SQ.chk()


def _phase4(S):
    nc = S.nc; sb = S.sb; V = S.V; VW = S.VW; EVm = S.EVm
    pe, vec, act, g, sy = S.pe, S.vec, S.act, S.g, S.sy
    SQ, GQ, PEG, EV, VS, AS_, PQ = S.SQ, S.GQ, S.PEG, S.EV, S.VS, S.AS, S.PQ
    pbh, pb = S.pbh, S.pb

    with ExitStack() as s4:
        p4 = lambda name, shape, dtype=F32: sb(name, shape, dtype, s4)
        W2ball = p4("W2ball", (128, JC * D), BF16)
        W2b = [_ColView(W2ball, j * D, D) for j in range(JC)]
        W1gall = [p4(f"W1gall{pp}", (128, DC * 512), BF16) for pp in range(4)]
        W1g = [[_ColView(W1gall[pp], r * 512, 512) for r in range(DC)]
               for pp in range(4)]
        ubuf = [p4(f"ubuf{j}", (128, LCAP), BF16) for j in range(JC)]
        xg_fm = [p4(f"xg_fm{r}", (128, LCAP), BF16) for r in range(DC)]
        b2bc = p4("b2bc", (128, D))
        yesb = p4("yesb", (128, D))

        S.AQ.d.wait(g, S.yz_tok)

        HALVES = [(0, 384), (384, 256)]
        trs_peg_prev = 0
        s2_peg_prev = 0
        b2bc_ev = 0
        evh = []
        scat_tok = 0
        trs_peg_set = [0, 0]

        def _gathers(e2, defer=False):
            ss = e2 % 2
            PEG.wait(g, trs_peg_set[ss])
            for rt in range(RT):
                GQ.dma(g.indirect_dma_start(
                    out=S.xg_tm[ss][rt][:], out_offset=None,
                    in_=S.xn_dram,
                    in_offset=IndirectOffsetOnAxis(
                        S.idxtok[:, RT * e2 + rt:RT * e2 + rt + 1], 0),
                    bounds_check=S.bnd_xn, oob_is_err=False))
            return GQ.mark() if defer else GQ.chk()

        g_marks = [0, 0]
        g_toks = list(S.g_toks)

        W1SL = [s4.enter_context(nc.semaphore(f"W1SL{i}")) for i in range(4)]
        s1_peg_step = {}
        w1_done = {}
        w1_cnt = [0, 0, 0, 0]

        def _w1_issue(n2):
            e2, jg2 = divmod(n2, 8)
            if e2 >= NE:
                return
            sl = n2 % 4
            PEG.wait(g, s1_peg_step.get(n2 - 4, 0))
            ins = g.dma_start(
                out=W1gall[sl][:].rearrange("p (r c) -> p r c", r=DC),
                in_=S.W1_h[e2, :, 512 * jg2:512 * (jg2 + 1)]
                .rearrange("(r p) c -> p r c", p=128))
            ins.then_inc(W1SL[sl], 16)
            w1_cnt[sl] += 1
            w1_done[n2] = 16 * w1_cnt[sl]

        for n2 in range(3):
            _w1_issue(n2)

        for e in range(NE):
            g_tok = g_toks[e % 2]
            PEG.wait(g, s2_peg_prev)
            EV.wait(pe, b2bc_ev)
            sel = S.idf[0:NE + 1, e:e + 1].to_broadcast([NE + 1, 128])
            PEG.inc(pe.matmul(pb[6][:, 0:512], sel, S.b2all[:, 0:512],
                              start=True, stop=True))
            PEG.inc(pe.matmul(pb[5][:, 0:512], sel, S.b2all[:, 512:1024],
                              start=True, stop=True))
            PEG.wait(vec)
            V(vec.tensor_copy(b2bc[:, 0:512], pb[6][:, 0:512]))
            ins = V(vec.tensor_copy(b2bc[:, 512:1024], pb[5][:, 0:512]))
            EVm()
            b2bc_ev = S.EV.n

            GQ.wait(pe, g_tok)
            for rt in range(RT):
                EV.wait(pe)
                for r in range(DC):
                    last = pe.transpose(pbh[:, 128 * r:128 * (r + 1)],
                                        S.xg_tm[e % 2][rt][:, 128 * r:128 * (r + 1)],
                                        S.idb[:])
                PEG.inc(last)
                PEG.wait(vec)
                ins = None
                for r in range(DC):
                    ins = V(vec.tensor_copy(xg_fm[r][:, 128 * rt:128 * (rt + 1)],
                                            pbh[:, 128 * r:128 * (r + 1)]))
                EVm()
            trs_peg_set[e % 2] = PEG.n

            gelu_hist = []
            for jg in range(8):
                n = 8 * e + jg
                par = n % 4
                if jg < 4:
                    w2v = W2ball[:].rearrange("p (j c) -> p j c", c=D)
                    for b in (2 * jg, 2 * jg + 1):
                        GQ.dma(g.dma_start(
                            out=w2v[:, 4 * b:4 * (b + 1), :],
                            in_=S.W2_h[e, 512 * b:512 * (b + 1), :]
                            .rearrange("(q p) c -> p q c", p=128)))
                _w1_issue(n + 3)
                if jg == 2 and e + 2 < NE:
                    g_marks[e % 2] = _gathers(e + 2, defer=True)
                if jg == 6 and e + 2 < NE:
                    g_toks[e % 2] = GQ.chk()
                if jg == 7:
                    w2_tok = GQ.chk()
                pe.wait_ge(W1SL[par], w1_done[n])
                VS.wait(pe)
                for j4 in range(4):
                    j = 4 * jg + j4
                    for hi, (h0, hl) in enumerate(HALVES):
                        gidx = len(gelu_hist)
                        bank = pb[gidx % 4]
                        if gidx >= 4:
                            AS_.wait(pe, gelu_hist[gidx - 4])
                        elif evh:
                            EV.wait(pe, evh[-1])
                        for r in range(DC):
                            last = pe.matmul(bank[:, 0:hl],
                                             W1g[par][r][:, 128 * j4:128 * (j4 + 1)],
                                             xg_fm[r][:, h0:h0 + hl],
                                             start=(r == 0), stop=(r == DC - 1))
                        PEG.inc(last)
                        PEG.wait(act)
                        AS_.inc(act.activation(ubuf[j][:, h0:h0 + hl], bank[:, 0:hl],
                                               AF.Gelu,
                                               bias=S.b1call[:, JC * e + j:
                                                             JC * e + j + 1]))
                        gelu_hist.append(AS_.n)
                s1_peg_step[n] = PEG.n

            GQ.wait(pe, w2_tok)
            AS_.wait(pe)
            for rt in range(RT):
                for dh in range(2):
                    bank = pb[len(evh) % 4]
                    if len(evh) >= 4:
                        EV.wait(pe, evh[len(evh) - 4])
                    for j in range(JC):
                        last = pe.matmul(bank[:, 0:512],
                                         ubuf[j][:, 128 * rt:128 * (rt + 1)],
                                         W2b[j][:, 512 * dh:512 * (dh + 1)],
                                         start=(j == 0), stop=(j == JC - 1))
                    PEG.inc(last)
                    PEG.wait(vec)
                    GQ.d.wait(vec, scat_tok)
                    ins = V(vec.tensor_tensor(yesb[:, 512 * dh:512 * (dh + 1)],
                                              bank[:, 0:512],
                                              b2bc[:, 512 * dh:512 * (dh + 1)],
                                              ALU.add))
                    EVm()
                    evh.append(S.EV.n)
                EV.wait(g)
                GQ.dma(g.indirect_dma_start(
                    out=S.Y_dram,
                    out_offset=IndirectOffsetOnAxis(
                        S.idxdsty[:, RT * e + rt:RT * e + rt + 1], 0),
                    in_=yesb[:], in_offset=None,
                    bounds_check=S.bnd_y, oob_is_err=False))
                scat_tok = GQ.chk()
            s2_peg_prev = PEG.n

        if KDEBUG:
            GQ.chk()
            with ExitStack() as sd:
                yy = sb("yy_dbg", (128, D), stk=sd)
                for i in range(2 * OWNT):
                    SQ.d.wait(sy)
                    GQ.d.wait(sy)
                    SQ.dma(sy.dma_start(out=yy[:], in_=S.Y_dram[128 * i:128 * (i + 1), :]))
                    SQ.chk()
                    SQ.dma(sy.dma_start(out=S.taps["t_y"][128 * i:128 * (i + 1), :],
                                        in_=yy[:]))
                    SQ.chk()


def _phase5(S):
    sb = S.sb; V = S.V; VW = S.VW; EVm = S.EVm
    pe, vec, act, g, sy = S.pe, S.vec, S.act, S.g, S.sy
    SQ, GQ, PEG, EV, VS, AS_ = S.SQ, S.GQ, S.PEG, S.EV, S.VS, S.AS
    pbh, pb = S.pbh, S.pb

    with ExitStack() as s5:
        p5 = lambda name, shape, dtype=F32: sb(name, shape, dtype, s5)
        fw1b = [p5(f"fw1b{r}", (128, DFF), BF16) for r in range(DC)]
        fw2b = [p5(f"fw2b{j}", (128, D), BF16) for j in range(JC)]
        u2 = [p5(f"u2_{j}", (128, GSZ), BF16) for j in range(JC)]
        xsn2 = [[p5(f"xsn{p}_{r}", (128, GSZ), BF16) for r in range(DC)]
                for p in range(2)]
        yfb = p5("yfb", (128, D))
        xres = p5("xres", (128, D))
        hres = p5("hres", (128, D))
        y1 = p5("y1", (128, D))
        y2 = p5("y2", (128, D))

        for r in range(DC):
            qq = (SQ, S.AQ)[r % 2]
            qq.dma(qq.eng.dma_start(out=fw1b[r][:],
                                    in_=S.fw1bf[128 * r:128 * (r + 1), :]))
        fw1_toks = (SQ.chk(), S.AQ.chk())
        for j in range(JC):
            qq = (GQ, SQ, S.AQ)[j % 3]
            qq.dma(qq.eng.dma_start(out=fw2b[j][:],
                                    in_=S.fw2bf[128 * j:128 * (j + 1), :]))
        fw2_toks = (GQ.chk(), SQ.chk(), S.AQ.chk())


        s1_peg = [0, 0]
        evh5 = []
        xs_toks = [None, None]
        S.out_st_tok = 0

        aq_xs = [None]

        def _xsn_load(gi2):
            par2 = gi2 % 2
            t02 = gi2 * GSZ
            PEG.wait(sy, s1_peg[par2])
            base = DC * 128 * gi2
            rr = range(DC // 2, DC) if gi2 == 0 else range(DC)
            for r in rr:
                SQ.dma(sy.dma_start(out=xsn2[par2][r][:],
                                    in_=S.xnfm_dram[base + 128 * r:
                                                    base + 128 * (r + 1), :]))
            xs_toks[par2] = SQ.chk()
            if gi2 == 0:
                for r in range(DC // 2):
                    S.AQ.dma(act.dma_start(
                        out=xsn2[par2][r][:],
                        in_=S.xnfm_dram[base + 128 * r:base + 128 * (r + 1), :]))
                aq_xs[0] = S.AQ.chk()

        _xsn_load(0)
        for gi in range(NG):
            t0 = gi * GSZ
            par = gi % 2
            xsn = xsn2[par]
            SQ.wait(pe, xs_toks[par])
            if gi == 0:
                S.AQ.wait(pe, aq_xs[0])
                SQ.wait(pe, fw1_toks[0]); S.AQ.wait(pe, fw1_toks[1])
            if evh5:
                EV.wait(pe, evh5[-1])
            gelu_hist = []
            for j in range(JC):
                bank = pb[j % 4]
                if j >= 4:
                    AS_.wait(pe, gelu_hist[j - 4])
                for r in range(DC):
                    last = pe.matmul(bank[:, 0:GSZ], fw1b[r][:, 128 * j:128 * (j + 1)],
                                     xsn[r][:], start=(r == 0), stop=(r == DC - 1))
                PEG.inc(last)
                PEG.wait(act)
                AS_.inc(act.activation(u2[j][:], bank[:, 0:GSZ], AF.Gelu,
                                       bias=S.fb1c[:, j:j + 1]))
                gelu_hist.append(AS_.n)
            s1_peg[par] = PEG.n
            if gi + 1 < NG:
                _xsn_load(gi + 1)
            if gi == 0:
                GQ.wait(pe, fw2_toks[0]); SQ.wait(pe, fw2_toks[1])
                S.AQ.wait(pe, fw2_toks[2])
            AS_.wait(pe)
            for tt2 in range(GSZ // 128):
                tt = gi * (GSZ // 128) + tt2
                for dh in range(2):
                    bank = pb[(2 * tt2 + dh) % 4]
                    if len(evh5) >= 4:
                        EV.wait(pe, evh5[len(evh5) - 4])
                    for j in range(JC):
                        pe.matmul(bank[:, 0:512],
                                  u2[j][:, 128 * tt2:128 * (tt2 + 1)],
                                  fw2b[j][:, 512 * dh:512 * (dh + 1)],
                                  start=(j == 0), stop=False)
                    last = pe.matmul(bank[:, 0:512],
                                     S.idf[0:NE + 1, NE:NE + 1]
                                     .to_broadcast([NE + 1, 128]),
                                     S.b2all[:, 512 * dh:512 * (dh + 1)],
                                     start=False, stop=True)
                    PEG.inc(last)
                    PEG.wait(vec)
                    ins = V(vec.tensor_copy(yfb[:, 512 * dh:512 * (dh + 1)],
                                            bank[:, 0:512]))
                    EVm()
                    evh5.append(S.EV.n)
                VS.wait(act)
                SQ.d.wait(act, S.out_st_tok)
                S.AQ.dma(act.dma_start(out=xres[:],
                                       in_=S.xsl[HALO + 128 * tt:
                                                 HALO + 128 * (tt + 1), :]))
                S.AQ.dma(act.dma_start(out=hres[:],
                                       in_=S.h_dram[128 * tt:128 * (tt + 1), :]))
                S.AQ.dma(act.dma_start(out=y1[:],
                                       in_=S.Y_dram[128 * tt:128 * (tt + 1), :]))
                S.AQ.dma(act.dma_start(out=y2[:],
                                       in_=S.Y_dram[TOK + 128 * tt:
                                                    TOK + 128 * (tt + 1), :]))
                f_tok = S.AQ.chk()
                S.AQ.wait(vec, f_tok)
                VW(); V(vec.tensor_scalar(y1[:], y1[:], S.ms1t[:, tt:tt + 1], None,
                                          ALU.mult))
                VW(); V(vec.scalar_tensor_tensor(y1[:], y2[:], S.ms2t[:, tt:tt + 1],
                                                 y1[:], ALU.mult, ALU.add))
                VW(); V(vec.scalar_tensor_tensor(y1[:], yfb[:],
                                                 S.mscf[:, tt:tt + 1], y1[:],
                                                 ALU.mult, ALU.add))
                VW(); V(vec.tensor_tensor(y1[:], y1[:], xres[:], ALU.add))
                VW(); ins = V(vec.tensor_tensor(y2[:], y1[:], hres[:], ALU.add))
                EVm()
                VS.wait(sy)
                SQ.dma(sy.dma_start(out=S.out_h[128 * tt:128 * (tt + 1), :], in_=y2[:]))
                S.out_st_tok = SQ.chk()


_NC_CACHE = {}


def _consts():
    c = {}
    k = np.arange(128)
    c["idf"] = np.eye(128, dtype=np.float32)
    c["idb"] = np.eye(128, dtype=ml_dtypes.bfloat16)
    c["cumW"] = (k[:, None] <= k[None, :]).astype(ml_dtypes.bfloat16)
    c["cumSW"] = (k[:, None] < k[None, :]).astype(ml_dtypes.bfloat16)
    c["onesB"] = np.ones((128, 128), dtype=ml_dtypes.bfloat16)
    c["onesR"] = np.ones((1, 128), dtype=np.float32)
    c["onesC"] = np.ones((128, 1), dtype=np.float32)
    c["epscol"] = np.full((128, 1), 1e-5, dtype=np.float32)
    c["iota8"] = np.broadcast_to(np.arange(NE, dtype=np.float32), (128, NE)).copy()
    c["iotam8"] = c["iota8"] - 8.0
    c["negbig"] = np.full((128, 1), -3.0e38, dtype=np.float32)
    c["imask"] = np.full((128, 1), 0x7FFFFF, dtype=np.int32)
    lt = (np.arange(OWNT)[None, :] * 128 + k[:, None]).astype(np.int64)
    c["paytok"] = (lt + HALO).astype(np.int32)
    c["pdst1"] = lt.astype(np.int32)
    c["pdst2"] = (lt + TOK).astype(np.int32)
    return c


def kernel(L0_attn_x, L0_attn_Wf, L0_attn_Wi, L0_attn_Wh, L0_ffn_Wr, L0_ffn_W1,
           L0_ffn_b1, L0_ffn_W2, L0_ffn_b2, L0_ffn_fw1, L0_ffn_fb1, L0_ffn_fw2,
           L0_ffn_fb2):
    X = np.ascontiguousarray(np.asarray(L0_attn_x, dtype=np.float32).reshape(16384, D))
    xt = np.ascontiguousarray(X.T)
    consts = _consts()

    if "nc" not in _NC_CACHE:
        _NC_CACHE["nc"] = build_nc()
    nc, taps = _NC_CACHE["nc"]

    weights = dict(
        Wf=np.asarray(L0_attn_Wf, np.float32), Wi=np.asarray(L0_attn_Wi, np.float32),
        Wh=np.asarray(L0_attn_Wh, np.float32), Wr=np.asarray(L0_ffn_Wr, np.float32),
        W1=np.asarray(L0_ffn_W1, np.float32), b1=np.asarray(L0_ffn_b1, np.float32),
        W2=np.asarray(L0_ffn_W2, np.float32), b2=np.asarray(L0_ffn_b2, np.float32),
        fw1=np.asarray(L0_ffn_fw1, np.float32),
        fb1=np.asarray(L0_ffn_fb1, np.float32).reshape(1, DFF),
        fw2=np.asarray(L0_ffn_fw2, np.float32),
        fb2=np.asarray(L0_ffn_fb2, np.float32).reshape(1, D),
    )
    weights["b1t"] = np.ascontiguousarray(
        weights["b1"].reshape(NE, JC, 128).transpose(2, 0, 1).reshape(128, NE * JC))
    weights["fb1t"] = np.ascontiguousarray(
        weights["fb1"].reshape(JC, 128).T)

    in_maps = []
    zpad = np.zeros((HALO, D), dtype=np.float32)
    for c in range(NCORES):
        first = c in (0, NCORES // 2)
        if first:
            xslc = np.concatenate([zpad, X[TOK * c:TOK * (c + 1)]], axis=0)
        else:
            xslc = X[TOK * c - HALO:TOK * (c + 1)]
        sown = np.zeros((128, OWNT), dtype=np.float32)
        for j in range(OWNT):
            sown[OWNT * c + j, j] = 1.0
        im = dict(consts)
        im.update(weights)
        im["xsl"] = np.ascontiguousarray(xslc)
        im["xt"] = xt
        im["sown"] = sown
        im["halo_s"] = np.full((128, 1), 0.0 if first else 1.0, dtype=np.float32)
        im["halo_h"] = np.full((128, 1), 0.5 if first else 0.0, dtype=np.float32)
        in_maps.append(im)

    res = run_bass_kernel_spmd(nc, in_maps, core_ids=list(range(NCORES)))
    kernel._last = res
    out = np.concatenate([res.results[c]["out"] for c in range(NCORES)], axis=0)
    return out.reshape(2, 8192, D)


# version 31 (final)
# speedup vs baseline: 1.0767x; 1.0043x over previous
import numpy as np
import ml_dtypes
from contextlib import ExitStack

import concourse.bass as bass
import concourse.mybir as mybir
from concourse.bass_core_rust import IndirectOffsetOnAxis
from concourse.bass_utils import run_bass_kernel_spmd

F32 = mybir.dt.float32
BF16 = mybir.dt.bfloat16
I32 = mybir.dt.int32
U32 = mybir.dt.uint32
AF = mybir.ActivationFunctionType
ALU = mybir.AluOpType
AXX = mybir.AxisListType.X

NCORES = 8
TOK = 2048
HALO = 256
TOKH = TOK + HALO
NT = TOKH // 128
OWNT = TOK // 128
D = 1024
DC = D // 128
DFF = 4096
JC = DFF // 128
NE = 8
CAP = 2560
LCAP = 640
RT = LCAP // 128
GT = 16384 // 128
GSZ = 512
NG = TOK // GSZ
BIG = 6.0e6
E23 = 8388608.0
BIG_I = 6000000

PHASES = 6
KDEBUG = False


class _NS:
    pass


class _ColView:

    def __init__(self, h, off, width):
        self.h = h; self.off = off; self.w = width

    def __getitem__(self, idx):
        if idx == slice(None):
            return self.h[:, self.off:self.off + self.w]
        ps, fs = idx
        a = fs.start if fs.start is not None else 0
        b = fs.stop if fs.stop is not None else self.w
        return self.h[:, self.off + a:self.off + b]


def build_nc():
    nc = bass.Bass("TRN2", target_bir_lowering=False)
    S = _NS()
    S.nc = nc

    def din(name, shape, dtype=F32):
        return nc.dram_tensor(name, list(shape), dtype, kind="ExternalInput").ap()

    S.xsl = din("xsl", (TOKH, D))
    S.xt = din("xt", (D, 16384))
    S.Wf_h = din("Wf", (D, D)); S.Wi_h = din("Wi", (D, D)); S.Wh_h = din("Wh", (D, D))
    S.Wr_h = din("Wr", (D, NE))
    S.W1_h = din("W1", (NE, D, DFF)); S.b1_h = din("b1", (NE, DFF))
    S.W2_h = din("W2", (NE, DFF, D)); S.b2_h = din("b2", (NE, D))
    S.fw1_h = din("fw1", (D, DFF)); S.fb1_h = din("fb1", (1, DFF))
    S.fw2_h = din("fw2", (DFF, D)); S.fb2_h = din("fb2", (1, D))
    S.b1t_h = din("b1t", (128, NE * JC)); S.fb1t_h = din("fb1t", (128, JC))
    consts_in = {}
    for nm, shape, dt in [
            ("idf", (128, 128), F32), ("idb", (128, 128), BF16),
            ("cumW", (128, 128), BF16), ("cumSW", (128, 128), BF16),
            ("onesB", (128, 128), BF16), ("onesR", (1, 128), F32),
            ("onesC", (128, 1), F32), ("epscol", (128, 1), F32),
            ("iota8", (128, NE), F32), ("iotam8", (128, NE), F32),
            ("negbig", (128, 1), F32), ("imask", (128, 1), I32),
            ("halo_s", (128, 1), F32), ("halo_h", (128, 1), F32),
            ("sown", (128, OWNT), F32),
            ("paytok", (128, OWNT), I32), ("pdst1", (128, OWNT), I32),
            ("pdst2", (128, OWNT), I32)]:
        consts_in[nm] = din(nm, shape, dt)

    S.out_h = nc.dram_tensor("out", [TOK, D], F32, kind="ExternalOutput").ap()
    S.taps = {}
    if KDEBUG:
        def tap(name, shape, dtype=F32):
            S.taps[name] = nc.dram_tensor(name, list(shape), dtype,
                                          kind="ExternalOutput").ap()
        tap("t_xn", (TOKH, D)); tap("t_utm", (128, GT * NE))
        tap("t_keep", (128, OWNT * NE)); tap("t_route", (128, OWNT * 12))
        tap("t_h", (TOK, D)); tap("t_idx", (128, RT * NE * 2), I32)
        tap("t_y", (2 * TOK, D))

    S.xn_dram = nc.dram_tensor("xn_dram", [TOKH, D], BF16, kind="Internal").ap()
    S.xnfm_dram = nc.dram_tensor("xnfm_dram", [NG * DC * 128, GSZ], BF16,
                                 kind="Internal").ap()
    S.h_dram = nc.dram_tensor("h_dram", [TOK, D], F32, kind="Internal").ap()
    S.idx_dram = nc.dram_tensor("idx_dram", [NE * LCAP, 1], I32, kind="Internal").ap()
    S.dst_dram = nc.dram_tensor("dst_dram", [NE * LCAP, 1], I32, kind="Internal").ap()
    S.Y_dram = nc.dram_tensor("Y_dram", [2 * TOK, D], F32, kind="Internal").ap()
    S.fw1bf = nc.dram_tensor("fw1bf", [D, DFF], BF16, kind="Internal").ap()
    S.fw2bf = nc.dram_tensor("fw2bf", [DFF, D], BF16, kind="Internal").ap()

    ctx = ExitStack()
    with ctx:
        def sb(name, shape, dtype=F32, stk=None):
            return (stk or ctx).enter_context(nc.sbuf_tensor(name, list(shape), dtype))
        S.sb = sb

        class Ctr:
            def __init__(self, name):
                self.s = ctx.enter_context(nc.semaphore(name)); self.n = 0
                self.wm = {}
            def inc(self, instr, k=1):
                instr.then_inc(self.s, k); self.n += k; return instr
            def wait(self, eng, n=None):
                v = self.n if n is None else n
                key = id(eng)
                if self.wm.get(key, -1) >= v:
                    return
                eng.wait_ge(self.s, v)
                self.wm[key] = v

        class Q:
            def __init__(self, eng, name):
                self.eng = eng
                self.s = ctx.enter_context(nc.semaphore(name)); self.n = 0
                self.d = Ctr(name + "d")
            def dma(self, instr):
                instr.then_inc(self.s, 16); self.n += 16; return instr
            def chk(self):
                self.eng.wait_ge(self.s, self.n)
                self.d.inc(self.eng.nop())
                return self.d.n
            def mark(self):
                return self.n
            def chk_for(self, n):
                self.eng.wait_ge(self.s, n)
                self.d.inc(self.eng.nop())
                return self.d.n
            def wait(self, eng, tok=None):
                self.d.wait(eng, tok)

        S.vec = nc.vector; S.act = nc.scalar; S.pe = nc.tensor
        S.g = nc.gpsimd; S.sy = nc.sync

        S.SQ = Q(S.sy, "SQ")
        S.AQ = Q(S.act, "AQ")
        S.GQ = Q(S.g, "GQ")
        S.PEG = Ctr("PEG")
        S.PQ = Ctr("PQ")
        S.VS = Ctr("VS")
        S.AS = Ctr("AS")
        S.EV = Ctr("EV")

        def V(instr):
            return S.VS.inc(instr)

        def VW():
            S.vec.wait_ge(S.VS.s, S.VS.n)

        def EVm():
            S.vec.wait_ge(S.VS.s, S.VS.n)
            S.EV.inc(S.vec.nop())

        S.V = V; S.VW = VW; S.EVm = EVm

        def barrier():
            S.SQ.chk(); S.GQ.chk(); S.AQ.chk()
            nc.all_engine_barrier()
        S.barrier = barrier

        for nm in consts_in:
            dt = consts_in[nm].dtype
            shp = list(consts_in[nm].shape)
            setattr(S, nm, sb("c_" + nm, shp, dt))
        S.rstd_all = sb("rstd_all", (128, NT))
        S.s1t = sb("s1t", (128, OWNT)); S.s2t = sb("s2t", (128, OWNT))
        S.mself = sb("mself", (128, OWNT))
        S.ms1t = sb("ms1t", (128, OWNT)); S.ms2t = sb("ms2t", (128, OWNT))
        S.mscf = sb("mscf", (128, OWNT))
        S.keep_own = sb("keep_own", (128, OWNT, NE))
        S.i1own = sb("i1own", (128, OWNT)); S.i2own = sb("i2own", (128, OWNT))
        S.idxtok = sb("idxtok", (128, RT * NE), I32)
        S.idxdsty = sb("idxdsty", (128, RT * NE), I32)
        S.b1call = sb("b1call", (128, NE * JC))
        S.fb1c = sb("fb1c", (128, JC))
        S.b2all = sb("b2all", (NE + 1, D))

        S.pb = [ctx.enter_context(nc.psum_tensor(f"pb{i}", [128, 512], F32))
                for i in range(7)]
        S.pbh = ctx.enter_context(nc.psum_tensor("pbh", [128, 1024], BF16))

        for nm in consts_in:
            S.SQ.dma(S.sy.dma_start(out=getattr(S, nm)[:], in_=consts_in[nm]))
        ctok = S.SQ.chk()
        for eng in (S.pe, S.vec, S.act, S.g):
            S.SQ.wait(eng, ctok)

        S.bnd_idx = S.g.alloc_register("bnd_idx"); S.g.reg_mov(S.bnd_idx, NE * LCAP - 1)
        S.bnd_xn = S.g.alloc_register("bnd_xn"); S.g.reg_mov(S.bnd_xn, TOKH - 1)
        S.bnd_y = S.g.alloc_register("bnd_y"); S.g.reg_mov(S.bnd_y, 2 * TOK - 1)

        with ExitStack() as s34:
            S.xg_tm = [[sb(f"xg_tm{ss}_{i}", (128, D), BF16, s34)
                        for i in range(RT)] for ss in range(2)]
            S.zb = sb("zb", (128, D), F32, s34)
            S.dst1i = sb("dst1i", (128, OWNT), I32, s34)
            S.dst2i = sb("dst2i", (128, OWNT), I32, s34)
            S.bigt = sb("bigt", (128, RT * NE), I32, s34)
            with ExitStack() as scopeA:
                S.WTall = {w: sb(f"W{w}Tall", (128, DC * D), BF16, scopeA)
                           for w in "fih"}
                S.WT = {w: [_ColView(S.WTall[w], r * D, D) for r in range(DC)]
                        for w in "fih"}
                S.wr_hi = [sb(f"wr_hi{r}", (128, NE), BF16, scopeA)
                           for r in range(DC)]
                S.wr_lo = [sb(f"wr_lo{r}", (128, NE), BF16, scopeA)
                           for r in range(DC)]
                S.xn_fmall = sb("xn_fmall", (128, DC * TOKH), BF16, scopeA)
                S.xn_fm = [_ColView(S.xn_fmall, r * TOKH, TOKH) for r in range(DC)]
                S.U_tm = sb("U_tm", (128, GT * NE), F32, scopeA)
                S.pbh_vs = [0]

                _phase0(S)
                barrier()
                with ExitStack() as scopeB:
                    p0b_tile = _phase0b_emitter(S, scopeB)
                    p1_tile, p1_fin = _phase1_emitter(S, scopeB)
                    _phase2a(S, scopeB, [(3 * DC, p0b_tile, 30), (NT, p1_tile, 22)])
                    p1_fin()
                _phase2b(S)
                _phase3(S)
                barrier()
            if PHASES >= 4:
                _phase4(S)
                barrier()
        if PHASES >= 5:
            _phase5(S)
        S.sy.wait_ge(S.SQ.s, S.SQ.n)
        S.g.wait_ge(S.GQ.s, S.GQ.n)

    return nc, S.taps


def _phase0(S):
    sb = S.sb; V = S.V; VW = S.VW; EVm = S.EVm
    pe, vec, act, g, sy = S.pe, S.vec, S.act, S.g, S.sy
    SQ, GQ, PEG, EV, VS = S.SQ, S.GQ, S.PEG, S.EV, S.VS
    act = S.act
    pbh, pb = S.pbh, S.pb

    with ExitStack() as s0:
        zb = S.zb
        VW(); ins = V(vec.memset(zb[:], 0.0)); EVm()
        EV.wait(act)
        for i in range(2 * OWNT):
            S.AQ.dma(act.dma_start(out=S.Y_dram[128 * i:128 * (i + 1), :], in_=zb[:]))
        S.yz_tok = S.AQ.chk()

        wr_f = [sb(f"wr_f{r}", (128, NE), stk=s0) for r in range(DC)]
        srow = sb("srow", (1, NE), stk=s0)
        sneg = sb("sneg", (1, NE), stk=s0)
        sbc = sb("sbc", (128, NE), stk=s0)
        wrhat = sb("wrhat", (128, NE), stk=s0)

        for r in range(DC):
            SQ.dma(sy.dma_start(out=wr_f[r][:], in_=S.Wr_h[128 * r:128 * (r + 1), :]))
        tok = SQ.chk()
        SQ.dma(sy.dma_start(out=S.b1call[:], in_=S.b1t_h))
        SQ.dma(sy.dma_start(out=S.fb1c[:], in_=S.fb1t_h))
        SQ.dma(sy.dma_start(out=S.b2all[0:NE, :], in_=S.b2_h[:, :]))
        SQ.dma(sy.dma_start(out=S.b2all[NE:NE + 1, :], in_=S.fb2_h))
        SQ.chk()
        SQ.wait(pe, tok)
        EV.wait(pe)
        for r in range(DC):
            last = pe.matmul(pb[5][0:1, 0:NE], S.onesC[:, 0:1], wr_f[r][:],
                             start=(r == 0), stop=(r == DC - 1))
        PEG.inc(last)
        PEG.wait(vec)
        V(vec.tensor_copy(srow[:], pb[5][0:1, 0:NE]))
        VW()
        ins = V(vec.tensor_scalar(sneg[:], srow[:], -1.0 / 1024.0, None, ALU.mult))
        EVm()
        EV.wait(pe)
        VS.wait(pe)
        last = pe.matmul(pb[5][0:128, 0:NE], S.onesR[0:1, :], sneg[:],
                         start=True, stop=True)
        PEG.inc(last)
        PEG.wait(vec)
        V(vec.tensor_copy(sbc[:], pb[5][0:128, 0:NE]))
        for r in range(DC):
            VW()
            V(vec.tensor_tensor(wrhat[:], wr_f[r][:], sbc[:], ALU.add))
            VW()
            V(vec.tensor_copy(S.wr_hi[r][:], wrhat[:]))
            VW()
            V(vec.tensor_tensor(S.wr_lo[r][:], wrhat[:], S.wr_hi[r][:], ALU.subtract))
        VW()
        ins = V(vec.nop())
        EVm()


def _phase0b_emitter(S, stk):
    sb = S.sb; V = S.V; EVm = S.EVm
    pe, vec, act, g = S.pe, S.vec, S.act, S.g
    GQ, PEG, EV, AS_, VS = S.GQ, S.PEG, S.EV, S.AS, S.VS
    pbh = S.pbh

    wrow = [sb(f"wrow{i}", (128, D), BF16, stk) for i in range(2)]
    wsrc = [("f", S.Wf_h), ("i", S.Wi_h), ("h", S.Wh_h)]
    wr_peg = [0, 0]

    def emit_chunk(k):
        w, whbm = wsrc[k // DC]
        ec = k % DC
        par = k % 2
        buf = wrow[par]
        PEG.wait(g, wr_peg[par])
        GQ.dma(g.dma_start(out=buf[:], in_=whbm[128 * ec:128 * (ec + 1), :]))
        tok = GQ.chk()
        GQ.wait(pe, tok)
        VS.wait(pe, S.pbh_vs[0])
        last = None
        for dr in range(DC):
            last = pe.transpose(pbh[:, 128 * dr:128 * (dr + 1)],
                                buf[:, 128 * dr:128 * (dr + 1)], S.idb[:])
        PEG.inc(last)
        wr_peg[par] = S.PEG.n
        PEG.wait(vec, wr_peg[par])
        V(vec.tensor_copy(
            S.WTall[w][:].rearrange("p (r d) -> p r d", r=DC)[:, :,
                                                             128 * ec:128 * (ec + 1)],
            pbh[:].rearrange("p (r q) -> p r q", r=DC)))
        S.pbh_vs[0] = S.VS.n

    return emit_chunk


def _phase1_emitter(S, stk):
    sb = S.sb; V = S.V; VW = S.VW; EVm = S.EVm
    pe, vec, act, g, sy = S.pe, S.vec, S.act, S.g, S.sy
    SQ, GQ, PEG, EV, VS, AS_ = S.AQ, S.GQ, S.PEG, S.EV, S.VS, S.AS
    pbh, pb = S.pbh, S.pb

    if True:
        xtile = [sb(f"xtile{i}", (128, D), stk=stk) for i in range(2)]
        sqs = sb("sqs", (128, D), stk=stk)
        mucol = sb("mucol", (128, 1), stk=stk)
        m2col = sb("m2col", (128, 1), stk=stk)
        varcol = sb("varcol", (128, 1), stk=stk)
        sdcol = sb("sdcol", (128, 1), stk=stk)
        xnbf = [sb(f"xnbf{i}", (128, D), BF16, stk) for i in range(2)]

        vs_xn = [0, 0]
        peg_xb = [0, 0]
        ev_var = [0]
        def emit_tile(i):
            par = i % 2
            buf = xtile[par]
            VS.wait(act, vs_xn[par])
            SQ.dma(act.dma_start(out=buf[:], in_=S.xsl[128 * i:128 * (i + 1), :]))
            ltok = SQ.chk()
            SQ.wait(vec, ltok)
            V(vec.reduce_sum(mucol[:], buf[:], AXX))
            SQ.wait(act, ltok)
            EV.wait(act, ev_var[0])
            AS_.inc(act.activation(sqs[:], buf[:], AF.Square, accum_out=pb[0][:, 0:1]))
            VW(); V(vec.tensor_scalar(mucol[:], mucol[:], 1.0 / 1024.0, None, ALU.mult))
            VW(); V(vec.tensor_tensor(m2col[:], mucol[:], mucol[:], ALU.mult))
            AS_.wait(vec)
            VW()
            ins = V(vec.scalar_tensor_tensor(varcol[:], pb[0][:, 0:1], 1.0 / 1024.0,
                                             m2col[:], ALU.mult, ALU.subtract))
            EVm()
            ev_var[0] = S.EV.n
            VS.wait(act)
            AS_.inc(act.activation(sdcol[:], varcol[:], AF.Sqrt, bias=S.epscol[:, 0:1]))
            AS_.wait(vec)
            V(vec.reciprocal(S.rstd_all[:, i:i + 1], sdcol[:]))
            VW()
            xb = xnbf[par]
            SQ.d.wait(vec)
            PEG.wait(vec, peg_xb[par])
            ins = V(vec.tensor_scalar(xb[:], buf[:], mucol[:], S.rstd_all[:, i:i + 1],
                                      ALU.subtract, ALU.mult))
            EVm()
            vs_xn[par] = S.VS.n
            VS.wait(act)
            SQ.dma(act.dma_start(out=S.xn_dram[128 * i:128 * (i + 1), :], in_=xb[:]))
            SQ.chk()
            VS.wait(pe, max(vs_xn[par], S.pbh_vs[0]))
            for r in range(DC):
                last = pe.transpose(pbh[:, 128 * r:128 * (r + 1)],
                                    xb[:, 128 * r:128 * (r + 1)], S.idb[:])
            PEG.inc(last)
            peg_xb[par] = PEG.n
            PEG.wait(vec, peg_xb[par])
            V(vec.tensor_copy(
                S.xn_fmall[:].rearrange("p (r t) -> p r t", r=DC)[:, :,
                                                                 128 * i:
                                                                 128 * (i + 1)],
                pbh[:].rearrange("p (r q) -> p r q", r=DC)))
            S.pbh_vs[0] = S.VS.n

        def finish():
            VS.wait(act)
            for gi in range(NG):
                for r in range(DC):
                    SQ.dma(act.dma_start(
                        out=S.xnfm_dram[DC * 128 * gi + 128 * r:
                                        DC * 128 * gi + 128 * (r + 1), :],
                        in_=S.xn_fm[r][:, HALO + GSZ * gi:HALO + GSZ * (gi + 1)]))
            _phase1_debug(S, sb)

        return emit_tile, finish


def _phase1_debug(S, sb):
    V = S.V; VW = S.VW; EVm = S.EVm
    act, vec, sy = S.act, S.vec, S.sy
    SQ, VS = S.AQ, S.VS
    if True:
        if KDEBUG:
            SQ.chk()
            with ExitStack() as sdbg:
                xbb = sb("xbb_dbg", (128, D), BF16, sdbg)
                xff = sb("xff_dbg", (128, D), F32, sdbg)
                for i in range(NT):
                    SQ.d.wait(sy)
                    SQ.dma(act.dma_start(out=xbb[:],
                                        in_=S.xn_dram[128 * i:128 * (i + 1), :]))
                    t2 = SQ.chk()
                    SQ.wait(vec, t2)
                    VW(); ins = V(vec.tensor_copy(xff[:], xbb[:]))
                    EVm()
                    VS.wait(act)
                    SQ.dma(act.dma_start(out=S.taps["t_xn"][128 * i:128 * (i + 1), :],
                                        in_=xff[:]))
                    SQ.chk()


def _phase2a(S, stk, tile_cb=None):
    nc = S.nc; sb = S.sb; V = S.V; VW = S.VW; EVm = S.EVm
    pe, vec, act, g, sy = S.pe, S.vec, S.act, S.g, S.sy
    SQ, GQ, PEG, EV, VS, AS_, PQ = S.SQ, S.GQ, S.PEG, S.EV, S.VS, S.AS, S.PQ
    pbh, pb = S.pbh, S.pb
    U_tm = S.U_tm

    with ExitStack() as s2:
        p2 = lambda name, shape, dtype=F32: sb(name, shape, dtype, stk)
        xtf = [p2(f"xtf{i}", (128, 512)) for i in range(10)]
        xth = [p2(f"xth{i}", (128, 512), BF16) for i in range(10)]
        xtl = [p2(f"xtl{i}", (128, 512), BF16) for i in range(10)]
        ufm = [p2(f"ufm{i}", (8, 512)) for i in range(2)]

        NSL = 10
        LSL = [stk.enter_context(nc.semaphore(f"LSL{i}")) for i in range(NSL)]
        CSL = [stk.enter_context(nc.semaphore(f"CSL{i}")) for i in range(NSL)]
        ln_ctr = [0] * NSL
        cs_ctr = [0] * NSL

        sub_done = {}; mm_done = {}; evac_hist = {}; tr_done = {}
        emitted = [0] * len(tile_cb or [])

        def _pace(t):
            for ei, (total, fn, hz) in enumerate(tile_cb or []):
                while emitted[ei] < min(total, ((t + 1) * total) // hz):
                    fn(emitted[ei]); emitted[ei] += 1

        for t in range(32):
            bank8 = (pb[5] if t % 2 == 0 else pb[6])[0:8, 0:512]
            for r in range(DC):
                q = t * DC + r
                bi = q % NSL
                if q >= NSL:
                    sy.wait_ge(CSL[bi], 16 * cs_ctr[bi])
                    VS.wait(sy, sub_done[q - NSL])
                ins = sy.dma_start(out=xtf[bi][:],
                                   in_=S.xt[128 * r:128 * (r + 1), 512 * t:512 * (t + 1)])
                ins.then_inc(LSL[bi], 16); ln_ctr[bi] += 1
                if q >= NSL:
                    PEG.wait(g, mm_done[q - NSL])
                    VS.wait(g, sub_done[q - NSL])
                g.wait_ge(LSL[bi], 16 * ln_ctr[bi])
                ins = g.dma_start(out=xth[bi][:], in_=xtf[bi][:])
                ins.then_inc(CSL[bi], 16); cs_ctr[bi] += 1
                if q >= NSL:
                    PEG.wait(vec, mm_done[q - NSL])
                vec.wait_ge(CSL[bi], 16 * cs_ctr[bi])
                V(vec.tensor_tensor(xtl[bi][:], xtf[bi][:], xth[bi][:], ALU.subtract))
                sub_done[q] = S.VS.n
                pe.wait_ge(CSL[bi], 16 * cs_ctr[bi])
                VS.wait(pe, sub_done[q])
                if r == 0 and t >= 2:
                    EV.wait(pe, evac_hist[t - 2])
                pe.matmul(bank8, S.wr_hi[r][:], xth[bi][:], start=(r == 0), stop=False)
                m2 = pe.matmul(bank8, S.wr_lo[r][:], xth[bi][:], start=False, stop=False)
                PEG.inc(m2)
                last = pe.matmul(bank8, S.wr_hi[r][:], xtl[bi][:], start=False,
                                 stop=(r == DC - 1))
                PEG.inc(last)
                mm_done[q] = PEG.n
            ub = ufm[t % 2]
            PEG.wait(vec)
            if t >= 2:
                PEG.wait(vec, tr_done[t - 2])
            ins = V(vec.tensor_copy(ub[:], bank8))
            EVm()
            evac_hist[t] = EV.n
            pack = (t // 4) % 2
            EV.wait(pe, evac_hist[t])
            for q4 in range(4):
                last = pe.transpose(pb[4][:, 256 * pack + 32 * (t % 4) + 8 * q4:
                                          256 * pack + 32 * (t % 4) + 8 * q4 + 8],
                                    ub[:, 128 * q4:128 * (q4 + 1)], S.idf[0:8, 0:8])
            PEG.inc(last)
            tr_done[t] = PEG.n
            if t % 4 == 3:
                gt0 = (t - 3) * 4
                PEG.wait(vec)
                ins = V(vec.tensor_copy(U_tm[:, NE * gt0:NE * gt0 + 128],
                                        pb[4][:, 256 * pack:256 * pack + 128]))
                EVm()
            _pace(t)

        if KDEBUG:
            VS.wait(sy)
            AS_.wait(sy)
            SQ.dma(sy.dma_start(out=S.taps["t_utm"], in_=U_tm[:]))
            SQ.chk()


def _phase2b(S):
    nc = S.nc; sb = S.sb; V = S.V; VW = S.VW; EVm = S.EVm
    pe, vec, act, g, sy = S.pe, S.vec, S.act, S.g, S.sy
    SQ, GQ, PEG, EV, VS, AS_, PQ = S.SQ, S.GQ, S.PEG, S.EV, S.VS, S.AS, S.PQ
    pbh, pb = S.pbh, S.pb
    U_tm3 = S.U_tm[:].rearrange("p (a b) -> p a b", b=NE)

    with ExitStack() as s2:
        p2 = lambda name, shape, dtype=F32: sb(name, shape, dtype, s2)

        for e in (sy, g, act, vec):
            PEG.wait(e); VS.wait(e); AS_.wait(e); S.AQ.wait(e)
        EV.wait(pe); AS_.wait(pe)

        m1 = p2("m1", (128, GT)); i1f = p2("i1f", (128, GT))
        m2 = p2("m2", (128, GT)); i2f = p2("i2f", (128, GT))
        equ = p2("equ", (128, GT, NE))
        scr3 = p2("scr3", (128, GT, NE))
        U2s = p2("U2s", (128, GT, NE))

        b3 = lambda colap: colap.unsqueeze(1).to_broadcast([128, GT, NE])
        bm = lambda m: m.unsqueeze(2).to_broadcast([128, GT, NE])
        iota3 = S.iota8[:].unsqueeze(1).to_broadcast([128, GT, NE])

        iotam3 = S.iotam8[:].unsqueeze(1).to_broadcast([128, GT, NE])
        BIGM = 1.0e38
        VW(); V(vec.tensor_reduce(m1[:], U_tm3, AXX, ALU.max))
        VW(); V(vec.tensor_tensor(equ[:], U_tm3, bm(m1[:]), ALU.is_ge))
        VW(); V(vec.tensor_tensor(scr3[:], equ[:], iotam3, ALU.mult))
        VW(); V(vec.tensor_scalar(scr3[:], scr3[:], 8.0, None, ALU.add))
        VW(); V(vec.tensor_reduce(i1f[:], scr3[:], AXX, ALU.min))
        VW(); V(vec.tensor_tensor(equ[:], iota3, bm(i1f[:]), ALU.is_equal))
        VW(); V(vec.tensor_scalar(U2s[:], U_tm3, BIGM, None, ALU.add))
        VW(); V(vec.tensor_tensor(U2s[:], equ[:], U2s[:], ALU.mult))
        VW(); V(vec.tensor_tensor(U2s[:], U_tm3, U2s[:], ALU.subtract))
        VW(); V(vec.tensor_reduce(m2[:], U2s[:], AXX, ALU.max))
        VW(); V(vec.tensor_tensor(equ[:], U2s[:], bm(m2[:]), ALU.is_ge))
        VW(); V(vec.tensor_tensor(scr3[:], equ[:], iotam3, ALU.mult))
        VW(); V(vec.tensor_scalar(scr3[:], scr3[:], 8.0, None, ALU.add))
        VW(); V(vec.tensor_reduce(i2f[:], scr3[:], AXX, ALU.min))

        nfb = p2("nfb", (128, 128))
        z128 = p2("z128", (128, 128))
        aT = p2("aT", (128, 128), BF16)
        a_sm = p2("a_sm", (128, 128))
        elig = p2("elig", (128, 128))
        rowc = p2("rowc", (128, 128))
        rtot = p2("rtot", (128, 1), BF16)
        rank = p2("rank", (128, 128))
        c1t = p2("c1t", (128, 128))
        keepC = p2("keepC", (128, 128))
        overC = p2("overC", (128, 128))
        scr128 = p2("scr128", (128, 128))
        sel16 = p2("sel16", (128, 128))

        def extract_own(src_ap, dst_ap, seg_src):
            if not seg_src:
                VS.wait(pe)
                EV.wait(pe)
                last = pe.transpose(pb[4][:, 0:128], src_ap, S.idf[:])
                PEG.inc(last)
                PEG.wait(vec)
                VW(); ins = V(vec.tensor_copy(a_sm[:], pb[4][:, 0:128]))
                EVm()
                seg = a_sm[:]
            else:
                seg = src_ap
            VS.wait(pe)
            EV.wait(pe)
            last = pe.matmul(pb[4][0:OWNT, 0:128], S.sown[:], seg, start=True, stop=True)
            PEG.inc(last)
            PEG.wait(vec)
            VW(); ins = V(vec.tensor_copy(sel16[0:OWNT, :], pb[4][0:OWNT, 0:128]))
            EVm()
            VS.wait(pe)
            EV.wait(pe)
            last = pe.transpose(pb[4][:, 128:256], sel16[:], S.idf[:])
            PEG.inc(last)
            PEG.wait(vec)
            VW(); ins = V(vec.tensor_copy(dst_ap, pb[4][:, 128:128 + OWNT]))
            EVm()

        VW(); V(vec.memset(nfb[:], 1.0))
        VW(); V(vec.memset(z128[:], 0.0))
        for e in range(NE):
            fe = float(e)
            VW(); V(vec.tensor_scalar(scr128[:], i1f[:], fe, None, ALU.is_equal))
            VW(); V(vec.tensor_scalar(overC[:], i2f[:], fe, None, ALU.is_equal))
            VW(); ins = V(vec.tensor_tensor(aT[:], scr128[:], overC[:], ALU.add))
            EVm()
            EV.wait(pe)
            VS.wait(pe)
            last = pe.transpose(pbh[:, 0:128], aT[:], S.idb[:])
            PEG.inc(last)
            PEG.wait(vec)
            VW(); ins = V(vec.tensor_copy(a_sm[:], pbh[:, 0:128]))
            EVm()
            VW(); V(vec.tensor_tensor(elig[:], a_sm[:], nfb[:], ALU.mult))
            VW(); V(vec.tensor_tensor_scan(rowc[:], elig[:], z128[:], 0.0,
                                           ALU.add, ALU.add))
            VW(); ins = V(vec.tensor_copy(rtot[:], rowc[:, 127:128]))
            EVm()
            EV.wait(pe)
            VS.wait(pe)
            last = pe.matmul(pb[6][:, 0:1], S.cumSW[:], rtot[:], start=True, stop=True)
            PEG.inc(last)
            PEG.wait(vec)
            VW(); ins = V(vec.tensor_scalar(rank[:], rowc[:], pb[6][:, 0:1], None, ALU.add))
            EVm()
            VW(); V(vec.tensor_scalar(c1t[:], rank[:], float(CAP), None, ALU.is_le))
            VW(); V(vec.tensor_tensor(keepC[:], elig[:], c1t[:], ALU.mult))
            VW(); V(vec.tensor_tensor(overC[:], elig[:], keepC[:], ALU.subtract))
            VW(); V(vec.tensor_tensor(nfb[:], nfb[:], overC[:], ALU.subtract))
            extract_own(keepC[:], S.keep_own[:, :, e:e + 1].squeeze(2), seg_src=True)

        extract_own(m1[:], S.s1t[:, 0:OWNT], seg_src=False)
        extract_own(m2[:], S.s2t[:, 0:OWNT], seg_src=False)
        extract_own(i1f[:], S.i1own[:, 0:OWNT], seg_src=False)
        extract_own(i2f[:], S.i2own[:, 0:OWNT], seg_src=False)

        d12 = p2("d12", (128, OWNT))
        z12 = p2("z12", (128, OWNT))
        k1c = p2("k1c", (128, OWNT)); k2c = p2("k2c", (128, OWNT))
        o3 = p2("o3", (128, OWNT, NE))
        lcum = p2("lcum", (128, OWNT, NE))
        slot1 = p2("slot1", (128, OWNT)); slot2 = p2("slot2", (128, OWNT))
        msel = p2("msel", (128, OWNT))
        scrO = p2("scrO", (128, OWNT))
        scrO2 = p2("scrO2", (128, OWNT))
        keep_bf = p2("keep_bf", (128, OWNT, NE), BF16)
        dst1i = S.dst1i
        dst2i = S.dst2i
        bigt = S.bigt

        VW(); V(vec.tensor_tensor(d12[:], S.s1t[:], S.s2t[:], ALU.subtract))
        VW(); V(vec.tensor_tensor(z12[:], d12[:], S.rstd_all[:, 2:2 + OWNT], ALU.mult))
        VS.wait(act)
        AS_.inc(act.activation(S.s1t[:], z12[:], AF.Sigmoid))
        AS_.wait(vec)
        VW(); V(vec.tensor_scalar(S.s2t[:], S.s1t[:], -1.0, 1.0, ALU.mult, ALU.add))

        bO = lambda m: m.unsqueeze(2).to_broadcast([128, OWNT, NE])
        iotaO = S.iota8[:].unsqueeze(1).to_broadcast([128, OWNT, NE])
        sO = scr3[:, 0:OWNT, :]
        VW(); V(vec.tensor_tensor(o3[:], iotaO, bO(S.i1own[:]), ALU.is_equal))
        VW(); V(vec.tensor_tensor(sO, o3[:], S.keep_own[:], ALU.mult))
        VW(); V(vec.tensor_reduce(k1c[:], sO, AXX, ALU.max))
        VW(); V(vec.tensor_tensor(o3[:], iotaO, bO(S.i2own[:]), ALU.is_equal))
        VW(); V(vec.tensor_tensor(sO, o3[:], S.keep_own[:], ALU.mult))
        VW(); V(vec.tensor_reduce(k2c[:], sO, AXX, ALU.max))
        VW(); ins = V(vec.tensor_copy(keep_bf[:], S.keep_own[:]))
        EVm()
        EV.wait(pe)
        VS.wait(pe)
        for i in range(OWNT):
            dst = pb[5][:, 8 * i:8 * (i + 1)]
            for jj in range(i + 1):
                last = pe.matmul(dst, lhs if False else (S.cumW[:] if jj == i else S.onesB[:]),
                                 keep_bf[:, jj, :], start=(jj == 0), stop=(jj == i))
        PEG.inc(last)
        PEG.wait(vec)
        VW()
        ins = V(vec.tensor_copy(lcum[:],
                                pb[5][:, 0:128].rearrange("p (a b) -> p a b", a=OWNT)))
        EVm()
        VW(); V(vec.tensor_scalar(sO, lcum[:], 1.0, None, ALU.subtract))
        VW(); V(vec.tensor_tensor(o3[:], iotaO, bO(S.i1own[:]), ALU.is_equal))
        VW(); V(vec.tensor_tensor(U2s[:, 0:OWNT, :], o3[:], sO, ALU.mult))
        VW(); V(vec.tensor_reduce(slot1[:], U2s[:, 0:OWNT, :], AXX, ALU.max))
        VW(); V(vec.tensor_tensor(o3[:], iotaO, bO(S.i2own[:]), ALU.is_equal))
        VW(); V(vec.tensor_tensor(U2s[:, 0:OWNT, :], o3[:], sO, ALU.mult))
        VW(); V(vec.tensor_reduce(slot2[:], U2s[:, 0:OWNT, :], AXX, ALU.max))
        VW(); V(vec.tensor_scalar(scrO[:], slot1[:], float(LCAP), None, ALU.is_lt))
        VW(); V(vec.tensor_tensor(k1c[:], k1c[:], scrO[:], ALU.mult))
        VW(); V(vec.tensor_scalar(scrO[:], slot2[:], float(LCAP), None, ALU.is_lt))
        VW(); V(vec.tensor_tensor(k2c[:], k2c[:], scrO[:], ALU.mult))
        VW(); V(vec.tensor_tensor(S.mself[:], k1c[:], k2c[:], ALU.mult))
        VW(); V(vec.tensor_copy(msel[:], S.mself[:]))
        VW(); V(vec.tensor_tensor(S.ms1t[:], S.mself[:], S.s1t[:], ALU.mult))
        VW(); V(vec.tensor_tensor(S.ms2t[:], S.mself[:], S.s2t[:], ALU.mult))
        VW(); V(vec.tensor_scalar(S.mscf[:], S.mself[:], -1.0, 1.0,
                                  ALU.mult, ALU.add))

        def build_dest(slot, ksel, eown, dst_i32_ap):
            VW(); V(vec.tensor_scalar(scrO[:], slot[:], 128.0, None, ALU.is_ge))
            for thr in (256.0, 384.0, 512.0):
                VW(); V(vec.tensor_scalar(scrO2[:], slot[:], thr, None, ALU.is_ge))
                VW(); V(vec.tensor_tensor(scrO[:], scrO[:], scrO2[:], ALU.add))
            VW(); V(vec.scalar_tensor_tensor(scrO2[:], scrO[:], -128.0, slot[:],
                                             ALU.mult, ALU.add))
            VW(); V(vec.tensor_scalar(scrO2[:], scrO2[:], float(RT * NE), None, ALU.mult))
            VW(); V(vec.tensor_tensor(scrO2[:], scrO2[:], scrO[:], ALU.add))
            VW(); V(vec.scalar_tensor_tensor(scrO2[:], eown[:], float(RT), scrO2[:],
                                             ALU.mult, ALU.add))
            VW(); V(vec.tensor_scalar(scrO2[:], scrO2[:], BIG, None, ALU.add))
            VW(); V(vec.scalar_tensor_tensor(scrO2[:], ksel[:], -BIG, scrO2[:],
                                             ALU.mult, ALU.add))
            VW(); V(vec.tensor_scalar(scrO2[:], scrO2[:], E23, None, ALU.add))
            VW(); ins = V(vec.tensor_scalar(dst_i32_ap, scrO2[:].bitcast(I32),
                                            S.imask[:, 0:1], None, ALU.bitwise_and))
            EVm()

        build_dest(slot1, k1c, S.i1own, dst1i[:])
        build_dest(slot2, k2c, S.i2own, dst2i[:])

        VW(); ins = V(vec.memset(bigt[:], BIG_I))
        EVm()
        EV.wait(g)
        VS.wait(g)
        GQ.dma(g.dma_start(out=S.idx_dram.rearrange("(p a) b -> p (a b)", p=128),
                           in_=bigt[:]))
        GQ.dma(g.dma_start(out=S.dst_dram.rearrange("(p a) b -> p (a b)", p=128),
                           in_=bigt[:]))
        GQ.chk()
        for i in range(OWNT):
            for dcol, pd in ((dst1i, S.pdst1), (dst2i, S.pdst2)):
                GQ.dma(g.indirect_dma_start(
                    out=S.idx_dram,
                    out_offset=IndirectOffsetOnAxis(dcol[:, i:i + 1], 0),
                    in_=S.paytok[:, i:i + 1], in_offset=None,
                    bounds_check=S.bnd_idx, oob_is_err=False))
                GQ.dma(g.indirect_dma_start(
                    out=S.dst_dram,
                    out_offset=IndirectOffsetOnAxis(dcol[:, i:i + 1], 0),
                    in_=pd[:, i:i + 1], in_offset=None,
                    bounds_check=S.bnd_idx, oob_is_err=False))
        GQ.chk()
        GQ.dma(g.dma_start(out=S.idxtok[:],
                           in_=S.idx_dram.rearrange("(p a) b -> p (a b)", p=128)))
        GQ.dma(g.dma_start(out=S.idxdsty[:],
                           in_=S.dst_dram.rearrange("(p a) b -> p (a b)", p=128)))
        GQ.chk()
        if KDEBUG:
            GQ.dma(g.dma_start(out=S.taps["t_idx"][:, 0:RT * NE], in_=S.idxtok[:]))
            GQ.dma(g.dma_start(out=S.taps["t_idx"][:, RT * NE:], in_=S.idxdsty[:]))
            GQ.chk()
            tb = p2("tb", (128, OWNT, 12))
            VW(); V(vec.memset(tb[:], 0.0))
            for ci, src in enumerate([S.s1t, S.s2t, S.mself, S.i1own, S.i2own,
                                      slot1, slot2, k1c, k2c]):
                VW(); V(vec.tensor_copy(tb[:, :, ci:ci + 1].squeeze(2), src[:, 0:OWNT]))
            VW(); ins = V(vec.tensor_copy(tb[:, :, 9:10].squeeze(2),
                                          S.rstd_all[:, 2:2 + OWNT]))
            EVm()
            VS.wait(sy)
            SQ.dma(sy.dma_start(out=S.taps["t_route"],
                                in_=tb[:].rearrange("p a b -> p (a b)")))
            kk = p2("kk", (128, OWNT * NE))
            VW(); ins = V(vec.tensor_copy(kk[:].rearrange("p (a b) -> p a b", a=OWNT),
                                          S.keep_own[:]))
            EVm()
            VS.wait(sy)
            SQ.dma(sy.dma_start(out=S.taps["t_keep"], in_=kk[:]))
            SQ.chk()


def _phase3(S):
    sb = S.sb; V = S.V; VW = S.VW; EVm = S.EVm
    pe, vec, act, g, sy = S.pe, S.vec, S.act, S.g, S.sy
    SQ, GQ, PEG, EV, VS, AS_ = S.SQ, S.GQ, S.PEG, S.EV, S.VS, S.AS
    pbh, pb = S.pbh, S.pb

    with ExitStack() as s3:
        fbuf = [sb(f"fbuf{i}", (128, TOKH), stk=s3) for i in range(2)]
        vbuf = [sb(f"vbuf{i}", (128, TOKH), stk=s3) for i in range(2)]
        sgbuf = [sb(f"sgbuf{i}", (128, TOKH), stk=s3) for i in range(2)]
        hbuf = [sb(f"hbuf{i}", (128, TOKH), stk=s3) for i in range(2)]
        h4 = [sb(f"h4_{i}", (128, 512), stk=s3) for i in range(2)]

        SUBS = [(0, 512), (512, 512), (1024, 512), (1536, 512), (2048, 256)]
        bank_free = []
        scan_done = [0, 0]
        tr_done = [0, 0]
        h4_st = [0, 0]
        pb4_as = [0]
        gctr = 0
        EV.wait(pe)
        AS_.wait(pe)
        VS.wait(pe)

        for r in range(DC):
            GQ.dma(g.dma_start(out=S.fw1bf[128 * r:128 * (r + 1), :],
                               in_=S.fw1_h[128 * r:128 * (r + 1), :]))
        for j in range(0, JC, 4):
            GQ.dma(g.dma_start(out=S.fw2bf[128 * j:128 * (j + 4), :],
                               in_=S.fw2_h[128 * j:128 * (j + 4), :]))
        S.SQ.d.wait(g)
        S.g_toks = []
        for e2 in range(2):
            for rt in range(RT):
                GQ.dma(g.indirect_dma_start(
                    out=S.xg_tm[e2][rt][:], out_offset=None,
                    in_=S.xn_dram,
                    in_offset=IndirectOffsetOnAxis(
                        S.idxtok[:, RT * e2 + rt:RT * e2 + rt + 1], 0),
                    bounds_check=S.bnd_xn, oob_is_err=False))
            S.g_toks.append(GQ.chk())
        for e in (act, vec):
            SQ.wait(e); PEG.wait(e)
        AS_.wait(vec); VS.wait(act); EV.wait(act); EV.wait(vec)

        def _emit_h_transposes(cc):
            parc = cc % 2
            hbc = hbuf[parc]
            for grp in range(4):
                VS.wait(pe, scan_done[parc])
                S.AQ.d.wait(pe, h4_st[parc])
                AS_.wait(pe, pb4_as[0])
                for q in range(4):
                    i = 4 * grp + q
                    last = pe.transpose(pb[4][:, 128 * q:128 * (q + 1)],
                                        hbc[:, HALO + 128 * i:HALO + 128 * (i + 1)],
                                        S.idf[:])
                PEG.inc(last)
                PEG.wait(act)
                S.AQ.d.wait(act, h4_st[parc])
                AS_.inc(act.activation(h4[parc][:], pb[4][:, 0:512], AF.Copy))
                pb4_as[0] = S.AS.n
                AS_.wait(act)
                for q in range(4):
                    i = 4 * grp + q
                    S.AQ.dma(act.dma_start(
                        out=S.h_dram[128 * i:128 * (i + 1),
                                     128 * cc:128 * (cc + 1)],
                        in_=h4[parc][:, 128 * q:128 * (q + 1)]))
                h4_st[parc] = S.AQ.chk()
            tr_done[parc] = PEG.n

        for c in range(DC):
            par = c % 2
            fb_, vb_, sg_, hb_ = fbuf[par], vbuf[par], sgbuf[par], hbuf[par]
            VS.wait(act, scan_done[par])
            for (t0, tl) in SUBS:
                for qi, wname in enumerate("fih"):
                    bank = pb[gctr % 4]
                    if gctr >= 4:
                        kind, val = bank_free[gctr - 4]
                        (AS_ if kind == "A" else VS).wait(pe, val)
                    for r in range(DC):
                        last = pe.matmul(bank[:, 0:tl],
                                         S.WT[wname][r][:, 128 * c:128 * (c + 1)],
                                         S.xn_fm[r][:, t0:t0 + tl],
                                         start=(r == 0), stop=(r == DC - 1))
                    PEG.inc(last)
                    PEG.wait(act)
                    if wname == "h":
                        AS_.inc(act.activation(sg_[:, t0:t0 + tl], bank[:, 0:tl],
                                               AF.Sigmoid))
                        AS_.wait(vec)
                        PEG.wait(vec, tr_done[par])
                        VW()
                        ins = V(vec.scalar_tensor_tensor(hb_[:, t0:t0 + tl],
                                                         bank[:, 0:tl], 0.5,
                                                         sg_[:, t0:t0 + tl],
                                                         ALU.add, ALU.max))
                        bank_free.append(("V", VS.n))
                    else:
                        dest = fb_ if wname == "f" else vb_
                        AS_.inc(act.activation(dest[:, t0:t0 + tl], bank[:, 0:tl],
                                               AF.Sigmoid))
                        bank_free.append(("A", AS_.n))
                    gctr += 1
            AS_.wait(vec)
            VW(); V(vec.tensor_tensor(sg_[:], vb_[:], hb_[:], ALU.mult))
            VW(); V(vec.tensor_scalar(fb_[:, 0:HALO], fb_[:, 0:HALO],
                                      S.halo_s[:, 0:1], None, ALU.mult))
            VW(); V(vec.tensor_scalar(sg_[:, 0:HALO], sg_[:, 0:HALO],
                                      S.halo_s[:, 0:1], None, ALU.mult))
            VW(); V(vec.tensor_scalar(sg_[:, HALO - 1:HALO], sg_[:, HALO - 1:HALO],
                                      S.halo_h[:, 0:1], None, ALU.add))
            VW(); ins = V(vec.tensor_tensor_scan(hb_[:], fb_[:], sg_[:], 0.0,
                                                 ALU.mult, ALU.add))
            scan_done[par] = VS.n
            if c >= 1:
                _emit_h_transposes(c - 1)
            if c == DC - 1:
                _emit_h_transposes(c)

        if KDEBUG:
            with ExitStack() as sd:
                hh = sb("hh_dbg", (128, D), stk=sd)
                S.AQ.d.wait(sy)
                for i in range(OWNT):
                    SQ.d.wait(sy)
                    SQ.dma(sy.dma_start(out=hh[:], in_=S.h_dram[128 * i:128 * (i + 1), :]))
                    SQ.chk()
                    SQ.dma(sy.dma_start(out=S.taps["t_h"][128 * i:128 * (i + 1), :],
                                        in_=hh[:]))
                    SQ.chk()


def _phase4(S):
    nc = S.nc; sb = S.sb; V = S.V; VW = S.VW; EVm = S.EVm
    pe, vec, act, g, sy = S.pe, S.vec, S.act, S.g, S.sy
    SQ, GQ, PEG, EV, VS, AS_, PQ = S.SQ, S.GQ, S.PEG, S.EV, S.VS, S.AS, S.PQ
    pbh, pb = S.pbh, S.pb

    with ExitStack() as s4:
        p4 = lambda name, shape, dtype=F32: sb(name, shape, dtype, s4)
        W2ball = p4("W2ball", (128, JC * D), BF16)
        W2b = [_ColView(W2ball, j * D, D) for j in range(JC)]
        W1gall = [p4(f"W1gall{pp}", (128, DC * 512), BF16) for pp in range(4)]
        W1g = [[_ColView(W1gall[pp], r * 512, 512) for r in range(DC)]
               for pp in range(4)]
        ubuf = [p4(f"ubuf{j}", (128, LCAP), BF16) for j in range(JC)]
        xg_fm = [p4(f"xg_fm{r}", (128, LCAP), BF16) for r in range(DC)]
        b2bc = p4("b2bc", (128, D))
        yesb = p4("yesb", (128, D))

        S.AQ.d.wait(g, S.yz_tok)

        HALVES = [(0, 384), (384, 256)]
        trs_peg_prev = 0
        s2_peg_prev = 0
        b2bc_ev = 0
        evh = []
        scat_tok = 0
        trs_peg_set = [0, 0]

        def _gathers(e2, defer=False):
            ss = e2 % 2
            PEG.wait(g, trs_peg_set[ss])
            for rt in range(RT):
                GQ.dma(g.indirect_dma_start(
                    out=S.xg_tm[ss][rt][:], out_offset=None,
                    in_=S.xn_dram,
                    in_offset=IndirectOffsetOnAxis(
                        S.idxtok[:, RT * e2 + rt:RT * e2 + rt + 1], 0),
                    bounds_check=S.bnd_xn, oob_is_err=False))
            return GQ.mark() if defer else GQ.chk()

        g_marks = [0, 0]
        g_toks = list(S.g_toks)

        W1SL = [s4.enter_context(nc.semaphore(f"W1SL{i}")) for i in range(4)]
        W2SL = s4.enter_context(nc.semaphore("W2SL"))
        s1_peg_step = {}
        w1_done = {}
        w1_cnt = [0, 0, 0, 0]

        def _w1_issue(n2):
            e2, jg2 = divmod(n2, 8)
            if e2 >= NE:
                return
            sl = n2 % 4
            PEG.wait(g, s1_peg_step.get(n2 - 4, 0))
            ins = g.dma_start(
                out=W1gall[sl][:].rearrange("p (r c) -> p r c", r=DC),
                in_=S.W1_h[e2, :, 512 * jg2:512 * (jg2 + 1)]
                .rearrange("(r p) c -> p r c", p=128))
            ins.then_inc(W1SL[sl], 16)
            w1_cnt[sl] += 1
            w1_done[n2] = 16 * w1_cnt[sl]

        for n2 in range(3):
            _w1_issue(n2)

        for e in range(NE):
            g_tok = g_toks[e % 2]
            PEG.wait(g, s2_peg_prev)
            EV.wait(pe, b2bc_ev)
            sel = S.idf[0:NE + 1, e:e + 1].to_broadcast([NE + 1, 128])
            PEG.inc(pe.matmul(pb[6][:, 0:512], sel, S.b2all[:, 0:512],
                              start=True, stop=True))
            PEG.inc(pe.matmul(pb[5][:, 0:512], sel, S.b2all[:, 512:1024],
                              start=True, stop=True))
            PEG.wait(vec)
            V(vec.tensor_copy(b2bc[:, 0:512], pb[6][:, 0:512]))
            ins = V(vec.tensor_copy(b2bc[:, 512:1024], pb[5][:, 0:512]))
            EVm()
            b2bc_ev = S.EV.n

            GQ.wait(pe, g_tok)
            for rt in range(RT):
                EV.wait(pe)
                for r in range(DC):
                    last = pe.transpose(pbh[:, 128 * r:128 * (r + 1)],
                                        S.xg_tm[e % 2][rt][:, 128 * r:128 * (r + 1)],
                                        S.idb[:])
                PEG.inc(last)
                PEG.wait(vec)
                ins = None
                for r in range(DC):
                    ins = V(vec.tensor_copy(xg_fm[r][:, 128 * rt:128 * (rt + 1)],
                                            pbh[:, 128 * r:128 * (r + 1)]))
                EVm()
            trs_peg_set[e % 2] = PEG.n

            gelu_hist = []
            for jg in range(8):
                n = 8 * e + jg
                par = n % 4
                if jg < 4:
                    w2v = W2ball[:].rearrange("p (j c) -> p j c", c=D)
                    for b in (2 * jg, 2 * jg + 1):
                        ins = g.dma_start(
                            out=w2v[:, 4 * b:4 * (b + 1), :],
                            in_=S.W2_h[e, 512 * b:512 * (b + 1), :]
                            .rearrange("(q p) c -> p q c", p=128))
                        ins.then_inc(W2SL, 16)
                _w1_issue(n + 3)
                if jg == 2 and e + 2 < NE:
                    g_marks[e % 2] = _gathers(e + 2, defer=True)
                if jg == 6 and e + 2 < NE:
                    g_toks[e % 2] = GQ.chk()
                pe.wait_ge(W1SL[par], w1_done[n])
                VS.wait(pe)
                for j4 in range(4):
                    j = 4 * jg + j4
                    for hi, (h0, hl) in enumerate(HALVES):
                        gidx = len(gelu_hist)
                        bank = pb[gidx % 4]
                        if gidx >= 4:
                            AS_.wait(pe, gelu_hist[gidx - 4])
                        elif evh:
                            EV.wait(pe, evh[-1])
                        for r in range(DC):
                            last = pe.matmul(bank[:, 0:hl],
                                             W1g[par][r][:, 128 * j4:128 * (j4 + 1)],
                                             xg_fm[r][:, h0:h0 + hl],
                                             start=(r == 0), stop=(r == DC - 1))
                        PEG.inc(last)
                        PEG.wait(act)
                        AS_.inc(act.activation(ubuf[j][:, h0:h0 + hl], bank[:, 0:hl],
                                               AF.Gelu,
                                               bias=S.b1call[:, JC * e + j:
                                                             JC * e + j + 1]))
                        gelu_hist.append(AS_.n)
                s1_peg_step[n] = PEG.n

            pe.wait_ge(W2SL, 16 * 8 * (e + 1))
            AS_.wait(pe)
            for rt in range(RT):
                for dh in range(2):
                    bank = pb[len(evh) % 4]
                    if len(evh) >= 4:
                        EV.wait(pe, evh[len(evh) - 4])
                    for j in range(JC):
                        last = pe.matmul(bank[:, 0:512],
                                         ubuf[j][:, 128 * rt:128 * (rt + 1)],
                                         W2b[j][:, 512 * dh:512 * (dh + 1)],
                                         start=(j == 0), stop=(j == JC - 1))
                    PEG.inc(last)
                    PEG.wait(vec)
                    GQ.d.wait(vec, scat_tok)
                    ins = V(vec.tensor_tensor(yesb[:, 512 * dh:512 * (dh + 1)],
                                              bank[:, 0:512],
                                              b2bc[:, 512 * dh:512 * (dh + 1)],
                                              ALU.add))
                    EVm()
                    evh.append(S.EV.n)
                EV.wait(g)
                GQ.dma(g.indirect_dma_start(
                    out=S.Y_dram,
                    out_offset=IndirectOffsetOnAxis(
                        S.idxdsty[:, RT * e + rt:RT * e + rt + 1], 0),
                    in_=yesb[:], in_offset=None,
                    bounds_check=S.bnd_y, oob_is_err=False))
                scat_tok = GQ.chk()
            s2_peg_prev = PEG.n

        if KDEBUG:
            GQ.chk()
            with ExitStack() as sd:
                yy = sb("yy_dbg", (128, D), stk=sd)
                for i in range(2 * OWNT):
                    SQ.d.wait(sy)
                    GQ.d.wait(sy)
                    SQ.dma(sy.dma_start(out=yy[:], in_=S.Y_dram[128 * i:128 * (i + 1), :]))
                    SQ.chk()
                    SQ.dma(sy.dma_start(out=S.taps["t_y"][128 * i:128 * (i + 1), :],
                                        in_=yy[:]))
                    SQ.chk()


def _phase5(S):
    sb = S.sb; V = S.V; VW = S.VW; EVm = S.EVm
    pe, vec, act, g, sy = S.pe, S.vec, S.act, S.g, S.sy
    SQ, GQ, PEG, EV, VS, AS_ = S.SQ, S.GQ, S.PEG, S.EV, S.VS, S.AS
    pbh, pb = S.pbh, S.pb

    with ExitStack() as s5:
        p5 = lambda name, shape, dtype=F32: sb(name, shape, dtype, s5)
        fw1b = [p5(f"fw1b{r}", (128, DFF), BF16) for r in range(DC)]
        fw2b = [p5(f"fw2b{j}", (128, D), BF16) for j in range(JC)]
        u2 = [p5(f"u2_{j}", (128, GSZ), BF16) for j in range(JC)]
        xsn2 = [[p5(f"xsn{p}_{r}", (128, GSZ), BF16) for r in range(DC)]
                for p in range(2)]
        yfb = p5("yfb", (128, D))
        xres = p5("xres", (128, D))
        hres = p5("hres", (128, D))
        y1 = p5("y1", (128, D))
        y2 = p5("y2", (128, D))

        for r in range(DC):
            qq = (SQ, S.AQ)[r % 2]
            qq.dma(qq.eng.dma_start(out=fw1b[r][:],
                                    in_=S.fw1bf[128 * r:128 * (r + 1), :]))
        fw1_toks = (SQ.chk(), S.AQ.chk())
        for j in range(JC):
            qq = (GQ, SQ, S.AQ)[j % 3]
            qq.dma(qq.eng.dma_start(out=fw2b[j][:],
                                    in_=S.fw2bf[128 * j:128 * (j + 1), :]))
        fw2_toks = (GQ.chk(), SQ.chk(), S.AQ.chk())


        s1_peg = [0, 0]
        evh5 = []
        xs_toks = [None, None]
        S.out_st_tok = 0

        aq_xs = [None]

        def _xsn_load(gi2):
            par2 = gi2 % 2
            t02 = gi2 * GSZ
            PEG.wait(sy, s1_peg[par2])
            base = DC * 128 * gi2
            rr = range(DC // 2, DC) if gi2 == 0 else range(DC)
            for r in rr:
                SQ.dma(sy.dma_start(out=xsn2[par2][r][:],
                                    in_=S.xnfm_dram[base + 128 * r:
                                                    base + 128 * (r + 1), :]))
            xs_toks[par2] = SQ.chk()
            if gi2 == 0:
                for r in range(DC // 2):
                    S.AQ.dma(act.dma_start(
                        out=xsn2[par2][r][:],
                        in_=S.xnfm_dram[base + 128 * r:base + 128 * (r + 1), :]))
                aq_xs[0] = S.AQ.chk()

        _xsn_load(0)
        for gi in range(NG):
            t0 = gi * GSZ
            par = gi % 2
            xsn = xsn2[par]
            SQ.wait(pe, xs_toks[par])
            if gi == 0:
                S.AQ.wait(pe, aq_xs[0])
                SQ.wait(pe, fw1_toks[0]); S.AQ.wait(pe, fw1_toks[1])
            if evh5:
                EV.wait(pe, evh5[-1])
            gelu_hist = []
            for j in range(JC):
                bank = pb[j % 4]
                if j >= 4:
                    AS_.wait(pe, gelu_hist[j - 4])
                for r in range(DC):
                    last = pe.matmul(bank[:, 0:GSZ], fw1b[r][:, 128 * j:128 * (j + 1)],
                                     xsn[r][:], start=(r == 0), stop=(r == DC - 1))
                PEG.inc(last)
                PEG.wait(act)
                AS_.inc(act.activation(u2[j][:], bank[:, 0:GSZ], AF.Gelu,
                                       bias=S.fb1c[:, j:j + 1]))
                gelu_hist.append(AS_.n)
            s1_peg[par] = PEG.n
            if gi + 1 < NG:
                _xsn_load(gi + 1)
            if gi == 0:
                GQ.wait(pe, fw2_toks[0]); SQ.wait(pe, fw2_toks[1])
                S.AQ.wait(pe, fw2_toks[2])
            AS_.wait(pe)
            for tt2 in range(GSZ // 128):
                tt = gi * (GSZ // 128) + tt2
                for dh in range(2):
                    bank = pb[(2 * tt2 + dh) % 4]
                    if len(evh5) >= 4:
                        EV.wait(pe, evh5[len(evh5) - 4])
                    for j in range(JC):
                        pe.matmul(bank[:, 0:512],
                                  u2[j][:, 128 * tt2:128 * (tt2 + 1)],
                                  fw2b[j][:, 512 * dh:512 * (dh + 1)],
                                  start=(j == 0), stop=False)
                    last = pe.matmul(bank[:, 0:512],
                                     S.idf[0:NE + 1, NE:NE + 1]
                                     .to_broadcast([NE + 1, 128]),
                                     S.b2all[:, 512 * dh:512 * (dh + 1)],
                                     start=False, stop=True)
                    PEG.inc(last)
                    PEG.wait(vec)
                    ins = V(vec.tensor_copy(yfb[:, 512 * dh:512 * (dh + 1)],
                                            bank[:, 0:512]))
                    EVm()
                    evh5.append(S.EV.n)
                VS.wait(act)
                SQ.d.wait(act, S.out_st_tok)
                S.AQ.dma(act.dma_start(out=xres[:],
                                       in_=S.xsl[HALO + 128 * tt:
                                                 HALO + 128 * (tt + 1), :]))
                S.AQ.dma(act.dma_start(out=hres[:],
                                       in_=S.h_dram[128 * tt:128 * (tt + 1), :]))
                S.AQ.dma(act.dma_start(out=y1[:],
                                       in_=S.Y_dram[128 * tt:128 * (tt + 1), :]))
                S.AQ.dma(act.dma_start(out=y2[:],
                                       in_=S.Y_dram[TOK + 128 * tt:
                                                    TOK + 128 * (tt + 1), :]))
                f_tok = S.AQ.chk()
                S.AQ.wait(vec, f_tok)
                VW(); V(vec.tensor_scalar(y1[:], y1[:], S.ms1t[:, tt:tt + 1], None,
                                          ALU.mult))
                VW(); V(vec.scalar_tensor_tensor(y1[:], y2[:], S.ms2t[:, tt:tt + 1],
                                                 y1[:], ALU.mult, ALU.add))
                VW(); V(vec.scalar_tensor_tensor(y1[:], yfb[:],
                                                 S.mscf[:, tt:tt + 1], y1[:],
                                                 ALU.mult, ALU.add))
                VW(); V(vec.tensor_tensor(y1[:], y1[:], xres[:], ALU.add))
                VW(); ins = V(vec.tensor_tensor(y2[:], y1[:], hres[:], ALU.add))
                EVm()
                VS.wait(sy)
                SQ.dma(sy.dma_start(out=S.out_h[128 * tt:128 * (tt + 1), :], in_=y2[:]))
                S.out_st_tok = SQ.chk()


_NC_CACHE = {}


def _consts():
    c = {}
    k = np.arange(128)
    c["idf"] = np.eye(128, dtype=np.float32)
    c["idb"] = np.eye(128, dtype=ml_dtypes.bfloat16)
    c["cumW"] = (k[:, None] <= k[None, :]).astype(ml_dtypes.bfloat16)
    c["cumSW"] = (k[:, None] < k[None, :]).astype(ml_dtypes.bfloat16)
    c["onesB"] = np.ones((128, 128), dtype=ml_dtypes.bfloat16)
    c["onesR"] = np.ones((1, 128), dtype=np.float32)
    c["onesC"] = np.ones((128, 1), dtype=np.float32)
    c["epscol"] = np.full((128, 1), 1e-5, dtype=np.float32)
    c["iota8"] = np.broadcast_to(np.arange(NE, dtype=np.float32), (128, NE)).copy()
    c["iotam8"] = c["iota8"] - 8.0
    c["negbig"] = np.full((128, 1), -3.0e38, dtype=np.float32)
    c["imask"] = np.full((128, 1), 0x7FFFFF, dtype=np.int32)
    lt = (np.arange(OWNT)[None, :] * 128 + k[:, None]).astype(np.int64)
    c["paytok"] = (lt + HALO).astype(np.int32)
    c["pdst1"] = lt.astype(np.int32)
    c["pdst2"] = (lt + TOK).astype(np.int32)
    return c


def kernel(L0_attn_x, L0_attn_Wf, L0_attn_Wi, L0_attn_Wh, L0_ffn_Wr, L0_ffn_W1,
           L0_ffn_b1, L0_ffn_W2, L0_ffn_b2, L0_ffn_fw1, L0_ffn_fb1, L0_ffn_fw2,
           L0_ffn_fb2):
    X = np.ascontiguousarray(np.asarray(L0_attn_x, dtype=np.float32).reshape(16384, D))
    xt = np.ascontiguousarray(X.T)
    consts = _consts()

    if "nc" not in _NC_CACHE:
        _NC_CACHE["nc"] = build_nc()
    nc, taps = _NC_CACHE["nc"]

    weights = dict(
        Wf=np.asarray(L0_attn_Wf, np.float32), Wi=np.asarray(L0_attn_Wi, np.float32),
        Wh=np.asarray(L0_attn_Wh, np.float32), Wr=np.asarray(L0_ffn_Wr, np.float32),
        W1=np.asarray(L0_ffn_W1, np.float32), b1=np.asarray(L0_ffn_b1, np.float32),
        W2=np.asarray(L0_ffn_W2, np.float32), b2=np.asarray(L0_ffn_b2, np.float32),
        fw1=np.asarray(L0_ffn_fw1, np.float32),
        fb1=np.asarray(L0_ffn_fb1, np.float32).reshape(1, DFF),
        fw2=np.asarray(L0_ffn_fw2, np.float32),
        fb2=np.asarray(L0_ffn_fb2, np.float32).reshape(1, D),
    )
    weights["b1t"] = np.ascontiguousarray(
        weights["b1"].reshape(NE, JC, 128).transpose(2, 0, 1).reshape(128, NE * JC))
    weights["fb1t"] = np.ascontiguousarray(
        weights["fb1"].reshape(JC, 128).T)

    in_maps = []
    zpad = np.zeros((HALO, D), dtype=np.float32)
    for c in range(NCORES):
        first = c in (0, NCORES // 2)
        if first:
            xslc = np.concatenate([zpad, X[TOK * c:TOK * (c + 1)]], axis=0)
        else:
            xslc = X[TOK * c - HALO:TOK * (c + 1)]
        sown = np.zeros((128, OWNT), dtype=np.float32)
        for j in range(OWNT):
            sown[OWNT * c + j, j] = 1.0
        im = dict(consts)
        im.update(weights)
        im["xsl"] = np.ascontiguousarray(xslc)
        im["xt"] = xt
        im["sown"] = sown
        im["halo_s"] = np.full((128, 1), 0.0 if first else 1.0, dtype=np.float32)
        im["halo_h"] = np.full((128, 1), 0.5 if first else 0.0, dtype=np.float32)
        in_maps.append(im)

    res = run_bass_kernel_spmd(nc, in_maps, core_ids=list(range(NCORES)))
    kernel._last = res
    out = np.concatenate([res.results[c]["out"] for c in range(NCORES)], axis=0)
    return out.reshape(2, 8192, D)
